# version 20
# baseline (speedup 1.0000x reference)
import numpy as np
from contextlib import ExitStack

import concourse.bass as bass
import concourse.mybir as mybir
from concourse.bass_utils import run_bass_kernel_spmd

F32 = mybir.dt.float32
BF16 = mybir.dt.bfloat16
I32 = mybir.dt.int32
AF = mybir.ActivationFunctionType
ALU = mybir.AluOpType
AX = mybir.AxisListType

D = 2048
S = 4096
NH = 16
DH = 128
DF = 8192
OWN = 2048
NIC = D // 128
NW_OWN = OWN // 512
NW_KV = S // 512
NKB = S // 128

TWO_PI = 2.0 * np.pi
CH = float(np.round(TWO_PI * 2 ** 11) * 2.0 ** -11)
CM = float(np.round((TWO_PI - CH) * 2 ** 25) * 2.0 ** -25)
CL = float(np.round((TWO_PI - CH - CM) * 2 ** 39) * 2.0 ** -39)
CR = float(np.float32(TWO_PI - CH - CM - CL))
IH = float(np.float32(1.0 / TWO_PI))
IL = float(np.float32(1.0 / TWO_PI - IH))
PH = float(np.float32(np.pi / 2))
PM2 = float(np.float32(np.pi / 2 - PH))
PL2 = float(np.float32(np.pi / 2 - PH - PM2))
PIM = float(np.nextafter(np.float32(np.pi), np.float32(0.0)))
MAGIC = float(3 * 2 ** 51)


class _RInst:
    __slots__ = ("incs",)

    def __init__(self):
        self.incs = []

    def then_inc(self, sem, val=1):
        self.incs.append((sem, val))
        return self


class Rec:
    def __init__(self, name):
        self.name = name
        self.ops = []

    def _op(self, meth, *a, **k):
        inst = _RInst()
        self.ops.append((meth, a, k, inst))
        return inst

    def __getattr__(self, meth):
        if meth.startswith("_") or meth in ("name", "ops", "replay"):
            raise AttributeError(meth)
        return lambda *a, **k: self._op(meth, *a, **k)

    def replay(self, eng):
        for meth, a, k, inst in self.ops:
            real = getattr(eng, meth)(*a, **k)
            for sem, val in inst.incs:
                real.then_inc(sem, val)


class Sem:
    def __init__(self, h, name):
        self.h = h
        self.name = name
        self.v = 0

    def inc(self, inst, n=1):
        inst.then_inc(self.h, n)
        self.v += n
        return self.v


def build_program(upto=99):
    nc = bass.Bass("TRN2", target_bir_lowering=False, debug=False)

    d_xq = nc.dram_tensor("xq", [OWN, D], F32, kind="ExternalInput")
    d_xkv = nc.dram_tensor("xkv", [S, D], F32, kind="ExternalInput")
    d_WqT = nc.dram_tensor("WqT", [D, D], F32, kind="ExternalInput")
    d_WkT = nc.dram_tensor("WkT", [D, D], F32, kind="ExternalInput")
    d_WvT = nc.dram_tensor("WvT", [D, D], F32, kind="ExternalInput")
    d_WoT = nc.dram_tensor("WoT", [D, D], F32, kind="ExternalInput")
    d_bqT = nc.dram_tensor("bqT", [128, NH], F32, kind="ExternalInput")
    d_bkT = nc.dram_tensor("bkT", [128, NH], F32, kind="ExternalInput")
    d_bvb = nc.dram_tensor("bvb", [128, D], F32, kind="ExternalInput")
    d_bob = nc.dram_tensor("bob", [128, D], F32, kind="ExternalInput")
    d_wgT = nc.dram_tensor("wgT", [D, DF], F32, kind="ExternalInput")
    d_wuT = nc.dram_tensor("wuT", [D, DF], F32, kind="ExternalInput")
    d_wdT = nc.dram_tensor("wdT", [DF, D], F32, kind="ExternalInput")
    d_qpos = nc.dram_tensor("qpos", [1, OWN], F32, kind="ExternalInput")
    d_pos = nc.dram_tensor("pos", [1, S], F32, kind="ExternalInput")
    d_jcol = nc.dram_tensor("jcol", [128, 1], F32, kind="ExternalInput")
    d_sgn = nc.dram_tensor("sgn", [128, 1], F32, kind="ExternalInput")
    d_ident = nc.dram_tensor("ident", [128, 128], F32, kind="ExternalInput")

    d_out = nc.dram_tensor("out", [OWN, D], F32, kind="ExternalOutput")
    d_xlnq = nc.dram_tensor("xlnq", [OWN, D], BF16)
    d_xlnkv = nc.dram_tensor("xlnkv", [S, D], BF16)
    d_qT = nc.dram_tensor("qTd", [D, OWN], BF16)
    d_kT = nc.dram_tensor("kTd", [D, S], BF16)
    d_v = nc.dram_tensor("vd", [S, D], BF16)
    d_ctxT = nc.dram_tensor("ctxTd", [D, OWN], BF16)
    d_x1 = nc.dram_tensor("x1d", [OWN, D], F32)
    d_xint = nc.dram_tensor("xintd", [OWN, D], BF16)
    d_hfp = nc.dram_tensor("hfpd", [OWN, DF], F32)
    d_hint = nc.dram_tensor("hintd", [OWN, DF], BF16)
    d_wdt = nc.dram_tensor("wdtd", [4, 8, 128, 4096], BF16)

    Y = Rec("sync")
    T = Rec("tensor")
    V = Rec("vector")
    Sc = Rec("scalar")
    G = Rec("gpsimd")
    RECS = [Y, T, V, Sc, G]

    st = ExitStack()
    sems = []

    def mksem(name):
        s = Sem(st.enter_context(nc.semaphore(name)), name)
        sems.append(s)
        return s

    sT = mksem("sT")
    sV = mksem("sV")
    sS = mksem("sS")
    sG = mksem("sG")
    dm = {n: mksem("d_" + n) for n in
          ["misc", "x0", "x1", "xt0", "xt1", "w0", "st0", "st1", "q0", "q1",
           "kv0", "kv1", "c0", "c1", "h0", "h1", "o0", "o1", "p0", "p1",
           "g0", "g1", "u0", "u1", "dd0", "dd1", "wr0", "wr1", "wr2", "tn0",
           "tn1", "wa0", "wa1"]}

    def wait(rec, *toks):
        for sem, val in toks:
            if val > 0:
                rec.wait_ge(sem.h, val)

    def barrier_all():
        for r in RECS:
            for s in sems:
                if s.v > 0:
                    r.wait_ge(s.h, s.v)

    sb = lambda name, shape, dt: st.enter_context(nc.sbuf_tensor(name, shape, dt))
    g_qpos = sb("g_qpos", [1, OWN], F32)
    g_bqT = sb("g_bqT", [128, NH], F32)
    g_bkT = sb("g_bkT", [128, NH], F32)
    g_bvb = sb("g_bvb", [128, D], F32)
    g_bob = sb("g_bob", [128, D], F32)
    g_ones_row = sb("g_ones_row", [1, 128], F32)
    g_ones_col = sb("g_ones_col", [128, 1], F32)
    g_sel4 = [sb(f"g_sel4_{t}", [128, 4], BF16) for t in range(4)]
    g_ones_colb = sb("g_ones_colb", [128, 1], BF16)
    g_ident = sb("g_ident", [128, 128], BF16)
    g_iota = sb("g_iota", [128, 1], F32)
    g_sxinv = sb("g_sxinv", [128, NW_OWN * 4], F32)
    g_shinv = sb("g_shinv", [128, NW_OWN * 4], F32)
    g_r128b = sb("g_r128b", [128, 1], F32)
    g_rec127 = sb("g_rec127", [128, 1], F32)
    g_hamax = sb("g_hamax", [128, NW_OWN * 4], F32)
    p01 = st.enter_context(ExitStack())
    sb01 = lambda name, shape, dt: p01.enter_context(nc.sbuf_tensor(name, shape, dt))
    g_pos = sb01("g_pos", [1, S], F32)
    g_jcol = sb01("g_jcol", [128, 1], F32)
    g_sgn = sb01("g_sgn", [128, 1], F32)
    g_c128 = sb01("g_c128", [1, 1], F32)
    g_c10k = sb01("g_c10k", [1, 1], F32)
    g_c127 = sb01("g_c127", [1, 1], F32)
    g_Ck = sb01("g_Ck", [128, S], F32)
    g_Sk = sb01("g_Sk", [128, S], F32)
    g_Cq = sb01("g_Cq", [128, OWN], F32)
    g_Sq = sb01("g_Sq", [128, OWN], F32)

    pb = [st.enter_context(nc.psum_tensor(f"pb{i}", [128, 512], F32)) for i in range(6)]
    prel = [(sT, 0)] * 8

    for ap_d, t_s in [(d_qpos, g_qpos), (d_pos, g_pos), (d_jcol, g_jcol),
                      (d_sgn, g_sgn), (d_bqT, g_bqT), (d_bkT, g_bkT),
                      (d_bvb, g_bvb), (d_bob, g_bob)]:
        dm["misc"].inc(Y.dma_start(t_s[:], ap_d.ap()), 16)

    def vop(meth, *a, **k):
        if sV.v:
            V.wait_ge(sV.h, sV.v)
        return sV.inc(getattr(V, meth)(*a, **k))

    def sop(meth, *a, **k):
        if sS.v:
            Sc.wait_ge(sS.h, sS.v)
        return sS.inc(getattr(Sc, meth)(*a, **k))

    def gop(meth, *a, **k):
        if sG.v:
            G.wait_ge(sG.h, sG.v)
        return sG.inc(getattr(G, meth)(*a, **k))

    def top(meth, *a, **k):
        return sT.inc(getattr(T, meth)(*a, **k))

    gop("memset", g_ones_row[:], 1.0)
    gop("memset", g_ones_col[:], 1.0)
    gop("memset", g_ones_colb[:], 1.0)
    for t in range(4):
        gop("memset", g_sel4[t][:], 0.0)
        gop("memset", g_sel4[t][:, t:t + 1], 1.0)
    gop("iota", g_iota[:], [[0, 1]], channel_multiplier=1,
        allow_small_or_imprecise_dtypes=True)
    gop("memset", g_c128[:], 128.0)
    gop("memset", g_c10k[:], 10000.0)
    gop("memset", g_c127[:], 127.0)
    gop("memset", g_sxinv[:], 0.0)
    gop("memset", g_shinv[:], 0.0)
    dm["w0"].inc(G.dma_start(g_ident[:], d_ident.ap()), 16)

    with ExitStack() as p0:
        sb0 = lambda name, shape, dt: p0.enter_context(nc.sbuf_tensor(name, shape, dt))
        t_ln10k = sb0("t_ln10k", [1, 1], F32)
        t_e64 = sb0("t_e64", [1, 1], F32)
        t_e64b = sb0("t_e64b", [128, 1], F32)
        t_fcol = sb0("t_fcol", [128, 1], F32)
        t_l128 = sb0("t_l128", [1, 1], F32)
        t_h2 = sb0("t_h2", [1, 1], F32)
        t_r128 = sb0("t_r128", [1, 1], F32)
        p0A = p0.enter_context(nc.psum_tensor("p0A", [128, 512], F32))
        p0B = p0.enter_context(nc.psum_tensor("p0B", [128, 512], F32))
        t_posb = sb0("t_posb", [128, 2048], F32)
        t_w1 = sb0("t_w1", [128, 2048], F32)
        t_w2 = sb0("t_w2", [128, 2048], F32)
        t_wi = sb0("t_wi", [128, 2048], I32)
        t_fhi = sb0("t_fhi", [128, 1], F32)
        t_flo = sb0("t_flo", [128, 1], F32)
        t_fi = sb0("t_fi", [128, 1], I32)
        t_fq = sb0("t_fq", [128, 1], F32)
        t_Ahi = sb0("t_Ahi", [128, 2048], F32)
        t_Alo = sb0("t_Alo", [128, 2048], F32)
        xb = [sb0(f"p1_x{i}", [128, D], F32) for i in range(4)]
        dump = sb0("p1_dump", [128, D], F32)
        NLN = OWN // 128 + S // 128
        lnsA = sb0("p1_lnsA", [128, NLN], F32)
        lnsB = sb0("p1_lnsB", [128, NLN], F32)
        lnM = sb0("p1_lnM", [128, NLN], F32)
        lnV = sb0("p1_lnV", [128, NLN], F32)
        lnb = [sb0(f"p1_ln{i}", [128, D], BF16) for i in range(2)]

        wait(Y, (dm["misc"], dm["misc"].v))
        wait(Sc, (sG, sG.v), (dm["misc"], dm["misc"].v))
        sop("activation", t_ln10k[:], g_c10k[:], AF.Ln)
        wait(V, (sS, sS.v))
        vop("tensor_scalar", t_e64[:], t_ln10k[:], -1.0 / 64.0, None, ALU.mult)
        wait(T, (sV, sV.v), (sG, sG.v))
        top("matmul", p0A[:, 0:1], g_ones_row[:], t_e64[:], start=True, stop=True)
        wait(Sc, (sT, sT.v))
        sop("activation", t_e64b[:], p0A[:, 0:1], AF.Identity)
        sop("activation", t_fcol[:], g_jcol[:], AF.Exp, bias=0.0, scale=t_e64b[:])
        sop("activation", t_l128[:], g_c128[:], AF.Ln)
        wait(V, (sS, sS.v))
        vop("tensor_scalar", t_h2[:], t_l128[:], -0.5, None, ALU.mult)
        wait(Sc, (sV, sV.v))
        sop("activation", t_r128[:], t_h2[:], AF.Exp)
        wait(T, (sS, sS.v))
        top("matmul", p0B[:, 0:1], g_ones_row[:], t_r128[:], start=True, stop=True)
        wait(Sc, (sT, sT.v))
        sop("activation", g_r128b[:], p0B[:, 0:1], AF.Identity)
        wait(V, (sG, sG.v), (sS, sS.v))
        vop("reciprocal", t_h2[:], g_c127[:])
        wait(T, (sV, sV.v))
        top("matmul", p0B[:, 0:1], g_ones_row[:], t_h2[:], start=True, stop=True)
        wait(Sc, (sT, sT.v))
        sop("activation", g_rec127[:], p0B[:, 0:1], AF.Identity)

        xin = [dm["x0"], dm["x1"]]
        sto = [dm["st0"], dm["st1"]]
        rel = [(sS, 0)] * 4
        relo = [(sS, 0), (sS, 0)]
        ln_jobs = []
        if upto >= 1:
            ln_jobs += [(d_xq, d_xlnq, tb) for tb in range(OWN // 128)]
            ln_jobs += [(d_xkv, d_xlnkv, tb) for tb in range(S // 128)]
        ln_i = [0]

        def emit_ln(n):
            for _ in range(n):
                i = ln_i[0]
                if i >= len(ln_jobs):
                    return
                ln_i[0] = i + 1
                lsrc, ldst, tb = ln_jobs[i]
                sl = i % 2
                wait(Y, rel[sl])
                dm_t = xin[sl]
                dm_t.inc(Y.dma_start(xb[sl][:], lsrc.ap()[tb * 128:(tb + 1) * 128, :]), 16)
                wait(Sc, (dm_t, dm_t.v))
                sop("activation", dump[:], xb[sl][:], AF.Identity,
                    accum_out=lnsA[:, i:i + 1])
                sop("activation", dump[:], xb[sl][:], AF.Square,
                    accum_out=lnsB[:, i:i + 1])
                rel[sl] = (sS, sS.v)

        def emit_lnB():
            if not ln_jobs:
                return
            wait(V, (sS, sS.v), (sV, sV.v))
            vop("tensor_scalar", lnM[:], lnsA[:], 1.0 / 2048.0, None, ALU.mult)
            vop("tensor_tensor", lnV[:], lnM[:], lnM[:], ALU.mult)
            vop("tensor_scalar", lnsB[:], lnsB[:], 1.0 / 2048.0, None, ALU.mult)
            vop("tensor_tensor", lnV[:], lnsB[:], lnV[:], ALU.subtract)
            vop("tensor_scalar", lnV[:], lnV[:], 1e-5, None, ALU.add)
            wait(Sc, (sV, sV.v))
            sop("activation", lnV[:], lnV[:], AF.Ln)
            sop("activation", lnV[:], lnV[:], AF.Exp, bias=0.0, scale=-0.5)
            wait(V, (sS, sS.v))
            vop("tensor_tensor", lnM[:], lnM[:], lnV[:], ALU.mult)
            vop("tensor_scalar", lnM[:], lnM[:], -1.0, None, ALU.mult)
            chain = (sV, sV.v)
            wait(Y, (dm["x0"], dm["x0"].v), (dm["x1"], dm["x1"].v))
            xinB = [dm["x0"], dm["wa0"], dm["x1"], dm["wa1"]]
            for i, (lsrc, ldst, tb) in enumerate(ln_jobs):
                sl = i % 4
                ring = Y if sl % 2 == 0 else G
                wait(ring, rel[sl])
                dm_t = xinB[sl]
                dm_t.inc(ring.dma_start(xb[sl][:], lsrc.ap()[tb * 128:(tb + 1) * 128, :]), 16)
                so = i % 2
                wait(Sc, (dm_t, dm_t.v), chain, relo[so])
                sop("activation", lnb[so][:], xb[sl][:], AF.Identity,
                    bias=lnM[:, i:i + 1], scale=lnV[:, i:i + 1])
                rel[sl] = (sS, sS.v)
                wait(Sc, (sS, sS.v))
                dmo = sto[so]
                dmo.inc(Sc.dma_start(ldst.ap()[tb * 128:(tb + 1) * 128, :], lnb[so][:]), 16)
                relo[so] = (dmo, dmo.v)

        tok_sin = [(sS, 0)]
        tok_ang = [(sV, 0)]

        wait(V, (sS, sS.v))
        vop("tensor_scalar", t_fhi[:], t_fcol[:], 4096.0, 0.5, ALU.mult, ALU.add)
        vop("tensor_copy", t_fi[:], t_fhi[:])
        vop("tensor_tensor", t_fq[:], t_fi[:], t_fhi[:], ALU.is_gt)
        vop("tensor_tensor", t_fhi[:], t_fi[:], t_fq[:], ALU.subtract)
        vop("tensor_scalar", t_fhi[:], t_fhi[:], 2.0 ** -12, None, ALU.mult)
        vop("tensor_tensor", t_flo[:], t_fcol[:], t_fhi[:], ALU.subtract)

        def reduce_and_sin(out, c0out, W, signed, cosshift):
            wait(V, tok_sin[0])
            vop("tensor_tensor", t_w2[:, :W], t_Ahi[:, :W], t_Alo[:, :W], ALU.add)
            if cosshift:
                vop("tensor_scalar", t_w2[:, :W], t_w2[:, :W], PH, None, ALU.add)
            vop("tensor_scalar", t_w1[:, :W], t_w2[:, :W], IH, None, ALU.mult)
            vop("scalar_tensor_tensor", t_w1[:, :W], t_w2[:, :W], IL,
                t_w1[:, :W], ALU.mult, ALU.add)
            vop("tensor_scalar", t_w1[:, :W], t_w1[:, :W], 0.5, None, ALU.add)
            vop("tensor_copy", t_wi[:, :W], t_w1[:, :W])
            vop("tensor_tensor", t_w2[:, :W], t_wi[:, :W], t_w1[:, :W], ALU.is_gt)
            vop("tensor_tensor", t_w1[:, :W], t_wi[:, :W], t_w2[:, :W],
                ALU.subtract)
            vop("scalar_tensor_tensor", t_w2[:, :W], t_w1[:, :W], -CH,
                t_Ahi[:, :W], ALU.mult, ALU.add)
            vop("tensor_tensor", t_w2[:, :W], t_w2[:, :W], t_Alo[:, :W], ALU.add)
            if cosshift:
                vop("tensor_scalar", t_w2[:, :W], t_w2[:, :W], PH, None, ALU.add)
                vop("tensor_scalar", t_w2[:, :W], t_w2[:, :W], PM2, None, ALU.add)
                vop("tensor_scalar", t_w2[:, :W], t_w2[:, :W], PL2, None, ALU.add)
            vop("scalar_tensor_tensor", t_w2[:, :W], t_w1[:, :W], -CM,
                t_w2[:, :W], ALU.mult, ALU.add)
            vop("scalar_tensor_tensor", t_w2[:, :W], t_w1[:, :W], -CL,
                t_w2[:, :W], ALU.mult, ALU.add)
            vop("scalar_tensor_tensor", t_w2[:, :W], t_w1[:, :W], -CR,
                t_w2[:, :W], ALU.mult, ALU.add)
            vop("tensor_scalar", t_w1[:, :W], t_w2[:, :W], PIM, None, ALU.is_gt)
            vop("scalar_tensor_tensor", t_w2[:, :W], t_w1[:, :W], -CH,
                t_w2[:, :W], ALU.mult, ALU.add)
            vop("scalar_tensor_tensor", t_w2[:, :W], t_w1[:, :W], -CM,
                t_w2[:, :W], ALU.mult, ALU.add)
            vop("scalar_tensor_tensor", t_w2[:, :W], t_w1[:, :W], -CL,
                t_w2[:, :W], ALU.mult, ALU.add)
            vop("tensor_scalar", t_w1[:, :W], t_w2[:, :W], -PIM, None, ALU.is_lt)
            vop("scalar_tensor_tensor", t_w2[:, :W], t_w1[:, :W], CH,
                t_w2[:, :W], ALU.mult, ALU.add)
            vop("scalar_tensor_tensor", t_w2[:, :W], t_w1[:, :W], CM,
                t_w2[:, :W], ALU.mult, ALU.add)
            vop("scalar_tensor_tensor", t_w2[:, :W], t_w1[:, :W], CL,
                t_w2[:, :W], ALU.mult, ALU.add)
            vop("tensor_scalar", t_w2[:, :W], t_w2[:, :W], PIM, None, ALU.min)
            vop("tensor_scalar", t_w2[:, :W], t_w2[:, :W], -PIM, None, ALU.max)
            if signed:
                vop("tensor_scalar", t_w2[:, :W], t_w2[:, :W], g_sgn[:], None,
                    ALU.mult)
            wait(Sc, (sV, sV.v))
            sop("activation", out[:, c0out:c0out + W], t_w2[:, :W], AF.Sin)
            tok_sin[0] = (sS, sS.v)

        def gen_table_chunk(row_t, c0, W, Cout, Sout, c0out, scale, first):
            for slc in range(0, W, 512):
                w = [(sS, sS.v)]
                if first and slc == 0:
                    w += [(sV, sV.v), (sG, sG.v)]
                wait(T, *w)
                top("matmul", p0A[:, 0:512], g_ones_row[:],
                    row_t[0:1, c0 + slc:c0 + slc + 512], start=True, stop=True)
                ws = [(sT, sT.v)]
                if slc == 0:
                    ws.append(tok_ang[0])
                wait(Sc, *ws)
                sop("activation", t_posb[:, slc:slc + 512], p0A[:, 0:512], AF.Identity)
            bc_tok = (sS, sS.v)
            wait(V, bc_tok, tok_sin[0])
            vop("tensor_scalar", t_Ahi[:, :W], t_posb[:, :W], t_fhi[:], None, ALU.mult)
            vop("tensor_scalar", t_Alo[:, :W], t_posb[:, :W], t_flo[:], None, ALU.mult)
            tok_ang[0] = (sV, sV.v)
            reduce_and_sin(Sout, c0out, W, True, False)
            emit_ln(8)
            reduce_and_sin(Cout, c0out, W, False, True)
            if scale:
                wait(V, tok_sin[0])
                vop("tensor_scalar", Cout[:, c0out:c0out + W], Cout[:, c0out:c0out + W],
                    g_r128b[:], None, ALU.mult)
                vop("tensor_scalar", Sout[:, c0out:c0out + W], Sout[:, c0out:c0out + W],
                    g_r128b[:], None, ALU.mult)
            emit_ln(8)

        first = True
        for c0 in range(0, S, 2048):
            gen_table_chunk(g_pos, c0, 2048, g_Ck, g_Sk, c0, False, first)
            first = False
        gen_table_chunk(g_qpos, 0, OWN, g_Cq, g_Sq, 0, True, False)
        emit_ln(len(ln_jobs))
        emit_lnB()
        barrier_all()

    if upto <= 1:
        return nc, st, RECS

    def proj_pass(Wsrc, xln_src, n_win, kind):
        with ExitStack() as pp:
            sbp = lambda name, shape, dt: pp.enter_context(
                nc.sbuf_tensor(f"{kind}_" + name, shape, dt))
            Wsb = sbp("pp_W", [128, NIC * D], BF16)
            xT = [sbp(f"pp_xT{i}", [128, NIC * 512], BF16) for i in range(2)]
            qb = [sbp(f"pp_qb{i}", [128, 512], F32) for i in range(2)]
            qs = [sbp(f"pp_qs{i}", [128, 512], F32) for i in range(2)]
            t1 = [sbp(f"pp_t1{i}", [128, 512], F32) for i in range(2)]
            t2 = [sbp(f"pp_t2{i}", [128, 512], F32) for i in range(2)]
            ro = [sbp(f"pp_ro{i}", [128, 512], BF16) for i in range(2)]
            wsem = dm["w0"]
            for part in range(4):
                rows = Wsrc.ap()[part * 512:(part + 1) * 512, :].rearrange(
                    "(c p) o -> p c o", p=128)
                dst3 = Wsb[:, part * 4 * D:(part + 1) * 4 * D].rearrange(
                    "p (c o) -> p c o", c=4)
                wsem.inc(G.dma_start(dst3, rows), 16)
            xsem = [dm["xt0"], dm["xt1"]]
            xrel = [(sT, 0), (sT, 0)]
            psrel = [(sT, 0)] * 6
            orel = [(dm["q0"], 0), (dm["q1"], 0)]
            osem = [dm["q0"], dm["q1"]]
            if kind == "k":
                Ct, St, bcol = g_Ck, g_Sk, g_bkT
            elif kind == "q":
                Ct, St, bcol = g_Cq, g_Sq, g_bqT
            else:
                Ct, St, bcol = None, None, None

            for w in range(n_win):
                sl = w % 2
                wait(Y, xrel[sl])
                for ic in range(NIC):
                    xsem[sl].inc(Y.dma_start(
                        xT[sl][:, ic * 512:(ic + 1) * 512],
                        xln_src.ap()[w * 512:(w + 1) * 512, ic * 128:(ic + 1) * 128],
                        transpose=True), 16)
                if kind in ("q", "k"):
                    for h in range(NH):
                        bank = (w * NH + h) % 6
                        wait(T, (xsem[sl], xsem[sl].v), (wsem, wsem.v), psrel[bank])
                        for ic in range(NIC):
                            mm = T.matmul(pb[bank][:],
                                          Wsb[:, ic * D + h * 128: ic * D + (h + 1) * 128],
                                          xT[sl][:, ic * 512:(ic + 1) * 512],
                                          start=(ic == 0), stop=(ic == NIC - 1))
                        sT.inc(mm)
                        mmv = sT.v
                        eb = (w * NH + h) % 2
                        wait(Sc, (sT, mmv), orel[eb])
                        sop("activation", qb[eb][:], pb[bank][:], AF.Identity,
                            bias=bcol[:, h:h + 1])
                        psrel[bank] = (sS, sS.v)
                        wait(V, (sS, sS.v))
                        vop("stream_shuffle", qs[eb][:], qb[eb][:],
                            [p ^ 1 for p in range(32)])
                        vop("tensor_tensor", t1[eb][:], qb[eb][:],
                            Ct[:, w * 512:(w + 1) * 512], ALU.mult)
                        wait(G, (sV, sV.v))
                        gop("tensor_tensor", t2[eb][:], qs[eb][:],
                            St[:, w * 512:(w + 1) * 512], ALU.mult)
                        wait(V, (sG, sG.v))
                        vop("tensor_tensor", ro[eb][:], t1[eb][:], t2[eb][:], ALU.add)
                        wait(Sc, (sV, sV.v))
                        dst = d_qT if kind == "q" else d_kT
                        osem[eb].inc(Sc.dma_start(
                            dst.ap()[h * 128:(h + 1) * 128, w * 512:(w + 1) * 512],
                            ro[eb][:]), 16)
                        orel[eb] = (osem[eb], osem[eb].v)
                    xrel[sl] = (sT, sT.v)
                else:
                    for tb in range(4):
                        for dvc in range(4):
                            bank = (tb * 4 + dvc) % 6
                            wait(T, (xsem[sl], xsem[sl].v), (wsem, wsem.v), psrel[bank])
                            for ic in range(NIC):
                                mm = T.matmul(
                                    pb[bank][:],
                                    xT[sl][:, ic * 512 + tb * 128: ic * 512 + (tb + 1) * 128],
                                    Wsb[:, ic * D + dvc * 512: ic * D + (dvc + 1) * 512],
                                    start=(ic == 0), stop=(ic == NIC - 1))
                            sT.inc(mm)
                            eb = (tb * 4 + dvc) % 2
                            wait(V, (sT, sT.v), orel[eb])
                            vop("tensor_tensor", ro[eb][:], pb[bank][:],
                                g_bvb[:, dvc * 512:(dvc + 1) * 512], ALU.add)
                            psrel[bank] = (sV, sV.v)
                            wait(Sc, (sV, sV.v))
                            osem[eb].inc(Sc.dma_start(
                                d_v.ap()[w * 512 + tb * 128: w * 512 + (tb + 1) * 128,
                                         dvc * 512:(dvc + 1) * 512], ro[eb][:]), 16)
                            orel[eb] = (osem[eb], osem[eb].v)
                    xrel[sl] = (sT, sT.v)
            barrier_all()

    proj_pass(d_WqT, d_xlnq, NW_OWN, "q")
    proj_pass(d_WkT, d_xlnkv, NW_KV, "k")
    proj_pass(d_WvT, d_xlnkv, NW_KV, "v")
    p01.close()

    if upto == 2:
        return nc, st, RECS

    wstats = {}
    for mi in range(3):
        wstats[mi] = (sb(f"p4a_swn{mi}", [128, 1], F32),
                      sb(f"p4a_tau{mi}", [128, 1], F32),
                      sb(f"p4a_invtau{mi}", [128, 1], F32))
    p4as = ExitStack()
    sba = lambda name, shape, dt: p4as.enter_context(nc.sbuf_tensor(name, shape, dt))
    wa_tile = [sba(f"wa_t{i}", [128, 2048], F32) for i in range(2)]
    wa_acc = [sba(f"wa_acc{i}", [128, 1], F32) for i in range(3)]
    wa_tmp = sba("wa_tmp", [128, 1], F32)
    wa_tot = sba("wa_tot", [1, 1], F32)
    for mi in range(3):
        gop("memset", wa_acc[mi][:], 0.0)
    wa_state = {"i": 0, "rel": [(sS, 0), (sS, 0)], "first": True}

    def wa_chunk(mi, wd, r0, c0, W):
        i = wa_state["i"]
        wa_state["i"] = i + 1
        slw = i % 2
        sem = [dm["wa0"], dm["wa1"]][slw]
        wait(G, wa_state["rel"][slw])
        sem.inc(G.dma_start(wa_tile[slw][:, :W], wd.ap()[r0:r0 + 128, c0:c0 + W]), 16)
        toks = [(sem, sem.v), (sV, sV.v)]
        if wa_state["first"]:
            toks.append((sG, sG.v))
            wa_state["first"] = False
        wait(V, *toks)
        vop("tensor_reduce", wa_tmp[:], wa_tile[slw][:, :W], AX.X, ALU.add,
            apply_absolute_value=True)
        wa_state["rel"][slw] = (sV, sV.v)
        wait(V, (sV, sV.v))
        vop("tensor_tensor", wa_acc[mi][:], wa_acc[mi][:], wa_tmp[:], ALU.add)

    wa_items = []
    for mi, (wdt, Rr, Cc) in enumerate([(d_wgT, D, DF), (d_wuT, D, DF),
                                        (d_wdT, DF, D)]):
        for ti in range(Rr // 128):
            for c in range(Cc // 2048):
                wa_items.append((lambda mi=mi, wdt=wdt, ti=ti, c=c:
                                 wa_chunk(mi, wdt, ti * 128, c * 2048, 2048)))

    def wa_feed(n):
        for _ in range(n):
            if wa_items:
                wa_items.pop(0)()

    def wa_epilogue():
        for mi, (Rr, Cc) in enumerate([(D, DF), (D, DF), (DF, D)]):
            b_swn, b_tau, b_invtau = wstats[mi]
            wait(T, prel[5], (sS, sS.v), (sV, sV.v))
            top("matmul", pb[5][0:1, 0:1], wa_acc[mi][:], g_ones_col[:],
                start=True, stop=True)
            wait(Sc, (sT, sT.v))
            sop("activation", wa_tot[:], pb[5][0:1, 0:1], AF.Identity)
            wait(V, (sS, sS.v))
            vop("tensor_scalar", wa_tot[:], wa_tot[:], 1.0 / (Rr * Cc), 1e-5,
                ALU.mult, ALU.max)
            wait(T, (sV, sV.v), (sS, sS.v))
            top("matmul", pb[5][:, 0:1], g_ones_row[:], wa_tot[:],
                start=True, stop=True)
            wait(Sc, (sT, sT.v))
            sop("activation", b_swn[:], pb[5][:, 0:1], AF.Identity,
                bias=0.0, scale=-1.0)
            wait(Sc, (sS, sS.v))
            sop("activation", b_tau[:], pb[5][:, 0:1], AF.Identity,
                bias=0.0, scale=0.5)
            prel[5] = (sS, sS.v)
            wait(V, (sS, sS.v))
            vop("reciprocal", b_invtau[:], b_tau[:])

    with ExitStack() as p2:
        sb2 = lambda name, shape, dt: p2.enter_context(nc.sbuf_tensor(name, shape, dt))
        p2A = p2.enter_context(nc.psum_tensor("p2A", [128, 512], F32))
        p2S = p2.enter_context(nc.psum_tensor("p2S", [128, 512], F32))
        masks = sb2("p2_masks", [128, 32 * 512], BF16)
        qpb = sb2("p2_qpb", [128, 512], F32)
        kcol = sb2("p2_kcol", [128, 1], F32)
        qTb = [sb2(f"p2_qT{i}", [128, OWN], BF16) for i in range(2)]
        kTb = [sb2(f"p2_kT{i}", [128, S], BF16) for i in range(2)]
        vb = [sb2(f"p2_v{i}", [128, NKB * 128], BF16) for i in range(2)]
        NEB = 8
        Eb = [sb2(f"p2_E{i}", [128, 512], BF16) for i in range(NEB)]
        dmp = sb2("p2_dump", [128, S], BF16)
        nm = [sb2(f"p2_nm{i}", [1, 1], F32) for i in range(8)]
        negB = [sb2(f"p2_negB{i}", [128, 1], F32) for i in range(2)]
        invr = sb2("p2_invr", [1, 512], F32)
        bcb = sb2("p2_bcb", [128, 512], F32)
        ctxb = [sb2(f"p2_ctx{i}", [128, 512], BF16) for i in range(2)]

        p2Arel = [(sS, 0)]
        for t in range(4):
            wait(T, p2Arel[0])
            top("matmul", p2A[:], g_ones_row[:],
                g_qpos[0:1, t * 512:(t + 1) * 512], start=True, stop=True)
            wait(Sc, (sT, sT.v), (sV, sV.v))
            sop("activation", qpb[:], p2A[:], AF.Identity)
            p2Arel[0] = (sS, sS.v)
            for j in range(8):
                kb = 8 * t + j
                wait(V, (sS, sS.v), (sV, sV.v))
                vop("tensor_scalar", kcol[:], g_iota[:], float(kb * 128), None, ALU.add)
                wait(V, (sV, sV.v))
                vop("tensor_scalar", masks[:, (t * 8 + j) * 512:(t * 8 + j + 1) * 512],
                    qpb[:], kcol[:], None, ALU.is_ge)
                wait(V, (sV, sV.v))

        hsem = [[dm["q0"], dm["kv0"], dm["p0"]], [dm["q1"], dm["kv1"], dm["p1"]]]
        hrel = [(sT, 0), (sT, 0)]
        p2rel = [(sT, 0)]

        def load_head(h):
            sl = h % 2
            wait(Y, hrel[sl])
            hsem[sl][0].inc(Y.dma_start(qTb[sl][:], d_qT.ap()[h * 128:(h + 1) * 128, :]), 16)
            hsem[sl][1].inc(Y.dma_start(kTb[sl][:], d_kT.ap()[h * 128:(h + 1) * 128, :]), 16)
            for part in range(4):
                src = d_v.ap()[part * 1024:(part + 1) * 1024,
                               h * 128:(h + 1) * 128].rearrange(
                    "(kb p) d -> p kb d", p=128)
                dst3 = vb[sl][:, part * 1024:(part + 1) * 1024].rearrange(
                    "p (kb d) -> p kb d", kb=8)
                hsem[sl][2].inc(Y.dma_start(dst3, src), 16)

        load_head(0)
        Erel = [(sS, 0)] * NEB
        pending2 = []
        eitems = []
        ctxrel = [(dm["c0"], 0), (dm["c1"], 0)]
        csem = [dm["c0"], dm["c1"]]
        nbrel = [(sV, 0)]
        negB_toks = [(sS, 0), (sS, 0)]

        def normB(h):
            sl = h % 2
            wq = (hsem[sl][0], hsem[sl][0].v)
            wk = (hsem[sl][1], hsem[sl][1].v)
            NB = [5, 0, 1]
            pending = []
            done = [False, False]

            def nred(i, which, bk, tv):
                out = nm[sl * 4 + which]
                wait(V, (sT, tv), (sV, sV.v))
                if not done[which]:
                    vop("tensor_reduce", out[:], pb[bk][0:1, :], AX.X, ALU.max)
                    done[which] = True
                else:
                    vop("tensor_reduce", nm[sl * 4 + 2][:], pb[bk][0:1, :],
                        AX.X, ALU.max)
                    vop("tensor_tensor", out[:], out[:], nm[sl * 4 + 2][:],
                        ALU.max)
                prel[bk] = (sV, sV.v)

            i = 0
            for which, buf, W in [(0, qTb[sl], OWN), (1, kTb[sl], S)]:
                wait(V, wq, wk, (sV, sV.v), (sT, sT.v))
                vop("tensor_tensor", dmp[:, :W], buf[:], buf[:], ALU.mult)
                vsq = (sV, sV.v)
                for slc in range(W // 512):
                    bk = NB[i % 3]
                    if len(pending) >= 3:
                        nred(*pending.pop(0))
                    wait(T, vsq, prel[bk])
                    top("matmul", pb[bk][0:1, :], g_ones_colb[:],
                        dmp[:, slc * 512:(slc + 1) * 512], start=True, stop=True)
                    pending.append((i, which, bk, sT.v))
                    i += 1
            while pending:
                nred(*pending.pop(0))
            wait(Sc, (sV, sV.v), (sS, sS.v))
            sop("activation", nm[sl * 4 + 0][:], nm[sl * 4 + 0][:], AF.Ln)
            wait(Sc, (sS, sS.v))
            sop("activation", nm[sl * 4 + 1][:], nm[sl * 4 + 1][:], AF.Ln)
            wait(V, (sS, sS.v))
            vop("tensor_tensor", nm[sl * 4 + 3][:], nm[sl * 4 + 0][:],
                nm[sl * 4 + 1][:], ALU.add)
            vop("tensor_scalar", nm[sl * 4 + 3][:], nm[sl * 4 + 3][:], 0.5,
                None, ALU.mult)
            wait(Sc, (sV, sV.v))
            sop("activation", nm[sl * 4 + 3][:], nm[sl * 4 + 3][:], AF.Exp)
            wait(T, (sS, sS.v), p2Arel[0])
            top("matmul", p2A[:, 0:1], g_ones_row[:], nm[sl * 4 + 3][:],
                start=True, stop=True)
            wait(Sc, (sT, sT.v), negB_toks[sl])
            sop("activation", negB[sl][:], p2A[:, 0:1], AF.Identity,
                bias=0.0, scale=-1.0)
            p2Arel[0] = (sS, sS.v)
            negB_toks[sl] = (sS, sS.v)

        normB(0)
        for h in range(NH):
            sl = h % 2
            if h + 1 < NH:
                load_head(h + 1)
                normB(h + 1)
            wq = (hsem[sl][0], hsem[sl][0].v)
            wk = (hsem[sl][1], hsem[sl][1].v)
            wv = (hsem[sl][2], hsem[sl][2].v)
            negB_tok = negB_toks[sl]

            SCB = [0, 1, 4]

            def drain(pend, first_sum):
                pkb, pt, etok = pend[0] if isinstance(pend[0], tuple) else pend
                eb = pkb % NEB
                K_t = 8 * (pt + 1)
                wait(T, etok, wv)
                if first_sum:
                    wait(T, p2rel[0])
                if pkb == 0:
                    wait(T, prel[2 + pt % 2])
                top("matmul", p2S[0:1, :], g_ones_colb[:], Eb[eb][:],
                    start=(pkb == 0), stop=(pkb == K_t - 1))
                top("matmul", pb[2 + pt % 2][:],
                    vb[sl][:, pkb * 128:(pkb + 1) * 128], Eb[eb][:],
                    start=(pkb == 0), stop=(pkb == K_t - 1))
                Erel[eb] = (sT, sT.v)

            for t in range(4):
                K_t = 8 * (t + 1)
                for kb in range(K_t):
                    bank = SCB[kb % 3]
                    wait(T, prel[bank], wq, wk)
                    top("matmul", pb[bank][:],
                        kTb[sl][:, kb * 128:(kb + 1) * 128],
                        qTb[sl][:, t * 512:(t + 1) * 512], start=True, stop=True)
                    qk_tok = sT.v
                    if kb >= 3 and eitems:
                        eitems.pop(0)()
                    while len(pending2) >= 3:
                        p_ = pending2.pop(0)
                        drain(p_, p_[1])
                    eb = kb % NEB
                    wait(Sc, (sT, qk_tok), Erel[eb], negB_tok)
                    sop("activation", Eb[eb][:], pb[bank][:], AF.Exp,
                        bias=negB[sl][:])
                    prel[bank] = (sS, sS.v)
                    etok = (sS, sS.v)
                    if kb >= 8 * t:
                        j = kb - 8 * t
                        wait(V, (sS, sS.v), (sV, sV.v))
                        vop("tensor_tensor", Eb[eb][:], Eb[eb][:],
                            masks[:, (t * 8 + j) * 512:(t * 8 + j + 1) * 512], ALU.mult)
                        etok = (sV, sV.v)
                    pending2.append(((kb, t, etok), kb == 0))
                def mk_epi(h, t, cb):
                    def e1():
                        wait(V, (sT, sT.v), (sV, sV.v))
                        vop("reciprocal", invr[:], p2S[0:1, :])
                        p2rel[0] = (sV, sV.v)
                    def e2():
                        wait(T, p2Arel[0], (sV, sV.v))
                        top("matmul", p2A[:], g_ones_row[:], invr[:],
                            start=True, stop=True)
                    def e3():
                        wait(Sc, (sT, sT.v))
                        sop("activation", bcb[:], p2A[:], AF.Identity)
                        p2Arel[0] = (sS, sS.v)
                    def e4():
                        wait(V, (sS, sS.v), ctxrel[cb])
                        vop("tensor_tensor", ctxb[cb][:], pb[2 + t % 2][:],
                            bcb[:], ALU.mult)
                        prel[2 + t % 2] = (sV, sV.v)
                        wait(Sc, (sV, sV.v))
                        csem[cb].inc(Sc.dma_start(
                            d_ctxT.ap()[h * 128:(h + 1) * 128,
                                        t * 512:(t + 1) * 512], ctxb[cb][:]), 16)
                        ctxrel[cb] = (csem[cb], csem[cb].v)
                    return [e1, e2, e3, e4]
                if t == 3:
                    while pending2:
                        p_ = pending2.pop(0)
                        drain(p_, p_[1])
                eitems.extend(mk_epi(h, t, t % 2))
            hrel[sl] = (sT, sT.v)
            wa_feed(12)
        while pending2:
            p_ = pending2.pop(0)
            drain(p_, p_[1])
        while eitems:
            eitems.pop(0)()
        barrier_all()

    if upto == 3:
        return nc, st, RECS

    with ExitStack() as p3:
        sb3 = lambda name, shape, dt: p3.enter_context(nc.sbuf_tensor(name, shape, dt))
        Wosb = sb3("p3_Wo", [128, NIC * D], BF16)
        ctin = [sb3(f"p3_ct{i}", [128, NIC * 128], BF16) for i in range(2)]
        xqb = [sb3(f"p3_xq{i}", [128, D], F32) for i in range(2)]
        x1b = [sb3(f"p3_x1{i}", [128, D], F32) for i in range(2)]
        tqb = sb3("p3_tq", [128, D], F32)
        qib = sb3("p3_qi", [128, D], I32)
        xib = [sb3(f"p3_xi{i}", [128, D], BF16) for i in range(2)]
        NTB = OWN // 128
        sw_ = [sb3(f"p3_sw{i}", [128, NTB], F32) for i in range(8)]
        dump3 = sb3("p3_dump", [128, D], F32)

        wsem = dm["w0"]
        for part in range(4):
            rows = d_WoT.ap()[part * 512:(part + 1) * 512, :].rearrange(
                "(c p) o -> p c o", p=128)
            dst3 = Wosb[:, part * 4 * D:(part + 1) * 4 * D].rearrange(
                "p (c o) -> p c o", c=4)
            wsem.inc(G.dma_start(dst3, rows), 16)

        wa_feed(len(wa_items))
        wa_epilogue()
        cin = [dm["c0"], dm["c1"]]
        xin = [dm["x0"], dm["x1"]]
        cinrel = [(sT, 0), (sT, 0)]
        xinrel = [(sV, 0), (sV, 0)]
        x1rel = [[(dm["st0"], 0)], [(dm["st1"], 0)]]
        xirel = [(dm["q0"], 0), (dm["q1"], 0)]
        x1sem = [dm["st0"], dm["st1"]]
        xisem = [dm["q0"], dm["q1"]]
        for tb in range(NTB):
            sl = tb % 2
            wait(Y, cinrel[sl])
            src = d_ctxT.ap()[:, tb * 128:(tb + 1) * 128].rearrange(
                "(c p) t -> p c t", p=128)
            dst3 = ctin[sl][:].rearrange("p (c t) -> p c t", c=NIC)
            cin[sl].inc(Y.dma_start(dst3, src), 16)
            wait(Y, xinrel[sl])
            xin[sl].inc(Y.dma_start(xqb[sl][:], d_xq.ap()[tb * 128:(tb + 1) * 128, :]), 16)
            for dmc in range(4):
                bank = (tb * 4 + dmc) % 6
                wait(T, (cin[sl], cin[sl].v), (wsem, wsem.v), prel[bank])
                for cfc in range(NIC):
                    mm = T.matmul(pb[bank][:],
                                  ctin[sl][:, cfc * 128:(cfc + 1) * 128],
                                  Wosb[:, cfc * D + dmc * 512: cfc * D + (dmc + 1) * 512],
                                  start=(cfc == 0), stop=(cfc == NIC - 1))
                sT.inc(mm)
                wait(V, (sT, sT.v), (xin[sl], xin[sl].v), (sV, sV.v),
                     *(x1rel[sl] if dmc == 0 else []))
                vop("tensor_tensor", x1b[sl][:, dmc * 512:(dmc + 1) * 512],
                    pb[bank][:], g_bob[:, dmc * 512:(dmc + 1) * 512], ALU.add)
                prel[bank] = (sV, sV.v)
                wait(V, (sV, sV.v))
                vop("tensor_tensor", x1b[sl][:, dmc * 512:(dmc + 1) * 512],
                    x1b[sl][:, dmc * 512:(dmc + 1) * 512],
                    xqb[sl][:, dmc * 512:(dmc + 1) * 512], ALU.add)
            xinrel[sl] = (sV, sV.v)
            cinrel[sl] = (sT, sT.v)
            x1tok = sV.v
            wait(Sc, (sV, x1tok))
            x1sem[sl].inc(Sc.dma_start(d_x1.ap()[tb * 128:(tb + 1) * 128, :], x1b[sl][:]), 16)
            wait(Sc, (sV, x1tok))
            sop("activation", dump3[:], x1b[sl][:], AF.Identity,
                accum_out=sw_[0][:, tb:tb + 1])
            wait(Sc, (sS, sS.v))
            sop("activation", dump3[:], x1b[sl][:], AF.Square,
                accum_out=sw_[1][:, tb:tb + 1])
            wait(V, (sV, sV.v), (sS, sS.v))
            vop("tensor_reduce", sw_[2][:, tb:tb + 1], x1b[sl][:], AX.X, ALU.max)
            wait(V, (sV, sV.v))
            vop("tensor_reduce", sw_[3][:, tb:tb + 1], x1b[sl][:], AX.X, ALU.min)
            x1rel[sl] = [(x1sem[sl], x1sem[sl].v), (sS, sS.v), (sV, sV.v)]
        wait(V, (sV, sV.v), (sS, sS.v))
        vop("tensor_scalar", sw_[4][:], sw_[0][:], 1.0 / 2048.0, None, ALU.mult)
        vop("tensor_tensor", sw_[5][:], sw_[4][:], sw_[4][:], ALU.mult)
        vop("tensor_scalar", sw_[6][:], sw_[1][:], 1.0 / 2048.0, None, ALU.mult)
        vop("tensor_tensor", sw_[6][:], sw_[6][:], sw_[5][:], ALU.subtract)
        vop("tensor_scalar", sw_[6][:], sw_[6][:], 1e-5, None, ALU.add)
        wait(Sc, (sV, sV.v))
        sop("activation", sw_[6][:], sw_[6][:], AF.Ln)
        wait(Sc, (sS, sS.v))
        sop("activation", sw_[6][:], sw_[6][:], AF.Exp, bias=0.0, scale=-0.5)
        wait(V, (sS, sS.v))
        vop("tensor_tensor", sw_[2][:], sw_[2][:], sw_[4][:], ALU.subtract)
        vop("tensor_tensor", sw_[3][:], sw_[4][:], sw_[3][:], ALU.subtract)
        vop("tensor_tensor", sw_[2][:], sw_[2][:], sw_[3][:], ALU.max)
        vop("tensor_tensor", sw_[2][:], sw_[2][:], sw_[6][:], ALU.mult)
        vop("tensor_scalar", sw_[2][:], sw_[2][:], 1e-5, None, ALU.max)
        vop("tensor_scalar", g_sxinv[:, 0:NTB], sw_[2][:], g_rec127[:], None,
            ALU.mult)
        vop("reciprocal", sw_[5][:], g_sxinv[:, 0:NTB])
        vop("tensor_tensor", sw_[6][:], sw_[6][:], sw_[5][:], ALU.mult)
        vop("tensor_tensor", sw_[7][:], sw_[4][:], sw_[6][:], ALU.mult)
        vop("tensor_scalar", sw_[7][:], sw_[7][:], -1.0, None, ALU.mult)
        stattok = (sV, sV.v)
        wait(Y, (x1sem[0], x1sem[0].v), (x1sem[1], x1sem[1].v),
             (sS, sS.v), (sV, sV.v))
        x1crel = [(sV, 0), (sV, 0)]
        for tb in range(NTB):
            sl = tb % 2
            wait(Y, x1crel[sl])
            xin[sl].inc(Y.dma_start(x1b[sl][:], d_x1.ap()[tb * 128:(tb + 1) * 128, :]), 16)
            wait(V, (xin[sl], xin[sl].v), (sV, sV.v), stattok)
            vop("tensor_scalar", tqb[:], x1b[sl][:], sw_[6][:, tb:tb + 1],
                sw_[7][:, tb:tb + 1], ALU.mult, ALU.add)
            vop("tensor_copy", qib[:], tqb[:])
            vop("tensor_tensor", tqb[:], tqb[:], qib[:], ALU.subtract)
            vop("tensor_scalar", tqb[:], tqb[:], MAGIC, -MAGIC, ALU.add, ALU.add)
            wait(V, xirel[sl])
            vop("tensor_tensor", xib[sl][:], qib[:], tqb[:], ALU.add)
            x1crel[sl] = (sV, sV.v)
            wait(Sc, (sV, sV.v))
            xisem[sl].inc(Sc.dma_start(d_xint.ap()[tb * 128:(tb + 1) * 128, :],
                                       xib[sl][:]), 16)
            xirel[sl] = (xisem[sl], xisem[sl].v)
        barrier_all()
    p4as.close()

    if upto == 4:
        return nc, st, RECS

    swn_g, tau_g, invtau_g = wstats[0]
    swn_u, tau_u, invtau_u = wstats[1]
    swn_d, tau_d, invtau_d = wstats[2]
    hamax = g_hamax
    with ExitStack() as p4:
        sb4 = lambda name, shape, dt: p4.enter_context(nc.sbuf_tensor(name, shape, dt))
        p4b6 = p4.enter_context(nc.psum_tensor("p4b6", [128, 512], F32))
        p4b7 = p4.enter_context(nc.psum_tensor("p4b7", [128, 512], F32))
        pbx = pb + [p4b6, p4b7]
        xiT = [sb4(f"p4_xiT{w}", [128, NIC * 512], BF16) for w in range(NW_OWN)]
        wfb = [sb4(f"p4_wf{i}", [128, 2048], F32) for i in range(2)]
        pmb = [sb4(f"p4_pm{i}", [128, 2048], F32) for i in range(2)]
        wgt = [sb4(f"p4_wgt{i}", [128, NIC * 512], BF16) for i in range(2)]
        wut = [sb4(f"p4_wut{i}", [128, NIC * 512], BF16) for i in range(2)]
        sgb = [sb4(f"p4_sg{i}", [128, 512], F32) for i in range(2)]
        htb = [sb4(f"p4_ht{i}", [128, 512], F32) for i in range(2)]
        amx = sb4("p4_amx", [128, 1], F32)
        cgc = [sb4(f"p4_cg{i}", [128, 1], F32) for i in range(2)]
        cuc = [sb4(f"p4_cu{i}", [128, 1], F32) for i in range(2)]
        gop("memset", hamax[:], 0.0)

        xts = dm["xt0"]
        for wnd in range(NW_OWN):
            for ic in range(NIC):
                xts.inc(Y.dma_start(
                    xiT[wnd][:, ic * 512:(ic + 1) * 512],
                    d_xint.ap()[wnd * 512:(wnd + 1) * 512, ic * 128:(ic + 1) * 128],
                    transpose=True), 16)

        tern_state = {"i": 0, "rel": [(sG, 0), (sG, 0)]}

        def ternarize(wsrc_ap, outbuf, ooff, W, invtau, wsem, flat=False):
            i = tern_state["i"]
            tern_state["i"] = i + 1
            wslot = i % 2
            wsem = [dm["tn0"], dm["tn1"]][wslot]
            wait(Y, tern_state["rel"][wslot])
            dst = wfb[wslot][:, :W] if flat else wfb[wslot][:, :W].rearrange(
                "p (c o) -> p c o", o=512)
            wsem.inc(Y.dma_start(dst, wsrc_ap), 16)
            wait(V, (wsem, wsem.v), (sV, sV.v))
            vop("tensor_scalar", pmb[wslot][:, :W], wfb[wslot][:, :W], invtau[:], 1.0,
                ALU.mult, ALU.is_ge)
            vop("tensor_scalar", wfb[wslot][:, :W], wfb[wslot][:, :W], invtau[:], -1.0,
                ALU.mult, ALU.is_le)
            wait(G, (sV, sV.v), (sG, sG.v))
            gop("tensor_tensor", outbuf[:, ooff:ooff + W], wfb[wslot][:, :W],
                pmb[wslot][:, :W], ALU.subtract)
            tern_state["rel"][wslot] = (sG, sG.v)

        hstrel = [(dm["h0"], 0), (dm["h1"], 0)]
        wdtb = [sb4(f"p4_wdtb{i}", [128, 2048], BF16) for i in range(2)]
        wdsem = [dm["dd0"], dm["dd1"]]
        wdst = [dm["p0"], dm["p1"]]
        wdrel = [(dm["p0"], 0), (dm["p1"], 0)]

        def wd_tern_tile(ti):
            slw = ti % 2
            wait(G, wdrel[slw])
            ternarize(d_wdT.ap()[ti * 128:(ti + 1) * 128, :], wdtb[slw], 0, 2048,
                      invtau_d, wdsem[slw], flat=True)
            wait(Sc, (sG, sG.v))
            for dmc in range(4):
                wdst[slw].inc(Sc.dma_start(
                    d_wdt.ap()[dmc, ti // 8, :, (ti % 8) * 512:(ti % 8 + 1) * 512],
                    wdtb[slw][:, dmc * 512:(dmc + 1) * 512]), 16)
            wdrel[slw] = (wdst[slw], wdst[slw].v)

        striperel = [(sT, 0), (sT, 0)]
        terntoks = {}

        def tern_items(fw):
            slf = fw % 2
            gsem = [dm["g0"], dm["g1"]][slf]
            usem = [dm["u0"], dm["u1"]][slf]
            first = [True]
            items = []
            for ti in range(fw * 4, fw * 4 + 4):
                items.append(lambda ti=ti: wd_tern_tile(ti))

            def mk(wT, obuf, grp, invt, wsem):
                def f():
                    if first[0]:
                        wait(G, striperel[slf])
                        first[0] = False
                    src3 = wT.ap()[grp * 512:(grp + 1) * 512,
                                   fw * 512:(fw + 1) * 512].rearrange(
                        "(c p) o -> p c o", p=128)
                    ternarize(src3, obuf, grp * 2048, 2048, invt, wsem)
                return f
            for grp in range(4):
                items.append(mk(d_wgT, wgt[slf], grp, invtau_g, gsem))
            for grp in range(4):
                items.append(mk(d_wuT, wut[slf], grp, invtau_u, usem))
            return items

        def emit_tern(fw):
            for it_ in tern_items(fw):
                it_()
            terntoks[fw] = (sG, sG.v)

        emit_tern(0)
        for fw in range(DF // 512):
            slf = fw % 2
            pend = tern_items(fw + 1) if fw + 1 < DF // 512 else []
            terntok = terntoks[fw]
            for wnd in range(NW_OWN):
                for tb in range(4):
                    tok = wnd * 4 + tb
                    bg = (tb % 4)
                    bu = 4 + (tb % 4)
                    eb = tb % 2
                    wait(T, (xts, xts.v), terntok, prel[bg], prel[bu])
                    for ic in range(NIC):
                        mm = T.matmul(pbx[bg][:],
                                      xiT[wnd][:, ic * 512 + tb * 128:
                                               ic * 512 + (tb + 1) * 128],
                                      wgt[slf][:, ic * 512:(ic + 1) * 512],
                                      start=(ic == 0), stop=(ic == NIC - 1))
                    sT.inc(mm)
                    gmm = sT.v
                    for ic in range(NIC):
                        mm = T.matmul(pbx[bu][:],
                                      xiT[wnd][:, ic * 512 + tb * 128:
                                               ic * 512 + (tb + 1) * 128],
                                      wut[slf][:, ic * 512:(ic + 1) * 512],
                                      start=(ic == 0), stop=(ic == NIC - 1))
                    sT.inc(mm)
                    umm = sT.v
                    wait(V, (sV, sV.v), (sS, sS.v))
                    vop("tensor_tensor", cgc[eb][:], g_sxinv[:, tok:tok + 1],
                        swn_g[:], ALU.mult)
                    vop("tensor_tensor", cuc[eb][:], g_sxinv[:, tok:tok + 1],
                        swn_u[:], ALU.mult)
                    wait(Sc, (sT, gmm), (sV, sV.v))
                    sop("activation", sgb[eb][:], pbx[bg][:], AF.Silu, bias=0.0,
                        scale=cgc[eb][:])
                    prel[bg] = (sS, sS.v)
                    wait(V, (sS, sS.v), (sT, umm), hstrel[eb])
                    vop("scalar_tensor_tensor", htb[eb][:], pbx[bu][:], cuc[eb][:],
                        sgb[eb][:], ALU.mult, ALU.mult)
                    prel[bu] = (sV, sV.v)
                    vop("tensor_reduce", amx[:], htb[eb][:], AX.X, ALU.max,
                        apply_absolute_value=True)
                    vop("tensor_tensor", hamax[:, tok:tok + 1],
                        hamax[:, tok:tok + 1], amx[:], ALU.max)
                    wait(Sc, (sV, sV.v))
                    hsm = [dm["h0"], dm["h1"]][eb]
                    hsm.inc(Sc.dma_start(
                        d_hfp.ap()[wnd * 512 + tb * 128: wnd * 512 + (tb + 1) * 128,
                                   fw * 512:(fw + 1) * 512], htb[eb][:]), 16)
                    hstrel[eb] = (hsm, hsm.v)
                    if pend:
                        pend.pop(0)()
            while pend:
                pend.pop(0)()
            if fw + 1 < DF // 512:
                terntoks[fw + 1] = (sG, sG.v)
            striperel[slf] = (sT, sT.v)
        barrier_all()

    with ExitStack() as p4e:
        sb4 = lambda name, shape, dt: p4e.enter_context(nc.sbuf_tensor(name, shape, dt))
        ptb = [p4e.enter_context(nc.psum_tensor(f"ptb{i}", [128, 1024], BF16))
               for i in range(2)]
        NQ = 1024
        hq = [sb4(f"p4_hq{i}", [128, NQ], F32) for i in range(2)]
        hqt = sb4("p4_hqt", [128, NQ], F32)
        hqi = sb4("p4_hqi", [128, NQ], I32)
        hqo = sb4("p4_hqo", [128, NQ], BF16)
        st4 = [sb4(f"p4_st{i}", [128, 1], F32) for i in range(4)]
        hTg = [sb4(f"p4_hTg{i}", [128, 64 * 512], BF16) for i in range(2)]
        wdstr = [sb4(f"p4_wds{i}", [128, 8 * 512], BF16) for i in range(3)]
        x1in = [sb4(f"p4_x1i{i}", [128, 512], F32) for i in range(2)]
        evb = [sb4(f"p4_ev{i}", [128, 512], F32) for i in range(2)]
        outb = [sb4(f"p4_ob{i}", [128, 512], F32) for i in range(2)]
        cdc = sb4("p4_cd", [128, 8], F32)

        hin = [dm["h0"], dm["h1"]]
        hinrel = [(sV, 0), (sV, 0)]
        x1s = [dm["x0"], dm["x1"]]
        x1rel2 = [(sG, 0), (sG, 0)]
        wdls = [dm["wr0"], dm["wr1"], dm["wr2"]]
        wdlrel = [(sT, 0)] * 3
        wdcnt = [0]
        osem = [dm["o0"], dm["o1"]]
        obrel = [(dm["o0"], 0), (dm["o1"], 0)]
        ptbrel = [(sS, 0), (sS, 0)]
        hTgev = [(sS, 0), (sS, 0)]
        hTgrd = [(sT, 0), (sT, 0)]
        qtr = [(sT, 0)]

        def q_load(tb, qq):
            slh = (tb * (DF // NQ) + qq) % 2
            ring = Y if slh == 0 else Sc
            wait(ring, hinrel[slh])
            hin[slh].inc(ring.dma_start(
                hq[slh][:], d_hfp.ap()[tb * 128:(tb + 1) * 128,
                                       qq * NQ:(qq + 1) * NQ]), 16)

        def q_comp(tb, qq, lead):
            tbl = tb % 4
            slh = (tb * (DF // NQ) + qq) % 2
            if qq == 0:
                wait(V, (sV, sV.v))
                vop("tensor_scalar", st4[0][:], hamax[:, tb:tb + 1], 1e-5, None,
                    ALU.max)
                vop("reciprocal", st4[1][:], st4[0][:])
                vop("tensor_scalar", st4[1][:], st4[1][:], 127.0, None, ALU.mult)
                vop("tensor_tensor", g_shinv[:, tb:tb + 1], st4[0][:], g_rec127[:],
                    ALU.mult)
            wait(V, (hin[slh], hin[slh].v), (sV, sV.v), qtr[0])
            vop("tensor_scalar", hqt[:], hq[slh][:], st4[1][:], None, ALU.mult)
            hinrel[slh] = (sV, sV.v)
            vop("tensor_copy", hqi[:], hqt[:])
            vop("tensor_tensor", hqt[:], hqt[:], hqi[:], ALU.subtract)
            vop("tensor_scalar", hqt[:], hqt[:], MAGIC, -MAGIC, ALU.add, ALU.add)
            vop("tensor_tensor", hqo[:], hqi[:], hqt[:], ALU.add)
            qtok = (sV, sV.v)
            for j2 in range(2):
                wait(T, qtok, ptbrel[j2], hTgrd[lead] if qq == 0 and j2 == 0
                     else (sT, 0))
                for j in range(4):
                    tr = T.transpose(ptb[j2][:, j * 128:(j + 1) * 128],
                                     hqo[:, (j2 * 4 + j) * 128:
                                         (j2 * 4 + j + 1) * 128],
                                     g_ident[:])
                sT.inc(tr)
                wait(Sc, (sT, sT.v), (sS, sS.v))
                for j in range(4):
                    fc = qq * 8 + j2 * 4 + j
                    ev = Sc.activation(
                        hTg[lead][:, fc * 512 + tbl * 128:
                                  fc * 512 + (tbl + 1) * 128],
                        ptb[j2][:, j * 128:(j + 1) * 128], AF.Identity)
                sS.inc(ev)
                ptbrel[j2] = (sS, sS.v)
            qtr[0] = (sT, sT.v)

        def quant_tokb(tb, lead):
            for qq in range(DF // NQ):
                q_load(tb, qq)
                q_comp(tb, qq, lead)
            hTgev[lead] = (sS, sS.v)

        for tbl in range(4):
            quant_tokb(tbl, 0)
        for tw in range(4):
            cdg = (tw % 2) * 4
            for tbl in range(4):
                tb = tw * 4 + tbl
                wait(V, (sV, sV.v), (sS, sS.v))
                vop("tensor_tensor", cdc[:, cdg + tbl:cdg + tbl + 1],
                    g_shinv[:, tb:tb + 1], swn_d[:], ALU.mult)
            cdtok = (sV, sV.v)
            lead = tw % 2
            qjobs = []
            ncomp = [0]
            if tw + 1 < 4:
                qjobs = [((tw + 1) * 4 + tbl2, qq) for tbl2 in range(4)
                         for qq in range(DF // NQ)]
                for k in range(2):
                    q_load(*qjobs[k])

            def q_step(n):
                for _ in range(n):
                    k = ncomp[0]
                    if k >= len(qjobs):
                        return
                    ncomp[0] = k + 1
                    q_comp(*qjobs[k], lead=(tw + 1) % 2)
                    if k + 2 < len(qjobs):
                        q_load(*qjobs[k + 2])
            for dmc in range(4):
                wait(T, hTgev[lead], prel[0], prel[1], prel[2], prel[3])
                for wg8 in range(8):
                    slw = wdcnt[0] % 3
                    wdcnt[0] += 1
                    ring = Sc if wg8 % 2 else Y
                    wait(ring, wdlrel[slw])
                    wdls[slw].inc(ring.dma_start(wdstr[slw][:],
                                                 d_wdt.ap()[dmc, wg8]), 16)
                    wait(T, (wdls[slw], wdls[slw].v))
                    for c8 in range(8):
                        fc = wg8 * 8 + c8
                        for tbl in range(4):
                            mm = T.matmul(pb[tbl][:],
                                          hTg[lead][:, fc * 512 + tbl * 128:
                                              fc * 512 + (tbl + 1) * 128],
                                          wdstr[slw][:, c8 * 512:(c8 + 1) * 512],
                                          start=(fc == 0), stop=(fc == 63))
                    sT.inc(mm)
                    wdlrel[slw] = (sT, sT.v)
                mmdone = (sT, sT.v)
                for tbl in range(4):
                    tb = tw * 4 + tbl
                    ob = tbl % 2
                    wait(Sc, mmdone, cdtok, (sG, sG.v))
                    sop("activation", evb[ob][:], pb[tbl][:], AF.Identity,
                        bias=0.0, scale=cdc[:, cdg + tbl:cdg + tbl + 1])
                    prel[tbl] = (sS, sS.v)
                    wait(Y, x1rel2[ob])
                    x1s[ob].inc(Y.dma_start(
                        x1in[ob][:], d_x1.ap()[tb * 128:(tb + 1) * 128,
                                               dmc * 512:(dmc + 1) * 512]), 16)
                    wait(G, (sS, sS.v), (x1s[ob], x1s[ob].v), obrel[ob], (sG, sG.v))
                    gop("tensor_tensor", outb[ob][:], evb[ob][:], x1in[ob][:],
                        ALU.add)
                    x1rel2[ob] = (sG, sG.v)
                    wait(G, (sG, sG.v))
                    osem[ob].inc(G.dma_start(
                        d_out.ap()[tb * 128:(tb + 1) * 128,
                                   dmc * 512:(dmc + 1) * 512], outb[ob][:]), 16)
                    obrel[ob] = (osem[ob], osem[ob].v)
                q_step(2)
            hTgrd[lead] = (sT, sT.v)
            if tw + 1 < 4:
                q_step(len(qjobs))
                hTgev[(tw + 1) % 2] = (sS, sS.v)
        wait(Sc, (osem[0], osem[0].v), (osem[1], osem[1].v))
        wait(G, (osem[0], osem[0].v), (osem[1], osem[1].v))

    return nc, st, RECS


def _finalize(nc, st, RECS):
    with nc.Block() as block:
        block.sync(lambda e: RECS[0].replay(e))
        block.tensor(lambda e: RECS[1].replay(e))
        block.vector(lambda e: RECS[2].replay(e))
        block.scalar(lambda e: RECS[3].replay(e))
        block.gpsimd(lambda e: RECS[4].replay(e))
    st.close()
    return nc


def build(upto=99):
    nc, st, RECS = build_program(upto)
    return _finalize(nc, st, RECS)


def own_index(par):
    return np.concatenate(
        [np.arange(1024 * t + 512 * par, 1024 * t + 512 * par + 512)
         for t in range(4)])


def host_inputs(inputs):
    x = np.asarray(inputs["L0_attn_x"], np.float32)
    Wq = np.asarray(inputs["L0_attn_Wq"], np.float32)
    Wk = np.asarray(inputs["L0_attn_Wk"], np.float32)
    Wv = np.asarray(inputs["L0_attn_Wv"], np.float32)
    Wo = np.asarray(inputs["L0_attn_Wo"], np.float32)
    bq = np.asarray(inputs["L0_attn_bq"], np.float32)
    bk = np.asarray(inputs["L0_attn_bk"], np.float32)
    bv = np.asarray(inputs["L0_attn_bv"], np.float32)
    bo = np.asarray(inputs["L0_attn_bo"], np.float32)
    wg = np.asarray(inputs["L0_ffn_w_gate"], np.float32)
    wu = np.asarray(inputs["L0_ffn_w_up"], np.float32)
    wdn = np.asarray(inputs["L0_ffn_w_down"], np.float32)

    WqT = np.ascontiguousarray(Wq.T)
    WkT = np.ascontiguousarray(Wk.T)
    WvT = np.ascontiguousarray(Wv.T)
    WoT = np.ascontiguousarray(Wo.T)
    wgT = np.ascontiguousarray(wg.T)
    wuT = np.ascontiguousarray(wu.T)
    wdT = np.ascontiguousarray(wdn.T)
    bqT = np.ascontiguousarray(bq.reshape(NH, 128).T)
    bkT = np.ascontiguousarray(bk.reshape(NH, 128).T)
    bvb = np.ascontiguousarray(np.broadcast_to(bv, (128, D)))
    bob = np.ascontiguousarray(np.broadcast_to(bo, (128, D)))
    pos = np.arange(S, dtype=np.float32).reshape(1, S)
    jcol = np.repeat(np.arange(64, dtype=np.float32), 2).reshape(128, 1)
    sgn = np.tile(np.array([-1.0, 1.0], np.float32), 64).reshape(128, 1)

    in_maps = []
    for c in range(8):
        b, par = c // 2, c % 2
        idx = own_index(par)
        in_maps.append(dict(
            xq=np.ascontiguousarray(x[b][idx]),
            xkv=np.ascontiguousarray(x[b]),
            WqT=WqT, WkT=WkT, WvT=WvT, WoT=WoT,
            bqT=bqT, bkT=bkT, bvb=bvb, bob=bob,
            wgT=wgT, wuT=wuT, wdT=wdT,
            qpos=idx.astype(np.float32).reshape(1, OWN),
            pos=pos, jcol=jcol, sgn=sgn, ident=np.eye(128, dtype=np.float32),
        ))
    return in_maps


def kernel(**inputs):
    in_maps = host_inputs(inputs)
    nc = build()
    res = run_bass_kernel_spmd(nc, in_maps, core_ids=list(range(8)))
    out = np.empty((4, S, D), np.float32)
    for c in range(8):
        b, par = c // 2, c % 2
        out[b, own_index(par)] = np.asarray(res.results[c]["out"])
    return out


# version 21
# speedup vs baseline: 1.0223x; 1.0223x over previous
import numpy as np
from contextlib import ExitStack

import concourse.bass as bass
import concourse.mybir as mybir
from concourse.bass_utils import run_bass_kernel_spmd

F32 = mybir.dt.float32
BF16 = mybir.dt.bfloat16
I32 = mybir.dt.int32
AF = mybir.ActivationFunctionType
ALU = mybir.AluOpType
AX = mybir.AxisListType

D = 2048
S = 4096
NH = 16
DH = 128
DF = 8192
OWN = 2048
NIC = D // 128
NW_OWN = OWN // 512
NW_KV = S // 512
NKB = S // 128

TWO_PI = 2.0 * np.pi
CH = float(np.round(TWO_PI * 2 ** 11) * 2.0 ** -11)
CM = float(np.round((TWO_PI - CH) * 2 ** 25) * 2.0 ** -25)
CL = float(np.round((TWO_PI - CH - CM) * 2 ** 39) * 2.0 ** -39)
CR = float(np.float32(TWO_PI - CH - CM - CL))
IH = float(np.float32(1.0 / TWO_PI))
IL = float(np.float32(1.0 / TWO_PI - IH))
PH = float(np.float32(np.pi / 2))
PM2 = float(np.float32(np.pi / 2 - PH))
PL2 = float(np.float32(np.pi / 2 - PH - PM2))
PIM = float(np.nextafter(np.float32(np.pi), np.float32(0.0)))
MAGIC = float(3 * 2 ** 51)


class _RInst:
    __slots__ = ("incs",)

    def __init__(self):
        self.incs = []

    def then_inc(self, sem, val=1):
        self.incs.append((sem, val))
        return self


class Rec:
    def __init__(self, name):
        self.name = name
        self.ops = []

    def _op(self, meth, *a, **k):
        inst = _RInst()
        self.ops.append((meth, a, k, inst))
        return inst

    def __getattr__(self, meth):
        if meth.startswith("_") or meth in ("name", "ops", "replay"):
            raise AttributeError(meth)
        return lambda *a, **k: self._op(meth, *a, **k)

    def replay(self, eng):
        for meth, a, k, inst in self.ops:
            real = getattr(eng, meth)(*a, **k)
            for sem, val in inst.incs:
                real.then_inc(sem, val)


class Sem:
    def __init__(self, h, name):
        self.h = h
        self.name = name
        self.v = 0

    def inc(self, inst, n=1):
        inst.then_inc(self.h, n)
        self.v += n
        return self.v


def build_program(upto=99):
    nc = bass.Bass("TRN2", target_bir_lowering=False, debug=False)

    d_xq = nc.dram_tensor("xq", [OWN, D], F32, kind="ExternalInput")
    d_xkv = nc.dram_tensor("xkv", [S, D], F32, kind="ExternalInput")
    d_WqT = nc.dram_tensor("WqT", [D, D], F32, kind="ExternalInput")
    d_WkT = nc.dram_tensor("WkT", [D, D], F32, kind="ExternalInput")
    d_WvT = nc.dram_tensor("WvT", [D, D], F32, kind="ExternalInput")
    d_WoT = nc.dram_tensor("WoT", [D, D], F32, kind="ExternalInput")
    d_bqT = nc.dram_tensor("bqT", [128, NH], F32, kind="ExternalInput")
    d_bkT = nc.dram_tensor("bkT", [128, NH], F32, kind="ExternalInput")
    d_bvb = nc.dram_tensor("bvb", [128, D], F32, kind="ExternalInput")
    d_bob = nc.dram_tensor("bob", [128, D], F32, kind="ExternalInput")
    d_wgT = nc.dram_tensor("wgT", [D, DF], F32, kind="ExternalInput")
    d_wuT = nc.dram_tensor("wuT", [D, DF], F32, kind="ExternalInput")
    d_wdT = nc.dram_tensor("wdT", [DF, D], F32, kind="ExternalInput")
    d_qpos = nc.dram_tensor("qpos", [1, OWN], F32, kind="ExternalInput")
    d_pos = nc.dram_tensor("pos", [1, S], F32, kind="ExternalInput")
    d_jcol = nc.dram_tensor("jcol", [128, 1], F32, kind="ExternalInput")
    d_sgn = nc.dram_tensor("sgn", [128, 1], F32, kind="ExternalInput")
    d_ident = nc.dram_tensor("ident", [128, 128], F32, kind="ExternalInput")

    d_out = nc.dram_tensor("out", [OWN, D], F32, kind="ExternalOutput")
    d_xlnq = nc.dram_tensor("xlnq", [OWN, D], BF16)
    d_xlnkv = nc.dram_tensor("xlnkv", [S, D], BF16)
    d_qT = nc.dram_tensor("qTd", [D, OWN], BF16)
    d_kT = nc.dram_tensor("kTd", [D, S], BF16)
    d_v = nc.dram_tensor("vd", [S, D], BF16)
    d_ctxT = nc.dram_tensor("ctxTd", [D, OWN], BF16)
    d_x1 = nc.dram_tensor("x1d", [OWN, D], F32)
    d_xint = nc.dram_tensor("xintd", [OWN, D], BF16)
    d_hfp = nc.dram_tensor("hfpd", [OWN, DF], F32)
    d_hint = nc.dram_tensor("hintd", [OWN, DF], BF16)
    d_wdt = nc.dram_tensor("wdtd", [4, 8, 128, 4096], BF16)

    Y = Rec("sync")
    T = Rec("tensor")
    V = Rec("vector")
    Sc = Rec("scalar")
    G = Rec("gpsimd")
    RECS = [Y, T, V, Sc, G]

    st = ExitStack()
    sems = []

    def mksem(name):
        s = Sem(st.enter_context(nc.semaphore(name)), name)
        sems.append(s)
        return s

    sT = mksem("sT")
    sV = mksem("sV")
    sS = mksem("sS")
    sG = mksem("sG")
    dm = {n: mksem("d_" + n) for n in
          ["misc", "x0", "x1", "xt0", "xt1", "w0", "st0", "st1", "q0", "q1",
           "kv0", "kv1", "c0", "c1", "h0", "h1", "o0", "o1", "p0", "p1",
           "g0", "g1", "u0", "u1", "dd0", "dd1", "wr0", "wr1", "wr2", "tn0",
           "tn1", "wa0", "wa1"]}

    def wait(rec, *toks):
        for sem, val in toks:
            if val > 0:
                rec.wait_ge(sem.h, val)

    def barrier_all():
        for r in RECS:
            for s in sems:
                if s.v > 0:
                    r.wait_ge(s.h, s.v)

    sb = lambda name, shape, dt: st.enter_context(nc.sbuf_tensor(name, shape, dt))
    g_qpos = sb("g_qpos", [1, OWN], F32)
    g_bqT = sb("g_bqT", [128, NH], F32)
    g_bkT = sb("g_bkT", [128, NH], F32)
    g_bvb = sb("g_bvb", [128, D], F32)
    g_bob = sb("g_bob", [128, D], F32)
    g_ones_row = sb("g_ones_row", [1, 128], F32)
    g_ones_col = sb("g_ones_col", [128, 1], F32)
    g_sel4 = [sb(f"g_sel4_{t}", [128, 4], BF16) for t in range(4)]
    g_ones_colb = sb("g_ones_colb", [128, 1], BF16)
    g_ident = sb("g_ident", [128, 128], BF16)
    g_iota = sb("g_iota", [128, 1], F32)
    g_sxinv = sb("g_sxinv", [128, NW_OWN * 4], F32)
    g_shinv = sb("g_shinv", [128, NW_OWN * 4], F32)
    g_r128b = sb("g_r128b", [128, 1], F32)
    g_rec127 = sb("g_rec127", [128, 1], F32)
    g_hamax = sb("g_hamax", [128, NW_OWN * 4], F32)
    p01 = st.enter_context(ExitStack())
    sb01 = lambda name, shape, dt: p01.enter_context(nc.sbuf_tensor(name, shape, dt))
    g_pos = sb01("g_pos", [1, S], F32)
    g_jcol = sb01("g_jcol", [128, 1], F32)
    g_sgn = sb01("g_sgn", [128, 1], F32)
    g_c128 = sb01("g_c128", [1, 1], F32)
    g_c10k = sb01("g_c10k", [1, 1], F32)
    g_c127 = sb01("g_c127", [1, 1], F32)
    g_Ck = sb01("g_Ck", [128, S], F32)
    g_Sk = sb01("g_Sk", [128, S], F32)
    g_Cq = sb01("g_Cq", [128, OWN], F32)
    g_Sq = sb01("g_Sq", [128, OWN], F32)

    pb = [st.enter_context(nc.psum_tensor(f"pb{i}", [128, 512], F32)) for i in range(6)]
    prel = [(sT, 0)] * 8

    for ap_d, t_s in [(d_qpos, g_qpos), (d_pos, g_pos), (d_jcol, g_jcol),
                      (d_sgn, g_sgn), (d_bqT, g_bqT), (d_bkT, g_bkT),
                      (d_bvb, g_bvb), (d_bob, g_bob)]:
        dm["misc"].inc(Y.dma_start(t_s[:], ap_d.ap()), 16)

    def vop(meth, *a, **k):
        if sV.v:
            V.wait_ge(sV.h, sV.v)
        return sV.inc(getattr(V, meth)(*a, **k))

    def sop(meth, *a, **k):
        if sS.v:
            Sc.wait_ge(sS.h, sS.v)
        return sS.inc(getattr(Sc, meth)(*a, **k))

    def gop(meth, *a, **k):
        if sG.v:
            G.wait_ge(sG.h, sG.v)
        return sG.inc(getattr(G, meth)(*a, **k))

    def top(meth, *a, **k):
        return sT.inc(getattr(T, meth)(*a, **k))

    gop("memset", g_ones_row[:], 1.0)
    gop("memset", g_ones_col[:], 1.0)
    gop("memset", g_ones_colb[:], 1.0)
    for t in range(4):
        gop("memset", g_sel4[t][:], 0.0)
        gop("memset", g_sel4[t][:, t:t + 1], 1.0)
    gop("iota", g_iota[:], [[0, 1]], channel_multiplier=1,
        allow_small_or_imprecise_dtypes=True)
    gop("memset", g_c128[:], 128.0)
    gop("memset", g_c10k[:], 10000.0)
    gop("memset", g_c127[:], 127.0)
    gop("memset", g_sxinv[:], 0.0)
    gop("memset", g_shinv[:], 0.0)
    dm["w0"].inc(G.dma_start(g_ident[:], d_ident.ap()), 16)

    with ExitStack() as p0:
        sb0 = lambda name, shape, dt: p0.enter_context(nc.sbuf_tensor(name, shape, dt))
        t_ln10k = sb0("t_ln10k", [1, 1], F32)
        t_e64 = sb0("t_e64", [1, 1], F32)
        t_e64b = sb0("t_e64b", [128, 1], F32)
        t_fcol = sb0("t_fcol", [128, 1], F32)
        t_l128 = sb0("t_l128", [1, 1], F32)
        t_h2 = sb0("t_h2", [1, 1], F32)
        t_r128 = sb0("t_r128", [1, 1], F32)
        p0A = p0.enter_context(nc.psum_tensor("p0A", [128, 512], F32))
        p0B = p0.enter_context(nc.psum_tensor("p0B", [128, 512], F32))
        t_posb = sb0("t_posb", [128, 2048], F32)
        t_w1 = sb0("t_w1", [128, 2048], F32)
        t_w2 = sb0("t_w2", [128, 2048], F32)
        t_wi = sb0("t_wi", [128, 2048], I32)
        t_fhi = sb0("t_fhi", [128, 1], F32)
        t_flo = sb0("t_flo", [128, 1], F32)
        t_fi = sb0("t_fi", [128, 1], I32)
        t_fq = sb0("t_fq", [128, 1], F32)
        t_Ahi = sb0("t_Ahi", [128, 2048], F32)
        t_Alo = sb0("t_Alo", [128, 2048], F32)
        xb = [sb0(f"p1_x{i}", [128, D], F32) for i in range(4)]
        dump = sb0("p1_dump", [128, D], F32)
        NLN = OWN // 128 + S // 128
        lnsA = sb0("p1_lnsA", [128, NLN], F32)
        lnsB = sb0("p1_lnsB", [128, NLN], F32)
        lnM = sb0("p1_lnM", [128, NLN], F32)
        lnV = sb0("p1_lnV", [128, NLN], F32)
        lnb = [sb0(f"p1_ln{i}", [128, D], BF16) for i in range(2)]

        wait(Y, (dm["misc"], dm["misc"].v))
        wait(Sc, (sG, sG.v), (dm["misc"], dm["misc"].v))
        sop("activation", t_ln10k[:], g_c10k[:], AF.Ln)
        wait(V, (sS, sS.v))
        vop("tensor_scalar", t_e64[:], t_ln10k[:], -1.0 / 64.0, None, ALU.mult)
        wait(T, (sV, sV.v), (sG, sG.v))
        top("matmul", p0A[:, 0:1], g_ones_row[:], t_e64[:], start=True, stop=True)
        wait(Sc, (sT, sT.v))
        sop("activation", t_e64b[:], p0A[:, 0:1], AF.Identity)
        sop("activation", t_fcol[:], g_jcol[:], AF.Exp, bias=0.0, scale=t_e64b[:])
        sop("activation", t_l128[:], g_c128[:], AF.Ln)
        wait(V, (sS, sS.v))
        vop("tensor_scalar", t_h2[:], t_l128[:], -0.5, None, ALU.mult)
        wait(Sc, (sV, sV.v))
        sop("activation", t_r128[:], t_h2[:], AF.Exp)
        wait(T, (sS, sS.v))
        top("matmul", p0B[:, 0:1], g_ones_row[:], t_r128[:], start=True, stop=True)
        wait(Sc, (sT, sT.v))
        sop("activation", g_r128b[:], p0B[:, 0:1], AF.Identity)
        wait(V, (sG, sG.v), (sS, sS.v))
        vop("reciprocal", t_h2[:], g_c127[:])
        wait(T, (sV, sV.v))
        top("matmul", p0B[:, 0:1], g_ones_row[:], t_h2[:], start=True, stop=True)
        wait(Sc, (sT, sT.v))
        sop("activation", g_rec127[:], p0B[:, 0:1], AF.Identity)

        xin = [dm["x0"], dm["x1"]]
        sto = [dm["st0"], dm["st1"]]
        rel = [(sS, 0)] * 4
        relo = [(sS, 0), (sS, 0)]
        ln_jobs = []
        if upto >= 1:
            ln_jobs += [(d_xq, d_xlnq, tb) for tb in range(OWN // 128)]
            ln_jobs += [(d_xkv, d_xlnkv, tb) for tb in range(S // 128)]
        ln_i = [0]

        def emit_ln(n):
            for _ in range(n):
                i = ln_i[0]
                if i >= len(ln_jobs):
                    return
                ln_i[0] = i + 1
                lsrc, ldst, tb = ln_jobs[i]
                sl = i % 2
                wait(Y, rel[sl])
                dm_t = xin[sl]
                dm_t.inc(Y.dma_start(xb[sl][:], lsrc.ap()[tb * 128:(tb + 1) * 128, :]), 16)
                wait(Sc, (dm_t, dm_t.v))
                sop("activation", dump[:], xb[sl][:], AF.Identity,
                    accum_out=lnsA[:, i:i + 1])
                sop("activation", dump[:], xb[sl][:], AF.Square,
                    accum_out=lnsB[:, i:i + 1])
                rel[sl] = (sS, sS.v)

        def emit_lnB():
            if not ln_jobs:
                return
            wait(V, (sS, sS.v), (sV, sV.v))
            vop("tensor_scalar", lnM[:], lnsA[:], 1.0 / 2048.0, None, ALU.mult)
            vop("tensor_tensor", lnV[:], lnM[:], lnM[:], ALU.mult)
            vop("tensor_scalar", lnsB[:], lnsB[:], 1.0 / 2048.0, None, ALU.mult)
            vop("tensor_tensor", lnV[:], lnsB[:], lnV[:], ALU.subtract)
            vop("tensor_scalar", lnV[:], lnV[:], 1e-5, None, ALU.add)
            wait(Sc, (sV, sV.v))
            sop("activation", lnV[:], lnV[:], AF.Ln)
            sop("activation", lnV[:], lnV[:], AF.Exp, bias=0.0, scale=-0.5)
            wait(V, (sS, sS.v))
            vop("tensor_tensor", lnM[:], lnM[:], lnV[:], ALU.mult)
            vop("tensor_scalar", lnM[:], lnM[:], -1.0, None, ALU.mult)
            chain = (sV, sV.v)
            wait(Y, (dm["x0"], dm["x0"].v), (dm["x1"], dm["x1"].v))
            xinB = [dm["x0"], dm["wa0"], dm["x1"], dm["wa1"]]
            for i, (lsrc, ldst, tb) in enumerate(ln_jobs):
                sl = i % 4
                ring = Y if sl % 2 == 0 else G
                wait(ring, rel[sl])
                dm_t = xinB[sl]
                dm_t.inc(ring.dma_start(xb[sl][:], lsrc.ap()[tb * 128:(tb + 1) * 128, :]), 16)
                so = i % 2
                wait(Sc, (dm_t, dm_t.v), chain, relo[so])
                sop("activation", lnb[so][:], xb[sl][:], AF.Identity,
                    bias=lnM[:, i:i + 1], scale=lnV[:, i:i + 1])
                rel[sl] = (sS, sS.v)
                wait(Sc, (sS, sS.v))
                dmo = sto[so]
                dmo.inc(Sc.dma_start(ldst.ap()[tb * 128:(tb + 1) * 128, :], lnb[so][:]), 16)
                relo[so] = (dmo, dmo.v)

        tok_sin = [(sS, 0)]
        tok_ang = [(sV, 0)]

        wait(V, (sS, sS.v))
        vop("tensor_scalar", t_fhi[:], t_fcol[:], 4096.0, 0.5, ALU.mult, ALU.add)
        vop("tensor_copy", t_fi[:], t_fhi[:])
        vop("tensor_tensor", t_fq[:], t_fi[:], t_fhi[:], ALU.is_gt)
        vop("tensor_tensor", t_fhi[:], t_fi[:], t_fq[:], ALU.subtract)
        vop("tensor_scalar", t_fhi[:], t_fhi[:], 2.0 ** -12, None, ALU.mult)
        vop("tensor_tensor", t_flo[:], t_fcol[:], t_fhi[:], ALU.subtract)

        def reduce_and_sin(out, c0out, W, signed, cosshift):
            wait(V, tok_sin[0])
            vop("tensor_tensor", t_w2[:, :W], t_Ahi[:, :W], t_Alo[:, :W], ALU.add)
            if cosshift:
                vop("tensor_scalar", t_w2[:, :W], t_w2[:, :W], PH, None, ALU.add)
            vop("tensor_scalar", t_w1[:, :W], t_w2[:, :W], IH, None, ALU.mult)
            vop("scalar_tensor_tensor", t_w1[:, :W], t_w2[:, :W], IL,
                t_w1[:, :W], ALU.mult, ALU.add)
            vop("tensor_scalar", t_w1[:, :W], t_w1[:, :W], 0.5, None, ALU.add)
            vop("tensor_copy", t_wi[:, :W], t_w1[:, :W])
            vop("tensor_tensor", t_w2[:, :W], t_wi[:, :W], t_w1[:, :W], ALU.is_gt)
            vop("tensor_tensor", t_w1[:, :W], t_wi[:, :W], t_w2[:, :W],
                ALU.subtract)
            vop("scalar_tensor_tensor", t_w2[:, :W], t_w1[:, :W], -CH,
                t_Ahi[:, :W], ALU.mult, ALU.add)
            vop("tensor_tensor", t_w2[:, :W], t_w2[:, :W], t_Alo[:, :W], ALU.add)
            if cosshift:
                vop("tensor_scalar", t_w2[:, :W], t_w2[:, :W], PH, None, ALU.add)
                vop("tensor_scalar", t_w2[:, :W], t_w2[:, :W], PM2, None, ALU.add)
                vop("tensor_scalar", t_w2[:, :W], t_w2[:, :W], PL2, None, ALU.add)
            vop("scalar_tensor_tensor", t_w2[:, :W], t_w1[:, :W], -CM,
                t_w2[:, :W], ALU.mult, ALU.add)
            vop("scalar_tensor_tensor", t_w2[:, :W], t_w1[:, :W], -CL,
                t_w2[:, :W], ALU.mult, ALU.add)
            vop("scalar_tensor_tensor", t_w2[:, :W], t_w1[:, :W], -CR,
                t_w2[:, :W], ALU.mult, ALU.add)
            vop("tensor_scalar", t_w1[:, :W], t_w2[:, :W], PIM, None, ALU.is_gt)
            vop("scalar_tensor_tensor", t_w2[:, :W], t_w1[:, :W], -CH,
                t_w2[:, :W], ALU.mult, ALU.add)
            vop("scalar_tensor_tensor", t_w2[:, :W], t_w1[:, :W], -CM,
                t_w2[:, :W], ALU.mult, ALU.add)
            vop("scalar_tensor_tensor", t_w2[:, :W], t_w1[:, :W], -CL,
                t_w2[:, :W], ALU.mult, ALU.add)
            vop("tensor_scalar", t_w1[:, :W], t_w2[:, :W], -PIM, None, ALU.is_lt)
            vop("scalar_tensor_tensor", t_w2[:, :W], t_w1[:, :W], CH,
                t_w2[:, :W], ALU.mult, ALU.add)
            vop("scalar_tensor_tensor", t_w2[:, :W], t_w1[:, :W], CM,
                t_w2[:, :W], ALU.mult, ALU.add)
            vop("scalar_tensor_tensor", t_w2[:, :W], t_w1[:, :W], CL,
                t_w2[:, :W], ALU.mult, ALU.add)
            vop("tensor_scalar", t_w2[:, :W], t_w2[:, :W], PIM, None, ALU.min)
            vop("tensor_scalar", t_w2[:, :W], t_w2[:, :W], -PIM, None, ALU.max)
            if signed:
                vop("tensor_scalar", t_w2[:, :W], t_w2[:, :W], g_sgn[:], None,
                    ALU.mult)
            wait(Sc, (sV, sV.v))
            sop("activation", out[:, c0out:c0out + W], t_w2[:, :W], AF.Sin)
            tok_sin[0] = (sS, sS.v)

        def gen_table_chunk(row_t, c0, W, Cout, Sout, c0out, scale, first):
            for slc in range(0, W, 512):
                w = [(sS, sS.v)]
                if first and slc == 0:
                    w += [(sV, sV.v), (sG, sG.v)]
                wait(T, *w)
                top("matmul", p0A[:, 0:512], g_ones_row[:],
                    row_t[0:1, c0 + slc:c0 + slc + 512], start=True, stop=True)
                ws = [(sT, sT.v)]
                if slc == 0:
                    ws.append(tok_ang[0])
                wait(Sc, *ws)
                sop("activation", t_posb[:, slc:slc + 512], p0A[:, 0:512], AF.Identity)
            bc_tok = (sS, sS.v)
            wait(V, bc_tok, tok_sin[0])
            vop("tensor_scalar", t_Ahi[:, :W], t_posb[:, :W], t_fhi[:], None, ALU.mult)
            vop("tensor_scalar", t_Alo[:, :W], t_posb[:, :W], t_flo[:], None, ALU.mult)
            tok_ang[0] = (sV, sV.v)
            reduce_and_sin(Sout, c0out, W, True, False)
            emit_ln(8)
            reduce_and_sin(Cout, c0out, W, False, True)
            if scale:
                wait(V, tok_sin[0])
                vop("tensor_scalar", Cout[:, c0out:c0out + W], Cout[:, c0out:c0out + W],
                    g_r128b[:], None, ALU.mult)
                vop("tensor_scalar", Sout[:, c0out:c0out + W], Sout[:, c0out:c0out + W],
                    g_r128b[:], None, ALU.mult)
            emit_ln(8)

        first = True
        for c0 in range(0, S, 2048):
            gen_table_chunk(g_pos, c0, 2048, g_Ck, g_Sk, c0, False, first)
            first = False
        gen_table_chunk(g_qpos, 0, OWN, g_Cq, g_Sq, 0, True, False)
        emit_ln(len(ln_jobs))
        emit_lnB()
        barrier_all()

    if upto <= 1:
        return nc, st, RECS

    def proj_pass(Wsrc, xln_src, n_win, kind):
        with ExitStack() as pp:
            sbp = lambda name, shape, dt: pp.enter_context(
                nc.sbuf_tensor(f"{kind}_" + name, shape, dt))
            Wsb = sbp("pp_W", [128, NIC * D], BF16)
            xT = [sbp(f"pp_xT{i}", [128, NIC * 512], BF16) for i in range(2)]
            qb = [sbp(f"pp_qb{i}", [128, 512], F32) for i in range(2)]
            qs = [sbp(f"pp_qs{i}", [128, 512], F32) for i in range(2)]
            t1 = [sbp(f"pp_t1{i}", [128, 512], F32) for i in range(2)]
            t2 = [sbp(f"pp_t2{i}", [128, 512], F32) for i in range(2)]
            ro = [sbp(f"pp_ro{i}", [128, 512], BF16) for i in range(2)]
            wsem = dm["w0"]
            for part in range(4):
                rows = Wsrc.ap()[part * 512:(part + 1) * 512, :].rearrange(
                    "(c p) o -> p c o", p=128)
                dst3 = Wsb[:, part * 4 * D:(part + 1) * 4 * D].rearrange(
                    "p (c o) -> p c o", c=4)
                wsem.inc(G.dma_start(dst3, rows), 16)
            xsem = [dm["xt0"], dm["xt1"]]
            xrel = [(sT, 0), (sT, 0)]
            psrel = [(sT, 0)] * 6
            orel = [(dm["q0"], 0), (dm["q1"], 0)]
            osem = [dm["q0"], dm["q1"]]
            if kind == "k":
                Ct, St, bcol = g_Ck, g_Sk, g_bkT
            elif kind == "q":
                Ct, St, bcol = g_Cq, g_Sq, g_bqT
            else:
                Ct, St, bcol = None, None, None

            for w in range(n_win):
                sl = w % 2
                wait(Y, xrel[sl])
                for ic in range(NIC):
                    xsem[sl].inc(Y.dma_start(
                        xT[sl][:, ic * 512:(ic + 1) * 512],
                        xln_src.ap()[w * 512:(w + 1) * 512, ic * 128:(ic + 1) * 128],
                        transpose=True), 16)
                if kind in ("q", "k"):
                    for h in range(NH):
                        bank = (w * NH + h) % 6
                        wait(T, (xsem[sl], xsem[sl].v), (wsem, wsem.v), psrel[bank])
                        for ic in range(NIC):
                            mm = T.matmul(pb[bank][:],
                                          Wsb[:, ic * D + h * 128: ic * D + (h + 1) * 128],
                                          xT[sl][:, ic * 512:(ic + 1) * 512],
                                          start=(ic == 0), stop=(ic == NIC - 1))
                        sT.inc(mm)
                        mmv = sT.v
                        eb = (w * NH + h) % 2
                        wait(Sc, (sT, mmv), orel[eb])
                        sop("activation", qb[eb][:], pb[bank][:], AF.Identity,
                            bias=bcol[:, h:h + 1])
                        psrel[bank] = (sS, sS.v)
                        wait(V, (sS, sS.v))
                        vop("stream_shuffle", qs[eb][:], qb[eb][:],
                            [p ^ 1 for p in range(32)])
                        vop("tensor_tensor", t1[eb][:], qb[eb][:],
                            Ct[:, w * 512:(w + 1) * 512], ALU.mult)
                        wait(G, (sV, sV.v))
                        gop("tensor_tensor", t2[eb][:], qs[eb][:],
                            St[:, w * 512:(w + 1) * 512], ALU.mult)
                        wait(V, (sG, sG.v))
                        vop("tensor_tensor", ro[eb][:], t1[eb][:], t2[eb][:], ALU.add)
                        wait(Sc, (sV, sV.v))
                        dst = d_qT if kind == "q" else d_kT
                        osem[eb].inc(Sc.dma_start(
                            dst.ap()[h * 128:(h + 1) * 128, w * 512:(w + 1) * 512],
                            ro[eb][:]), 16)
                        orel[eb] = (osem[eb], osem[eb].v)
                    xrel[sl] = (sT, sT.v)
                else:
                    for tb in range(4):
                        for dvc in range(4):
                            bank = (tb * 4 + dvc) % 6
                            wait(T, (xsem[sl], xsem[sl].v), (wsem, wsem.v), psrel[bank])
                            for ic in range(NIC):
                                mm = T.matmul(
                                    pb[bank][:],
                                    xT[sl][:, ic * 512 + tb * 128: ic * 512 + (tb + 1) * 128],
                                    Wsb[:, ic * D + dvc * 512: ic * D + (dvc + 1) * 512],
                                    start=(ic == 0), stop=(ic == NIC - 1))
                            sT.inc(mm)
                            eb = (tb * 4 + dvc) % 2
                            wait(V, (sT, sT.v), orel[eb])
                            vop("tensor_tensor", ro[eb][:], pb[bank][:],
                                g_bvb[:, dvc * 512:(dvc + 1) * 512], ALU.add)
                            psrel[bank] = (sV, sV.v)
                            wait(Sc, (sV, sV.v))
                            osem[eb].inc(Sc.dma_start(
                                d_v.ap()[w * 512 + tb * 128: w * 512 + (tb + 1) * 128,
                                         dvc * 512:(dvc + 1) * 512], ro[eb][:]), 16)
                            orel[eb] = (osem[eb], osem[eb].v)
                    xrel[sl] = (sT, sT.v)
            barrier_all()

    proj_pass(d_WqT, d_xlnq, NW_OWN, "q")
    proj_pass(d_WkT, d_xlnkv, NW_KV, "k")
    proj_pass(d_WvT, d_xlnkv, NW_KV, "v")
    p01.close()

    if upto == 2:
        return nc, st, RECS

    wstats = {}
    for mi in range(3):
        wstats[mi] = (sb(f"p4a_swn{mi}", [128, 1], F32),
                      sb(f"p4a_tau{mi}", [128, 1], F32),
                      sb(f"p4a_invtau{mi}", [128, 1], F32))
    p4as = ExitStack()
    sba = lambda name, shape, dt: p4as.enter_context(nc.sbuf_tensor(name, shape, dt))
    wa_tile = [sba(f"wa_t{i}", [128, 2048], F32) for i in range(2)]
    wa_acc = [sba(f"wa_acc{i}", [128, 1], F32) for i in range(3)]
    wa_tmp = sba("wa_tmp", [128, 1], F32)
    wa_tot = sba("wa_tot", [1, 1], F32)
    for mi in range(3):
        gop("memset", wa_acc[mi][:], 0.0)
    wa_state = {"i": 0, "rel": [(sS, 0), (sS, 0)], "first": True}

    def wa_chunk(mi, wd, r0, c0, W):
        i = wa_state["i"]
        wa_state["i"] = i + 1
        slw = i % 2
        sem = [dm["wa0"], dm["wa1"]][slw]
        wait(G, wa_state["rel"][slw])
        sem.inc(G.dma_start(wa_tile[slw][:, :W], wd.ap()[r0:r0 + 128, c0:c0 + W]), 16)
        toks = [(sem, sem.v), (sV, sV.v)]
        if wa_state["first"]:
            toks.append((sG, sG.v))
            wa_state["first"] = False
        wait(V, *toks)
        vop("tensor_reduce", wa_tmp[:], wa_tile[slw][:, :W], AX.X, ALU.add,
            apply_absolute_value=True)
        wa_state["rel"][slw] = (sV, sV.v)
        wait(V, (sV, sV.v))
        vop("tensor_tensor", wa_acc[mi][:], wa_acc[mi][:], wa_tmp[:], ALU.add)

    wa_items = []
    for mi, (wdt, Rr, Cc) in enumerate([(d_wgT, D, DF), (d_wuT, D, DF),
                                        (d_wdT, DF, D)]):
        for ti in range(Rr // 128):
            for c in range(Cc // 2048):
                wa_items.append((lambda mi=mi, wdt=wdt, ti=ti, c=c:
                                 wa_chunk(mi, wdt, ti * 128, c * 2048, 2048)))

    def wa_feed(n):
        for _ in range(n):
            if wa_items:
                wa_items.pop(0)()

    def wa_epilogue():
        for mi, (Rr, Cc) in enumerate([(D, DF), (D, DF), (DF, D)]):
            b_swn, b_tau, b_invtau = wstats[mi]
            wait(T, prel[5], (sS, sS.v), (sV, sV.v))
            top("matmul", pb[5][0:1, 0:1], wa_acc[mi][:], g_ones_col[:],
                start=True, stop=True)
            wait(Sc, (sT, sT.v))
            sop("activation", wa_tot[:], pb[5][0:1, 0:1], AF.Identity)
            wait(V, (sS, sS.v))
            vop("tensor_scalar", wa_tot[:], wa_tot[:], 1.0 / (Rr * Cc), 1e-5,
                ALU.mult, ALU.max)
            wait(T, (sV, sV.v), (sS, sS.v))
            top("matmul", pb[5][:, 0:1], g_ones_row[:], wa_tot[:],
                start=True, stop=True)
            wait(Sc, (sT, sT.v))
            sop("activation", b_swn[:], pb[5][:, 0:1], AF.Identity,
                bias=0.0, scale=-1.0)
            wait(Sc, (sS, sS.v))
            sop("activation", b_tau[:], pb[5][:, 0:1], AF.Identity,
                bias=0.0, scale=0.5)
            prel[5] = (sS, sS.v)
            wait(V, (sS, sS.v))
            vop("reciprocal", b_invtau[:], b_tau[:])

    with ExitStack() as p2:
        sb2 = lambda name, shape, dt: p2.enter_context(nc.sbuf_tensor(name, shape, dt))
        p2A = p2.enter_context(nc.psum_tensor("p2A", [128, 512], F32))
        p2S = p2.enter_context(nc.psum_tensor("p2S", [128, 512], F32))
        masks = sb2("p2_masks", [128, 32 * 512], BF16)
        qpb = sb2("p2_qpb", [128, 512], F32)
        kcol = sb2("p2_kcol", [128, 1], F32)
        qTb = [sb2(f"p2_qT{i}", [128, OWN], BF16) for i in range(2)]
        kTb = [sb2(f"p2_kT{i}", [128, S], BF16) for i in range(2)]
        vb = [sb2(f"p2_v{i}", [128, NKB * 128], BF16) for i in range(2)]
        NEB = 8
        Eb = [sb2(f"p2_E{i}", [128, 512], BF16) for i in range(NEB)]
        dmp = sb2("p2_dump", [128, S], BF16)
        nm = [sb2(f"p2_nm{i}", [1, 1], F32) for i in range(8)]
        negB = [sb2(f"p2_negB{i}", [128, 1], F32) for i in range(2)]
        invr = sb2("p2_invr", [1, 512], F32)
        bcb = sb2("p2_bcb", [128, 512], F32)
        ctxb = [sb2(f"p2_ctx{i}", [128, 512], BF16) for i in range(2)]

        p2Arel = [(sS, 0)]
        for t in range(4):
            wait(T, p2Arel[0])
            top("matmul", p2A[:], g_ones_row[:],
                g_qpos[0:1, t * 512:(t + 1) * 512], start=True, stop=True)
            wait(Sc, (sT, sT.v), (sV, sV.v))
            sop("activation", qpb[:], p2A[:], AF.Identity)
            p2Arel[0] = (sS, sS.v)
            for j in range(8):
                kb = 8 * t + j
                wait(V, (sS, sS.v), (sV, sV.v))
                vop("tensor_scalar", kcol[:], g_iota[:], float(kb * 128), None, ALU.add)
                wait(V, (sV, sV.v))
                vop("tensor_scalar", masks[:, (t * 8 + j) * 512:(t * 8 + j + 1) * 512],
                    qpb[:], kcol[:], None, ALU.is_ge)
                wait(V, (sV, sV.v))

        hsem = [[dm["q0"], dm["kv0"], dm["p0"]], [dm["q1"], dm["kv1"], dm["p1"]]]
        hrel = [(sT, 0), (sT, 0)]
        p2rel = [(sT, 0)]

        def load_head(h):
            sl = h % 2
            wait(Y, hrel[sl])
            hsem[sl][0].inc(Y.dma_start(qTb[sl][:], d_qT.ap()[h * 128:(h + 1) * 128, :]), 16)
            hsem[sl][1].inc(Y.dma_start(kTb[sl][:], d_kT.ap()[h * 128:(h + 1) * 128, :]), 16)
            for part in range(4):
                src = d_v.ap()[part * 1024:(part + 1) * 1024,
                               h * 128:(h + 1) * 128].rearrange(
                    "(kb p) d -> p kb d", p=128)
                dst3 = vb[sl][:, part * 1024:(part + 1) * 1024].rearrange(
                    "p (kb d) -> p kb d", kb=8)
                hsem[sl][2].inc(Y.dma_start(dst3, src), 16)

        load_head(0)
        Erel = [(sS, 0)] * NEB
        pending2 = []
        eitems = []
        ctxrel = [(dm["c0"], 0), (dm["c1"], 0)]
        csem = [dm["c0"], dm["c1"]]
        nbrel = [(sV, 0)]
        negB_toks = [(sS, 0), (sS, 0)]

        def normB(h):
            sl = h % 2
            wq = (hsem[sl][0], hsem[sl][0].v)
            wk = (hsem[sl][1], hsem[sl][1].v)
            NB = [5, 0, 1]
            pending = []
            done = [False, False]

            def nred(i, which, bk, tv):
                out = nm[sl * 4 + which]
                wait(V, (sT, tv), (sV, sV.v))
                if not done[which]:
                    vop("tensor_reduce", out[:], pb[bk][0:1, :], AX.X, ALU.max)
                    done[which] = True
                else:
                    vop("tensor_reduce", nm[sl * 4 + 2][:], pb[bk][0:1, :],
                        AX.X, ALU.max)
                    vop("tensor_tensor", out[:], out[:], nm[sl * 4 + 2][:],
                        ALU.max)
                prel[bk] = (sV, sV.v)

            i = 0
            for which, buf, W in [(0, qTb[sl], OWN), (1, kTb[sl], S)]:
                wait(V, wq, wk, (sV, sV.v), (sT, sT.v))
                vop("tensor_tensor", dmp[:, :W], buf[:], buf[:], ALU.mult)
                vsq = (sV, sV.v)
                for slc in range(W // 512):
                    bk = NB[i % 3]
                    if len(pending) >= 3:
                        nred(*pending.pop(0))
                    wait(T, vsq, prel[bk])
                    top("matmul", pb[bk][0:1, :], g_ones_colb[:],
                        dmp[:, slc * 512:(slc + 1) * 512], start=True, stop=True)
                    pending.append((i, which, bk, sT.v))
                    i += 1
            while pending:
                nred(*pending.pop(0))
            wait(Sc, (sV, sV.v), (sS, sS.v))
            sop("activation", nm[sl * 4 + 0][:], nm[sl * 4 + 0][:], AF.Ln)
            wait(Sc, (sS, sS.v))
            sop("activation", nm[sl * 4 + 1][:], nm[sl * 4 + 1][:], AF.Ln)
            wait(V, (sS, sS.v))
            vop("tensor_tensor", nm[sl * 4 + 3][:], nm[sl * 4 + 0][:],
                nm[sl * 4 + 1][:], ALU.add)
            vop("tensor_scalar", nm[sl * 4 + 3][:], nm[sl * 4 + 3][:], 0.5,
                None, ALU.mult)
            wait(Sc, (sV, sV.v))
            sop("activation", nm[sl * 4 + 3][:], nm[sl * 4 + 3][:], AF.Exp)
            wait(T, (sS, sS.v), p2Arel[0])
            top("matmul", p2A[:, 0:1], g_ones_row[:], nm[sl * 4 + 3][:],
                start=True, stop=True)
            wait(Sc, (sT, sT.v), negB_toks[sl])
            sop("activation", negB[sl][:], p2A[:, 0:1], AF.Identity,
                bias=0.0, scale=-1.0)
            p2Arel[0] = (sS, sS.v)
            negB_toks[sl] = (sS, sS.v)

        normB(0)
        for h in range(NH):
            sl = h % 2
            if h + 1 < NH:
                load_head(h + 1)
                normB(h + 1)
            wq = (hsem[sl][0], hsem[sl][0].v)
            wk = (hsem[sl][1], hsem[sl][1].v)
            wv = (hsem[sl][2], hsem[sl][2].v)
            negB_tok = negB_toks[sl]

            SCB = [0, 1, 4]

            def drain(pend, first_sum):
                pkb, pt, etok = pend[0] if isinstance(pend[0], tuple) else pend
                eb = pkb % NEB
                K_t = 8 * (pt + 1)
                wait(T, etok, wv)
                if first_sum:
                    wait(T, p2rel[0])
                if pkb == 0:
                    wait(T, prel[2 + pt % 2])
                top("matmul", p2S[0:1, :], g_ones_colb[:], Eb[eb][:],
                    start=(pkb == 0), stop=(pkb == K_t - 1))
                top("matmul", pb[2 + pt % 2][:],
                    vb[sl][:, pkb * 128:(pkb + 1) * 128], Eb[eb][:],
                    start=(pkb == 0), stop=(pkb == K_t - 1))
                Erel[eb] = (sT, sT.v)

            for t in range(4):
                K_t = 8 * (t + 1)
                for kb in range(K_t):
                    bank = SCB[kb % 3]
                    wait(T, prel[bank], wq, wk)
                    top("matmul", pb[bank][:],
                        kTb[sl][:, kb * 128:(kb + 1) * 128],
                        qTb[sl][:, t * 512:(t + 1) * 512], start=True, stop=True)
                    qk_tok = sT.v
                    if kb >= 3 and eitems:
                        eitems.pop(0)()
                    while len(pending2) >= 3:
                        p_ = pending2.pop(0)
                        drain(p_, p_[1])
                    eb = kb % NEB
                    wait(Sc, (sT, qk_tok), Erel[eb], negB_tok)
                    sop("activation", Eb[eb][:], pb[bank][:], AF.Exp,
                        bias=negB[sl][:])
                    prel[bank] = (sS, sS.v)
                    etok = (sS, sS.v)
                    if kb >= 8 * t:
                        j = kb - 8 * t
                        wait(V, (sS, sS.v), (sV, sV.v))
                        vop("tensor_tensor", Eb[eb][:], Eb[eb][:],
                            masks[:, (t * 8 + j) * 512:(t * 8 + j + 1) * 512], ALU.mult)
                        etok = (sV, sV.v)
                    pending2.append(((kb, t, etok), kb == 0))
                def mk_epi(h, t, cb):
                    def e1():
                        wait(V, (sT, sT.v), (sV, sV.v))
                        vop("reciprocal", invr[:], p2S[0:1, :])
                        p2rel[0] = (sV, sV.v)
                    def e2():
                        wait(T, p2Arel[0], (sV, sV.v))
                        top("matmul", p2A[:], g_ones_row[:], invr[:],
                            start=True, stop=True)
                    def e3():
                        wait(Sc, (sT, sT.v))
                        sop("activation", bcb[:], p2A[:], AF.Identity)
                        p2Arel[0] = (sS, sS.v)
                    def e4():
                        wait(V, (sS, sS.v), ctxrel[cb])
                        vop("tensor_tensor", ctxb[cb][:], pb[2 + t % 2][:],
                            bcb[:], ALU.mult)
                        prel[2 + t % 2] = (sV, sV.v)
                        wait(Sc, (sV, sV.v))
                        csem[cb].inc(Sc.dma_start(
                            d_ctxT.ap()[h * 128:(h + 1) * 128,
                                        t * 512:(t + 1) * 512], ctxb[cb][:]), 16)
                        ctxrel[cb] = (csem[cb], csem[cb].v)
                    return [e1, e2, e3, e4]
                if t == 3:
                    while pending2:
                        p_ = pending2.pop(0)
                        drain(p_, p_[1])
                eitems.extend(mk_epi(h, t, t % 2))
                wa_feed(2)
            hrel[sl] = (sT, sT.v)
        while pending2:
            p_ = pending2.pop(0)
            drain(p_, p_[1])
        while eitems:
            eitems.pop(0)()
        barrier_all()

    if upto == 3:
        return nc, st, RECS

    with ExitStack() as p3:
        sb3 = lambda name, shape, dt: p3.enter_context(nc.sbuf_tensor(name, shape, dt))
        Wosb = sb3("p3_Wo", [128, NIC * D], BF16)
        ctin = [sb3(f"p3_ct{i}", [128, NIC * 128], BF16) for i in range(2)]
        xqb = [sb3(f"p3_xq{i}", [128, D], F32) for i in range(2)]
        x1b = [sb3(f"p3_x1{i}", [128, D], F32) for i in range(2)]
        tqb = sb3("p3_tq", [128, D], F32)
        qib = sb3("p3_qi", [128, D], I32)
        xib = [sb3(f"p3_xi{i}", [128, D], BF16) for i in range(2)]
        NTB = OWN // 128
        sw_ = [sb3(f"p3_sw{i}", [128, NTB], F32) for i in range(8)]
        dump3 = sb3("p3_dump", [128, D], F32)

        wsem = dm["w0"]
        for part in range(4):
            rows = d_WoT.ap()[part * 512:(part + 1) * 512, :].rearrange(
                "(c p) o -> p c o", p=128)
            dst3 = Wosb[:, part * 4 * D:(part + 1) * 4 * D].rearrange(
                "p (c o) -> p c o", c=4)
            wsem.inc(G.dma_start(dst3, rows), 16)

        cin = [dm["c0"], dm["c1"]]
        xin = [dm["x0"], dm["x1"]]
        cinrel = [(sT, 0), (sT, 0)]
        xinrel = [(sV, 0), (sV, 0)]
        x1rel = [[(dm["st0"], 0)], [(dm["st1"], 0)]]
        xirel = [(dm["q0"], 0), (dm["q1"], 0)]
        x1sem = [dm["st0"], dm["st1"]]
        xisem = [dm["q0"], dm["q1"]]
        for tb in range(NTB):
            sl = tb % 2
            wait(Y, cinrel[sl])
            src = d_ctxT.ap()[:, tb * 128:(tb + 1) * 128].rearrange(
                "(c p) t -> p c t", p=128)
            dst3 = ctin[sl][:].rearrange("p (c t) -> p c t", c=NIC)
            cin[sl].inc(Y.dma_start(dst3, src), 16)
            wait(Y, xinrel[sl])
            xin[sl].inc(Y.dma_start(xqb[sl][:], d_xq.ap()[tb * 128:(tb + 1) * 128, :]), 16)
            for dmc in range(4):
                bank = (tb * 4 + dmc) % 6
                wait(T, (cin[sl], cin[sl].v), (wsem, wsem.v), prel[bank])
                for cfc in range(NIC):
                    mm = T.matmul(pb[bank][:],
                                  ctin[sl][:, cfc * 128:(cfc + 1) * 128],
                                  Wosb[:, cfc * D + dmc * 512: cfc * D + (dmc + 1) * 512],
                                  start=(cfc == 0), stop=(cfc == NIC - 1))
                sT.inc(mm)
                wait(V, (sT, sT.v), (xin[sl], xin[sl].v), (sV, sV.v),
                     *(x1rel[sl] if dmc == 0 else []))
                vop("tensor_tensor", x1b[sl][:, dmc * 512:(dmc + 1) * 512],
                    pb[bank][:], g_bob[:, dmc * 512:(dmc + 1) * 512], ALU.add)
                prel[bank] = (sV, sV.v)
                wait(V, (sV, sV.v))
                vop("tensor_tensor", x1b[sl][:, dmc * 512:(dmc + 1) * 512],
                    x1b[sl][:, dmc * 512:(dmc + 1) * 512],
                    xqb[sl][:, dmc * 512:(dmc + 1) * 512], ALU.add)
            xinrel[sl] = (sV, sV.v)
            cinrel[sl] = (sT, sT.v)
            x1tok = sV.v
            wait(Sc, (sV, x1tok))
            x1sem[sl].inc(Sc.dma_start(d_x1.ap()[tb * 128:(tb + 1) * 128, :], x1b[sl][:]), 16)
            wait(Sc, (sV, x1tok))
            sop("activation", dump3[:], x1b[sl][:], AF.Identity,
                accum_out=sw_[0][:, tb:tb + 1])
            wait(Sc, (sS, sS.v))
            sop("activation", dump3[:], x1b[sl][:], AF.Square,
                accum_out=sw_[1][:, tb:tb + 1])
            wait(V, (sV, sV.v), (sS, sS.v))
            vop("tensor_reduce", sw_[2][:, tb:tb + 1], x1b[sl][:], AX.X, ALU.max)
            wait(V, (sV, sV.v))
            vop("tensor_reduce", sw_[3][:, tb:tb + 1], x1b[sl][:], AX.X, ALU.min)
            x1rel[sl] = [(x1sem[sl], x1sem[sl].v), (sS, sS.v), (sV, sV.v)]
            wa_feed(4)
        wa_feed(len(wa_items))
        wa_epilogue()
        wait(V, (sV, sV.v), (sS, sS.v))
        vop("tensor_scalar", sw_[4][:], sw_[0][:], 1.0 / 2048.0, None, ALU.mult)
        vop("tensor_tensor", sw_[5][:], sw_[4][:], sw_[4][:], ALU.mult)
        vop("tensor_scalar", sw_[6][:], sw_[1][:], 1.0 / 2048.0, None, ALU.mult)
        vop("tensor_tensor", sw_[6][:], sw_[6][:], sw_[5][:], ALU.subtract)
        vop("tensor_scalar", sw_[6][:], sw_[6][:], 1e-5, None, ALU.add)
        wait(Sc, (sV, sV.v))
        sop("activation", sw_[6][:], sw_[6][:], AF.Ln)
        wait(Sc, (sS, sS.v))
        sop("activation", sw_[6][:], sw_[6][:], AF.Exp, bias=0.0, scale=-0.5)
        wait(V, (sS, sS.v))
        vop("tensor_tensor", sw_[2][:], sw_[2][:], sw_[4][:], ALU.subtract)
        vop("tensor_tensor", sw_[3][:], sw_[4][:], sw_[3][:], ALU.subtract)
        vop("tensor_tensor", sw_[2][:], sw_[2][:], sw_[3][:], ALU.max)
        vop("tensor_tensor", sw_[2][:], sw_[2][:], sw_[6][:], ALU.mult)
        vop("tensor_scalar", sw_[2][:], sw_[2][:], 1e-5, None, ALU.max)
        vop("tensor_scalar", g_sxinv[:, 0:NTB], sw_[2][:], g_rec127[:], None,
            ALU.mult)
        vop("reciprocal", sw_[5][:], g_sxinv[:, 0:NTB])
        vop("tensor_tensor", sw_[6][:], sw_[6][:], sw_[5][:], ALU.mult)
        vop("tensor_tensor", sw_[7][:], sw_[4][:], sw_[6][:], ALU.mult)
        vop("tensor_scalar", sw_[7][:], sw_[7][:], -1.0, None, ALU.mult)
        stattok = (sV, sV.v)
        wait(Y, (x1sem[0], x1sem[0].v), (x1sem[1], x1sem[1].v),
             (sS, sS.v), (sV, sV.v))
        x1crel = [(sV, 0), (sV, 0)]
        for tb in range(NTB):
            sl = tb % 2
            wait(Y, x1crel[sl])
            xin[sl].inc(Y.dma_start(x1b[sl][:], d_x1.ap()[tb * 128:(tb + 1) * 128, :]), 16)
            wait(V, (xin[sl], xin[sl].v), (sV, sV.v), stattok)
            vop("tensor_scalar", tqb[:], x1b[sl][:], sw_[6][:, tb:tb + 1],
                sw_[7][:, tb:tb + 1], ALU.mult, ALU.add)
            vop("tensor_copy", qib[:], tqb[:])
            vop("tensor_tensor", tqb[:], tqb[:], qib[:], ALU.subtract)
            vop("tensor_scalar", tqb[:], tqb[:], MAGIC, -MAGIC, ALU.add, ALU.add)
            wait(V, xirel[sl])
            vop("tensor_tensor", xib[sl][:], qib[:], tqb[:], ALU.add)
            x1crel[sl] = (sV, sV.v)
            wait(Sc, (sV, sV.v))
            xisem[sl].inc(Sc.dma_start(d_xint.ap()[tb * 128:(tb + 1) * 128, :],
                                       xib[sl][:]), 16)
            xirel[sl] = (xisem[sl], xisem[sl].v)
        barrier_all()
    p4as.close()

    if upto == 4:
        return nc, st, RECS

    swn_g, tau_g, invtau_g = wstats[0]
    swn_u, tau_u, invtau_u = wstats[1]
    swn_d, tau_d, invtau_d = wstats[2]
    hamax = g_hamax
    with ExitStack() as p4:
        sb4 = lambda name, shape, dt: p4.enter_context(nc.sbuf_tensor(name, shape, dt))
        p4b6 = p4.enter_context(nc.psum_tensor("p4b6", [128, 512], F32))
        p4b7 = p4.enter_context(nc.psum_tensor("p4b7", [128, 512], F32))
        pbx = pb + [p4b6, p4b7]
        xiT = [sb4(f"p4_xiT{w}", [128, NIC * 512], BF16) for w in range(NW_OWN)]
        wfb = [sb4(f"p4_wf{i}", [128, 2048], F32) for i in range(2)]
        pmb = [sb4(f"p4_pm{i}", [128, 2048], F32) for i in range(2)]
        wgt = [sb4(f"p4_wgt{i}", [128, NIC * 512], BF16) for i in range(2)]
        wut = [sb4(f"p4_wut{i}", [128, NIC * 512], BF16) for i in range(2)]
        sgb = [sb4(f"p4_sg{i}", [128, 512], F32) for i in range(2)]
        htb = [sb4(f"p4_ht{i}", [128, 512], F32) for i in range(2)]
        amx = sb4("p4_amx", [128, 1], F32)
        cgc = [sb4(f"p4_cg{i}", [128, 1], F32) for i in range(2)]
        cuc = [sb4(f"p4_cu{i}", [128, 1], F32) for i in range(2)]
        gop("memset", hamax[:], 0.0)

        xts = dm["xt0"]
        for wnd in range(NW_OWN):
            for ic in range(NIC):
                xts.inc(Y.dma_start(
                    xiT[wnd][:, ic * 512:(ic + 1) * 512],
                    d_xint.ap()[wnd * 512:(wnd + 1) * 512, ic * 128:(ic + 1) * 128],
                    transpose=True), 16)

        tern_state = {"i": 0, "rel": [(sG, 0), (sG, 0)]}

        def ternarize(wsrc_ap, outbuf, ooff, W, invtau, wsem, flat=False):
            i = tern_state["i"]
            tern_state["i"] = i + 1
            wslot = i % 2
            wsem = [dm["tn0"], dm["tn1"]][wslot]
            wait(Y, tern_state["rel"][wslot])
            dst = wfb[wslot][:, :W] if flat else wfb[wslot][:, :W].rearrange(
                "p (c o) -> p c o", o=512)
            wsem.inc(Y.dma_start(dst, wsrc_ap), 16)
            wait(V, (wsem, wsem.v), (sV, sV.v))
            vop("tensor_scalar", pmb[wslot][:, :W], wfb[wslot][:, :W], invtau[:], 1.0,
                ALU.mult, ALU.is_ge)
            vop("tensor_scalar", wfb[wslot][:, :W], wfb[wslot][:, :W], invtau[:], -1.0,
                ALU.mult, ALU.is_le)
            wait(G, (sV, sV.v), (sG, sG.v))
            gop("tensor_tensor", outbuf[:, ooff:ooff + W], wfb[wslot][:, :W],
                pmb[wslot][:, :W], ALU.subtract)
            tern_state["rel"][wslot] = (sG, sG.v)

        hstrel = [(dm["h0"], 0), (dm["h1"], 0)]
        wdtb = [sb4(f"p4_wdtb{i}", [128, 2048], BF16) for i in range(2)]
        wdsem = [dm["dd0"], dm["dd1"]]
        wdst = [dm["p0"], dm["p1"]]
        wdrel = [(dm["p0"], 0), (dm["p1"], 0)]

        def wd_tern_tile(ti):
            slw = ti % 2
            wait(G, wdrel[slw])
            ternarize(d_wdT.ap()[ti * 128:(ti + 1) * 128, :], wdtb[slw], 0, 2048,
                      invtau_d, wdsem[slw], flat=True)
            wait(Sc, (sG, sG.v))
            for dmc in range(4):
                wdst[slw].inc(Sc.dma_start(
                    d_wdt.ap()[dmc, ti // 8, :, (ti % 8) * 512:(ti % 8 + 1) * 512],
                    wdtb[slw][:, dmc * 512:(dmc + 1) * 512]), 16)
            wdrel[slw] = (wdst[slw], wdst[slw].v)

        striperel = [(sT, 0), (sT, 0)]
        terntoks = {}

        def tern_items(fw):
            slf = fw % 2
            gsem = [dm["g0"], dm["g1"]][slf]
            usem = [dm["u0"], dm["u1"]][slf]
            first = [True]
            items = []
            for ti in range(fw * 4, fw * 4 + 4):
                items.append(lambda ti=ti: wd_tern_tile(ti))

            def mk(wT, obuf, grp, invt, wsem):
                def f():
                    if first[0]:
                        wait(G, striperel[slf])
                        first[0] = False
                    src3 = wT.ap()[grp * 512:(grp + 1) * 512,
                                   fw * 512:(fw + 1) * 512].rearrange(
                        "(c p) o -> p c o", p=128)
                    ternarize(src3, obuf, grp * 2048, 2048, invt, wsem)
                return f
            for grp in range(4):
                items.append(mk(d_wgT, wgt[slf], grp, invtau_g, gsem))
            for grp in range(4):
                items.append(mk(d_wuT, wut[slf], grp, invtau_u, usem))
            return items

        def emit_tern(fw):
            for it_ in tern_items(fw):
                it_()
            terntoks[fw] = (sG, sG.v)

        emit_tern(0)
        for fw in range(DF // 512):
            slf = fw % 2
            pend = tern_items(fw + 1) if fw + 1 < DF // 512 else []
            terntok = terntoks[fw]
            for wnd in range(NW_OWN):
                for tb in range(4):
                    tok = wnd * 4 + tb
                    bg = (tb % 4)
                    bu = 4 + (tb % 4)
                    eb = tb % 2
                    wait(T, (xts, xts.v), terntok, prel[bg], prel[bu])
                    for ic in range(NIC):
                        mm = T.matmul(pbx[bg][:],
                                      xiT[wnd][:, ic * 512 + tb * 128:
                                               ic * 512 + (tb + 1) * 128],
                                      wgt[slf][:, ic * 512:(ic + 1) * 512],
                                      start=(ic == 0), stop=(ic == NIC - 1))
                    sT.inc(mm)
                    gmm = sT.v
                    for ic in range(NIC):
                        mm = T.matmul(pbx[bu][:],
                                      xiT[wnd][:, ic * 512 + tb * 128:
                                               ic * 512 + (tb + 1) * 128],
                                      wut[slf][:, ic * 512:(ic + 1) * 512],
                                      start=(ic == 0), stop=(ic == NIC - 1))
                    sT.inc(mm)
                    umm = sT.v
                    wait(V, (sV, sV.v), (sS, sS.v))
                    vop("tensor_tensor", cgc[eb][:], g_sxinv[:, tok:tok + 1],
                        swn_g[:], ALU.mult)
                    vop("tensor_tensor", cuc[eb][:], g_sxinv[:, tok:tok + 1],
                        swn_u[:], ALU.mult)
                    wait(Sc, (sT, gmm), (sV, sV.v))
                    sop("activation", sgb[eb][:], pbx[bg][:], AF.Silu, bias=0.0,
                        scale=cgc[eb][:])
                    prel[bg] = (sS, sS.v)
                    wait(V, (sS, sS.v), (sT, umm), hstrel[eb])
                    vop("scalar_tensor_tensor", htb[eb][:], pbx[bu][:], cuc[eb][:],
                        sgb[eb][:], ALU.mult, ALU.mult)
                    prel[bu] = (sV, sV.v)
                    vop("tensor_reduce", amx[:], htb[eb][:], AX.X, ALU.max,
                        apply_absolute_value=True)
                    vop("tensor_tensor", hamax[:, tok:tok + 1],
                        hamax[:, tok:tok + 1], amx[:], ALU.max)
                    wait(Sc, (sV, sV.v))
                    hsm = [dm["h0"], dm["h1"]][eb]
                    hsm.inc(Sc.dma_start(
                        d_hfp.ap()[wnd * 512 + tb * 128: wnd * 512 + (tb + 1) * 128,
                                   fw * 512:(fw + 1) * 512], htb[eb][:]), 16)
                    hstrel[eb] = (hsm, hsm.v)
                    if pend:
                        pend.pop(0)()
            while pend:
                pend.pop(0)()
            if fw + 1 < DF // 512:
                terntoks[fw + 1] = (sG, sG.v)
            striperel[slf] = (sT, sT.v)
        barrier_all()

    with ExitStack() as p4e:
        sb4 = lambda name, shape, dt: p4e.enter_context(nc.sbuf_tensor(name, shape, dt))
        ptb = [p4e.enter_context(nc.psum_tensor(f"ptb{i}", [128, 1024], BF16))
               for i in range(2)]
        NQ = 1024
        hq = [sb4(f"p4_hq{i}", [128, NQ], F32) for i in range(2)]
        hqt = sb4("p4_hqt", [128, NQ], F32)
        hqi = sb4("p4_hqi", [128, NQ], I32)
        hqo = sb4("p4_hqo", [128, NQ], BF16)
        st4 = [sb4(f"p4_st{i}", [128, 1], F32) for i in range(4)]
        hTg = [sb4(f"p4_hTg{i}", [128, 64 * 512], BF16) for i in range(2)]
        wdstr = [sb4(f"p4_wds{i}", [128, 8 * 512], BF16) for i in range(3)]
        x1in = [sb4(f"p4_x1i{i}", [128, 512], F32) for i in range(2)]
        evb = [sb4(f"p4_ev{i}", [128, 512], F32) for i in range(2)]
        outb = [sb4(f"p4_ob{i}", [128, 512], F32) for i in range(2)]
        cdc = sb4("p4_cd", [128, 8], F32)

        hin = [dm["h0"], dm["h1"]]
        hinrel = [(sV, 0), (sV, 0)]
        x1s = [dm["x0"], dm["x1"]]
        x1rel2 = [(sG, 0), (sG, 0)]
        wdls = [dm["wr0"], dm["wr1"], dm["wr2"]]
        wdlrel = [(sT, 0)] * 3
        wdcnt = [0]
        osem = [dm["o0"], dm["o1"]]
        obrel = [(dm["o0"], 0), (dm["o1"], 0)]
        ptbrel = [(sS, 0), (sS, 0)]
        hTgev = [(sS, 0), (sS, 0)]
        hTgrd = [(sT, 0), (sT, 0)]
        qtr = [(sT, 0)]

        def q_load(tb, qq):
            slh = (tb * (DF // NQ) + qq) % 2
            ring = Y if slh == 0 else Sc
            wait(ring, hinrel[slh])
            hin[slh].inc(ring.dma_start(
                hq[slh][:], d_hfp.ap()[tb * 128:(tb + 1) * 128,
                                       qq * NQ:(qq + 1) * NQ]), 16)

        def q_comp(tb, qq, lead):
            tbl = tb % 4
            slh = (tb * (DF // NQ) + qq) % 2
            if qq == 0:
                wait(V, (sV, sV.v))
                vop("tensor_scalar", st4[0][:], hamax[:, tb:tb + 1], 1e-5, None,
                    ALU.max)
                vop("reciprocal", st4[1][:], st4[0][:])
                vop("tensor_scalar", st4[1][:], st4[1][:], 127.0, None, ALU.mult)
                vop("tensor_tensor", g_shinv[:, tb:tb + 1], st4[0][:], g_rec127[:],
                    ALU.mult)
            wait(V, (hin[slh], hin[slh].v), (sV, sV.v), qtr[0])
            vop("tensor_scalar", hqt[:], hq[slh][:], st4[1][:], None, ALU.mult)
            hinrel[slh] = (sV, sV.v)
            vop("tensor_copy", hqi[:], hqt[:])
            vop("tensor_tensor", hqt[:], hqt[:], hqi[:], ALU.subtract)
            vop("tensor_scalar", hqt[:], hqt[:], MAGIC, -MAGIC, ALU.add, ALU.add)
            vop("tensor_tensor", hqo[:], hqi[:], hqt[:], ALU.add)
            qtok = (sV, sV.v)
            for j2 in range(2):
                wait(T, qtok, ptbrel[j2], hTgrd[lead] if qq == 0 and j2 == 0
                     else (sT, 0))
                for j in range(4):
                    tr = T.transpose(ptb[j2][:, j * 128:(j + 1) * 128],
                                     hqo[:, (j2 * 4 + j) * 128:
                                         (j2 * 4 + j + 1) * 128],
                                     g_ident[:])
                sT.inc(tr)
                wait(Sc, (sT, sT.v), (sS, sS.v))
                for j in range(4):
                    fc = qq * 8 + j2 * 4 + j
                    ev = Sc.activation(
                        hTg[lead][:, fc * 512 + tbl * 128:
                                  fc * 512 + (tbl + 1) * 128],
                        ptb[j2][:, j * 128:(j + 1) * 128], AF.Identity)
                sS.inc(ev)
                ptbrel[j2] = (sS, sS.v)
            qtr[0] = (sT, sT.v)

        def quant_tokb(tb, lead):
            for qq in range(DF // NQ):
                q_load(tb, qq)
                q_comp(tb, qq, lead)
            hTgev[lead] = (sS, sS.v)

        for tbl in range(4):
            quant_tokb(tbl, 0)
        for tw in range(4):
            cdg = (tw % 2) * 4
            for tbl in range(4):
                tb = tw * 4 + tbl
                wait(V, (sV, sV.v), (sS, sS.v))
                vop("tensor_tensor", cdc[:, cdg + tbl:cdg + tbl + 1],
                    g_shinv[:, tb:tb + 1], swn_d[:], ALU.mult)
            cdtok = (sV, sV.v)
            lead = tw % 2
            qjobs = []
            ncomp = [0]
            if tw + 1 < 4:
                qjobs = [((tw + 1) * 4 + tbl2, qq) for tbl2 in range(4)
                         for qq in range(DF // NQ)]
                for k in range(2):
                    q_load(*qjobs[k])

            def q_step(n):
                for _ in range(n):
                    k = ncomp[0]
                    if k >= len(qjobs):
                        return
                    ncomp[0] = k + 1
                    q_comp(*qjobs[k], lead=(tw + 1) % 2)
                    if k + 2 < len(qjobs):
                        q_load(*qjobs[k + 2])
            for dmc in range(4):
                wait(T, hTgev[lead], prel[0], prel[1], prel[2], prel[3])
                for wg8 in range(8):
                    slw = wdcnt[0] % 3
                    wdcnt[0] += 1
                    ring = Sc if wg8 % 2 else Y
                    wait(ring, wdlrel[slw])
                    wdls[slw].inc(ring.dma_start(wdstr[slw][:],
                                                 d_wdt.ap()[dmc, wg8]), 16)
                    wait(T, (wdls[slw], wdls[slw].v))
                    for c8 in range(8):
                        fc = wg8 * 8 + c8
                        for tbl in range(4):
                            mm = T.matmul(pb[tbl][:],
                                          hTg[lead][:, fc * 512 + tbl * 128:
                                              fc * 512 + (tbl + 1) * 128],
                                          wdstr[slw][:, c8 * 512:(c8 + 1) * 512],
                                          start=(fc == 0), stop=(fc == 63))
                    sT.inc(mm)
                    wdlrel[slw] = (sT, sT.v)
                mmdone = (sT, sT.v)
                for tbl in range(4):
                    tb = tw * 4 + tbl
                    ob = tbl % 2
                    wait(Sc, mmdone, cdtok, (sG, sG.v))
                    sop("activation", evb[ob][:], pb[tbl][:], AF.Identity,
                        bias=0.0, scale=cdc[:, cdg + tbl:cdg + tbl + 1])
                    prel[tbl] = (sS, sS.v)
                    wait(Y, x1rel2[ob])
                    x1s[ob].inc(Y.dma_start(
                        x1in[ob][:], d_x1.ap()[tb * 128:(tb + 1) * 128,
                                               dmc * 512:(dmc + 1) * 512]), 16)
                    wait(G, (sS, sS.v), (x1s[ob], x1s[ob].v), obrel[ob], (sG, sG.v))
                    gop("tensor_tensor", outb[ob][:], evb[ob][:], x1in[ob][:],
                        ALU.add)
                    x1rel2[ob] = (sG, sG.v)
                    wait(G, (sG, sG.v))
                    osem[ob].inc(G.dma_start(
                        d_out.ap()[tb * 128:(tb + 1) * 128,
                                   dmc * 512:(dmc + 1) * 512], outb[ob][:]), 16)
                    obrel[ob] = (osem[ob], osem[ob].v)
                q_step(2)
            hTgrd[lead] = (sT, sT.v)
            if tw + 1 < 4:
                q_step(len(qjobs))
                hTgev[(tw + 1) % 2] = (sS, sS.v)
        wait(Sc, (osem[0], osem[0].v), (osem[1], osem[1].v))
        wait(G, (osem[0], osem[0].v), (osem[1], osem[1].v))

    return nc, st, RECS


def _finalize(nc, st, RECS):
    with nc.Block() as block:
        block.sync(lambda e: RECS[0].replay(e))
        block.tensor(lambda e: RECS[1].replay(e))
        block.vector(lambda e: RECS[2].replay(e))
        block.scalar(lambda e: RECS[3].replay(e))
        block.gpsimd(lambda e: RECS[4].replay(e))
    st.close()
    return nc


def build(upto=99):
    nc, st, RECS = build_program(upto)
    return _finalize(nc, st, RECS)


def own_index(par):
    return np.concatenate(
        [np.arange(1024 * t + 512 * par, 1024 * t + 512 * par + 512)
         for t in range(4)])


def host_inputs(inputs):
    x = np.asarray(inputs["L0_attn_x"], np.float32)
    Wq = np.asarray(inputs["L0_attn_Wq"], np.float32)
    Wk = np.asarray(inputs["L0_attn_Wk"], np.float32)
    Wv = np.asarray(inputs["L0_attn_Wv"], np.float32)
    Wo = np.asarray(inputs["L0_attn_Wo"], np.float32)
    bq = np.asarray(inputs["L0_attn_bq"], np.float32)
    bk = np.asarray(inputs["L0_attn_bk"], np.float32)
    bv = np.asarray(inputs["L0_attn_bv"], np.float32)
    bo = np.asarray(inputs["L0_attn_bo"], np.float32)
    wg = np.asarray(inputs["L0_ffn_w_gate"], np.float32)
    wu = np.asarray(inputs["L0_ffn_w_up"], np.float32)
    wdn = np.asarray(inputs["L0_ffn_w_down"], np.float32)

    WqT = np.ascontiguousarray(Wq.T)
    WkT = np.ascontiguousarray(Wk.T)
    WvT = np.ascontiguousarray(Wv.T)
    WoT = np.ascontiguousarray(Wo.T)
    wgT = np.ascontiguousarray(wg.T)
    wuT = np.ascontiguousarray(wu.T)
    wdT = np.ascontiguousarray(wdn.T)
    bqT = np.ascontiguousarray(bq.reshape(NH, 128).T)
    bkT = np.ascontiguousarray(bk.reshape(NH, 128).T)
    bvb = np.ascontiguousarray(np.broadcast_to(bv, (128, D)))
    bob = np.ascontiguousarray(np.broadcast_to(bo, (128, D)))
    pos = np.arange(S, dtype=np.float32).reshape(1, S)
    jcol = np.repeat(np.arange(64, dtype=np.float32), 2).reshape(128, 1)
    sgn = np.tile(np.array([-1.0, 1.0], np.float32), 64).reshape(128, 1)

    in_maps = []
    for c in range(8):
        b, par = c // 2, c % 2
        idx = own_index(par)
        in_maps.append(dict(
            xq=np.ascontiguousarray(x[b][idx]),
            xkv=np.ascontiguousarray(x[b]),
            WqT=WqT, WkT=WkT, WvT=WvT, WoT=WoT,
            bqT=bqT, bkT=bkT, bvb=bvb, bob=bob,
            wgT=wgT, wuT=wuT, wdT=wdT,
            qpos=idx.astype(np.float32).reshape(1, OWN),
            pos=pos, jcol=jcol, sgn=sgn, ident=np.eye(128, dtype=np.float32),
        ))
    return in_maps


def kernel(**inputs):
    in_maps = host_inputs(inputs)
    nc = build()
    res = run_bass_kernel_spmd(nc, in_maps, core_ids=list(range(8)))
    out = np.empty((4, S, D), np.float32)
    for c in range(8):
        b, par = c // 2, c % 2
        out[b, own_index(par)] = np.asarray(res.results[c]["out"])
    return out


# version 22
# speedup vs baseline: 1.0254x; 1.0030x over previous
import numpy as np
from contextlib import ExitStack

import concourse.bass as bass
import concourse.mybir as mybir
from concourse.bass_utils import run_bass_kernel_spmd

F32 = mybir.dt.float32
BF16 = mybir.dt.bfloat16
I32 = mybir.dt.int32
AF = mybir.ActivationFunctionType
ALU = mybir.AluOpType
AX = mybir.AxisListType

D = 2048
S = 4096
NH = 16
DH = 128
DF = 8192
OWN = 2048
NIC = D // 128
NW_OWN = OWN // 512
NW_KV = S // 512
NKB = S // 128

TWO_PI = 2.0 * np.pi
CH = float(np.round(TWO_PI * 2 ** 11) * 2.0 ** -11)
CM = float(np.round((TWO_PI - CH) * 2 ** 25) * 2.0 ** -25)
CL = float(np.round((TWO_PI - CH - CM) * 2 ** 39) * 2.0 ** -39)
CR = float(np.float32(TWO_PI - CH - CM - CL))
IH = float(np.float32(1.0 / TWO_PI))
IL = float(np.float32(1.0 / TWO_PI - IH))
PH = float(np.float32(np.pi / 2))
PM2 = float(np.float32(np.pi / 2 - PH))
PL2 = float(np.float32(np.pi / 2 - PH - PM2))
PIM = float(np.nextafter(np.float32(np.pi), np.float32(0.0)))
MAGIC = float(3 * 2 ** 51)


class _RInst:
    __slots__ = ("incs",)

    def __init__(self):
        self.incs = []

    def then_inc(self, sem, val=1):
        self.incs.append((sem, val))
        return self


class Rec:
    def __init__(self, name):
        self.name = name
        self.ops = []

    def _op(self, meth, *a, **k):
        inst = _RInst()
        self.ops.append((meth, a, k, inst))
        return inst

    def __getattr__(self, meth):
        if meth.startswith("_") or meth in ("name", "ops", "replay"):
            raise AttributeError(meth)
        return lambda *a, **k: self._op(meth, *a, **k)

    def replay(self, eng):
        for meth, a, k, inst in self.ops:
            real = getattr(eng, meth)(*a, **k)
            for sem, val in inst.incs:
                real.then_inc(sem, val)


class Sem:
    def __init__(self, h, name):
        self.h = h
        self.name = name
        self.v = 0

    def inc(self, inst, n=1):
        inst.then_inc(self.h, n)
        self.v += n
        return self.v


def build_program(upto=99):
    nc = bass.Bass("TRN2", target_bir_lowering=False, debug=False)

    d_xq = nc.dram_tensor("xq", [OWN, D], F32, kind="ExternalInput")
    d_xkv = nc.dram_tensor("xkv", [S, D], F32, kind="ExternalInput")
    d_WqT = nc.dram_tensor("WqT", [D, D], F32, kind="ExternalInput")
    d_WkT = nc.dram_tensor("WkT", [D, D], F32, kind="ExternalInput")
    d_WvT = nc.dram_tensor("WvT", [D, D], F32, kind="ExternalInput")
    d_WoT = nc.dram_tensor("WoT", [D, D], F32, kind="ExternalInput")
    d_bqT = nc.dram_tensor("bqT", [128, NH], F32, kind="ExternalInput")
    d_bkT = nc.dram_tensor("bkT", [128, NH], F32, kind="ExternalInput")
    d_bvb = nc.dram_tensor("bvb", [128, D], F32, kind="ExternalInput")
    d_bob = nc.dram_tensor("bob", [128, D], F32, kind="ExternalInput")
    d_wgT = nc.dram_tensor("wgT", [D, DF], F32, kind="ExternalInput")
    d_wuT = nc.dram_tensor("wuT", [D, DF], F32, kind="ExternalInput")
    d_wdT = nc.dram_tensor("wdT", [DF, D], F32, kind="ExternalInput")
    d_qpos = nc.dram_tensor("qpos", [1, OWN], F32, kind="ExternalInput")
    d_pos = nc.dram_tensor("pos", [1, S], F32, kind="ExternalInput")
    d_jcol = nc.dram_tensor("jcol", [128, 1], F32, kind="ExternalInput")
    d_sgn = nc.dram_tensor("sgn", [128, 1], F32, kind="ExternalInput")
    d_ident = nc.dram_tensor("ident", [128, 128], F32, kind="ExternalInput")

    d_out = nc.dram_tensor("out", [OWN, D], F32, kind="ExternalOutput")
    d_xlnq = nc.dram_tensor("xlnq", [OWN, D], BF16)
    d_xlnkv = nc.dram_tensor("xlnkv", [S, D], BF16)
    d_qT = nc.dram_tensor("qTd", [D, OWN], BF16)
    d_kT = nc.dram_tensor("kTd", [D, S], BF16)
    d_v = nc.dram_tensor("vd", [S, D], BF16)
    d_ctxT = nc.dram_tensor("ctxTd", [D, OWN], BF16)
    d_x1 = nc.dram_tensor("x1d", [OWN, D], F32)
    d_xint = nc.dram_tensor("xintd", [OWN, D], BF16)
    d_hfp = nc.dram_tensor("hfpd", [OWN, DF], F32)
    d_hint = nc.dram_tensor("hintd", [OWN, DF], BF16)
    d_wdt = nc.dram_tensor("wdtd", [4, 8, 128, 4096], BF16)

    Y = Rec("sync")
    T = Rec("tensor")
    V = Rec("vector")
    Sc = Rec("scalar")
    G = Rec("gpsimd")
    RECS = [Y, T, V, Sc, G]

    st = ExitStack()
    sems = []

    def mksem(name):
        s = Sem(st.enter_context(nc.semaphore(name)), name)
        sems.append(s)
        return s

    sT = mksem("sT")
    sV = mksem("sV")
    sS = mksem("sS")
    sG = mksem("sG")
    dm = {n: mksem("d_" + n) for n in
          ["misc", "x0", "x1", "xt0", "xt1", "w0", "st0", "st1", "q0", "q1",
           "kv0", "kv1", "c0", "c1", "h0", "h1", "o0", "o1", "p0", "p1",
           "g0", "g1", "u0", "u1", "dd0", "dd1", "wr0", "wr1", "wr2", "tn0",
           "tn1", "wa0", "wa1"]}

    def wait(rec, *toks):
        for sem, val in toks:
            if val > 0:
                rec.wait_ge(sem.h, val)

    def barrier_all():
        for r in RECS:
            for s in sems:
                if s.v > 0:
                    r.wait_ge(s.h, s.v)

    sb = lambda name, shape, dt: st.enter_context(nc.sbuf_tensor(name, shape, dt))
    g_qpos = sb("g_qpos", [1, OWN], F32)
    g_bqT = sb("g_bqT", [128, NH], F32)
    g_bkT = sb("g_bkT", [128, NH], F32)
    g_bvb = sb("g_bvb", [128, D], F32)
    g_bob = sb("g_bob", [128, D], F32)
    g_ones_row = sb("g_ones_row", [1, 128], F32)
    g_ones_col = sb("g_ones_col", [128, 1], F32)
    g_sel4 = [sb(f"g_sel4_{t}", [128, 4], BF16) for t in range(4)]
    g_ones_colb = sb("g_ones_colb", [128, 1], BF16)
    g_ident = sb("g_ident", [128, 128], BF16)
    g_iota = sb("g_iota", [128, 1], F32)
    g_sxinv = sb("g_sxinv", [128, NW_OWN * 4], F32)
    g_shinv = sb("g_shinv", [128, NW_OWN * 4], F32)
    g_r128b = sb("g_r128b", [128, 1], F32)
    g_rec127 = sb("g_rec127", [128, 1], F32)
    g_hamax = sb("g_hamax", [128, NW_OWN * 4], F32)
    p01 = st.enter_context(ExitStack())
    sb01 = lambda name, shape, dt: p01.enter_context(nc.sbuf_tensor(name, shape, dt))
    g_pos = sb01("g_pos", [1, S], F32)
    g_jcol = sb01("g_jcol", [128, 1], F32)
    g_sgn = sb01("g_sgn", [128, 1], F32)
    g_c128 = sb01("g_c128", [1, 1], F32)
    g_c10k = sb01("g_c10k", [1, 1], F32)
    g_c127 = sb01("g_c127", [1, 1], F32)
    g_Ck = sb01("g_Ck", [128, S], F32)
    g_Sk = sb01("g_Sk", [128, S], F32)
    g_Cq = sb01("g_Cq", [128, OWN], F32)
    g_Sq = sb01("g_Sq", [128, OWN], F32)

    pb = [st.enter_context(nc.psum_tensor(f"pb{i}", [128, 512], F32)) for i in range(6)]
    prel = [(sT, 0)] * 8

    for ap_d, t_s in [(d_qpos, g_qpos), (d_pos, g_pos), (d_jcol, g_jcol),
                      (d_sgn, g_sgn), (d_bqT, g_bqT), (d_bkT, g_bkT),
                      (d_bvb, g_bvb), (d_bob, g_bob)]:
        dm["misc"].inc(Y.dma_start(t_s[:], ap_d.ap()), 16)

    def vop(meth, *a, **k):
        if sV.v:
            V.wait_ge(sV.h, sV.v)
        return sV.inc(getattr(V, meth)(*a, **k))

    def sop(meth, *a, **k):
        if sS.v:
            Sc.wait_ge(sS.h, sS.v)
        return sS.inc(getattr(Sc, meth)(*a, **k))

    def gop(meth, *a, **k):
        if sG.v:
            G.wait_ge(sG.h, sG.v)
        return sG.inc(getattr(G, meth)(*a, **k))

    def top(meth, *a, **k):
        return sT.inc(getattr(T, meth)(*a, **k))

    gop("memset", g_ones_row[:], 1.0)
    gop("memset", g_ones_col[:], 1.0)
    gop("memset", g_ones_colb[:], 1.0)
    for t in range(4):
        gop("memset", g_sel4[t][:], 0.0)
        gop("memset", g_sel4[t][:, t:t + 1], 1.0)
    gop("iota", g_iota[:], [[0, 1]], channel_multiplier=1,
        allow_small_or_imprecise_dtypes=True)
    gop("memset", g_c128[:], 128.0)
    gop("memset", g_c10k[:], 10000.0)
    gop("memset", g_c127[:], 127.0)
    gop("memset", g_sxinv[:], 0.0)
    gop("memset", g_shinv[:], 0.0)
    dm["w0"].inc(G.dma_start(g_ident[:], d_ident.ap()), 16)

    with ExitStack() as p0:
        sb0 = lambda name, shape, dt: p0.enter_context(nc.sbuf_tensor(name, shape, dt))
        t_ln10k = sb0("t_ln10k", [1, 1], F32)
        t_e64 = sb0("t_e64", [1, 1], F32)
        t_e64b = sb0("t_e64b", [128, 1], F32)
        t_fcol = sb0("t_fcol", [128, 1], F32)
        t_l128 = sb0("t_l128", [1, 1], F32)
        t_h2 = sb0("t_h2", [1, 1], F32)
        t_r128 = sb0("t_r128", [1, 1], F32)
        p0A = p0.enter_context(nc.psum_tensor("p0A", [128, 512], F32))
        p0B = p0.enter_context(nc.psum_tensor("p0B", [128, 512], F32))
        t_posb = sb0("t_posb", [128, 2048], F32)
        t_w1 = sb0("t_w1", [128, 2048], F32)
        t_w2 = sb0("t_w2", [128, 2048], F32)
        t_wi = sb0("t_wi", [128, 2048], I32)
        t_fhi = sb0("t_fhi", [128, 1], F32)
        t_flo = sb0("t_flo", [128, 1], F32)
        t_fi = sb0("t_fi", [128, 1], I32)
        t_fq = sb0("t_fq", [128, 1], F32)
        t_Ahi = sb0("t_Ahi", [128, 2048], F32)
        t_Alo = sb0("t_Alo", [128, 2048], F32)
        xb = [sb0(f"p1_x{i}", [128, D], F32) for i in range(4)]
        dump = sb0("p1_dump", [128, D], F32)
        NLN = OWN // 128 + S // 128
        lnsA = sb0("p1_lnsA", [128, NLN], F32)
        lnsB = sb0("p1_lnsB", [128, NLN], F32)
        lnM = sb0("p1_lnM", [128, NLN], F32)
        lnV = sb0("p1_lnV", [128, NLN], F32)
        lnb = [sb0(f"p1_ln{i}", [128, D], BF16) for i in range(2)]

        wait(Y, (dm["misc"], dm["misc"].v))
        wait(Sc, (sG, sG.v), (dm["misc"], dm["misc"].v))
        sop("activation", t_ln10k[:], g_c10k[:], AF.Ln)
        wait(V, (sS, sS.v))
        vop("tensor_scalar", t_e64[:], t_ln10k[:], -1.0 / 64.0, None, ALU.mult)
        wait(T, (sV, sV.v), (sG, sG.v))
        top("matmul", p0A[:, 0:1], g_ones_row[:], t_e64[:], start=True, stop=True)
        wait(Sc, (sT, sT.v))
        sop("activation", t_e64b[:], p0A[:, 0:1], AF.Identity)
        sop("activation", t_fcol[:], g_jcol[:], AF.Exp, bias=0.0, scale=t_e64b[:])
        sop("activation", t_l128[:], g_c128[:], AF.Ln)
        wait(V, (sS, sS.v))
        vop("tensor_scalar", t_h2[:], t_l128[:], -0.5, None, ALU.mult)
        wait(Sc, (sV, sV.v))
        sop("activation", t_r128[:], t_h2[:], AF.Exp)
        wait(T, (sS, sS.v))
        top("matmul", p0B[:, 0:1], g_ones_row[:], t_r128[:], start=True, stop=True)
        wait(Sc, (sT, sT.v))
        sop("activation", g_r128b[:], p0B[:, 0:1], AF.Identity)
        wait(V, (sG, sG.v), (sS, sS.v))
        vop("reciprocal", t_h2[:], g_c127[:])
        wait(T, (sV, sV.v))
        top("matmul", p0B[:, 0:1], g_ones_row[:], t_h2[:], start=True, stop=True)
        wait(Sc, (sT, sT.v))
        sop("activation", g_rec127[:], p0B[:, 0:1], AF.Identity)

        xin = [dm["x0"], dm["x1"]]
        sto = [dm["st0"], dm["st1"]]
        rel = [(sS, 0)] * 4
        relo = [(sS, 0), (sS, 0)]
        ln_jobs = []
        if upto >= 1:
            ln_jobs += [(d_xq, d_xlnq, tb) for tb in range(OWN // 128)]
            ln_jobs += [(d_xkv, d_xlnkv, tb) for tb in range(S // 128)]
        ln_i = [0]

        def emit_ln(n):
            for _ in range(n):
                i = ln_i[0]
                if i >= len(ln_jobs):
                    return
                ln_i[0] = i + 1
                lsrc, ldst, tb = ln_jobs[i]
                sl = i % 2
                wait(Y, rel[sl])
                dm_t = xin[sl]
                dm_t.inc(Y.dma_start(xb[sl][:], lsrc.ap()[tb * 128:(tb + 1) * 128, :]), 16)
                wait(Sc, (dm_t, dm_t.v))
                sop("activation", dump[:], xb[sl][:], AF.Identity,
                    accum_out=lnsA[:, i:i + 1])
                sop("activation", dump[:], xb[sl][:], AF.Square,
                    accum_out=lnsB[:, i:i + 1])
                rel[sl] = (sS, sS.v)

        def emit_lnB():
            if not ln_jobs:
                return
            wait(V, (sS, sS.v), (sV, sV.v))
            vop("tensor_scalar", lnM[:], lnsA[:], 1.0 / 2048.0, None, ALU.mult)
            vop("tensor_tensor", lnV[:], lnM[:], lnM[:], ALU.mult)
            vop("tensor_scalar", lnsB[:], lnsB[:], 1.0 / 2048.0, None, ALU.mult)
            vop("tensor_tensor", lnV[:], lnsB[:], lnV[:], ALU.subtract)
            vop("tensor_scalar", lnV[:], lnV[:], 1e-5, None, ALU.add)
            wait(Sc, (sV, sV.v))
            sop("activation", lnV[:], lnV[:], AF.Ln)
            sop("activation", lnV[:], lnV[:], AF.Exp, bias=0.0, scale=-0.5)
            wait(V, (sS, sS.v))
            vop("tensor_tensor", lnM[:], lnM[:], lnV[:], ALU.mult)
            vop("tensor_scalar", lnM[:], lnM[:], -1.0, None, ALU.mult)
            chain = (sV, sV.v)
            wait(Y, (dm["x0"], dm["x0"].v), (dm["x1"], dm["x1"].v))
            xinB = [dm["x0"], dm["kv0"], dm["x1"], dm["kv1"]]
            for i, (lsrc, ldst, tb) in enumerate(ln_jobs):
                sl = i % 4
                ring = Y
                wait(ring, rel[sl])
                dm_t = xinB[sl]
                dm_t.inc(ring.dma_start(xb[sl][:], lsrc.ap()[tb * 128:(tb + 1) * 128, :]), 16)
                so = i % 2
                wait(Sc, (dm_t, dm_t.v), chain, relo[so])
                sop("activation", lnb[so][:], xb[sl][:], AF.Identity,
                    bias=lnM[:, i:i + 1], scale=lnV[:, i:i + 1])
                rel[sl] = (sS, sS.v)
                wait(Sc, (sS, sS.v))
                dmo = sto[so]
                dmo.inc(Sc.dma_start(ldst.ap()[tb * 128:(tb + 1) * 128, :], lnb[so][:]), 16)
                relo[so] = (dmo, dmo.v)

        tok_sin = [(sS, 0)]
        tok_ang = [(sV, 0)]

        wait(V, (sS, sS.v))
        vop("tensor_scalar", t_fhi[:], t_fcol[:], 4096.0, 0.5, ALU.mult, ALU.add)
        vop("tensor_copy", t_fi[:], t_fhi[:])
        vop("tensor_tensor", t_fq[:], t_fi[:], t_fhi[:], ALU.is_gt)
        vop("tensor_tensor", t_fhi[:], t_fi[:], t_fq[:], ALU.subtract)
        vop("tensor_scalar", t_fhi[:], t_fhi[:], 2.0 ** -12, None, ALU.mult)
        vop("tensor_tensor", t_flo[:], t_fcol[:], t_fhi[:], ALU.subtract)

        def reduce_and_sin(out, c0out, W, signed, cosshift):
            wait(V, tok_sin[0])
            vop("tensor_tensor", t_w2[:, :W], t_Ahi[:, :W], t_Alo[:, :W], ALU.add)
            if cosshift:
                vop("tensor_scalar", t_w2[:, :W], t_w2[:, :W], PH, None, ALU.add)
            vop("tensor_scalar", t_w1[:, :W], t_w2[:, :W], IH, None, ALU.mult)
            vop("scalar_tensor_tensor", t_w1[:, :W], t_w2[:, :W], IL,
                t_w1[:, :W], ALU.mult, ALU.add)
            vop("tensor_scalar", t_w1[:, :W], t_w1[:, :W], 0.5, None, ALU.add)
            vop("tensor_copy", t_wi[:, :W], t_w1[:, :W])
            vop("tensor_tensor", t_w2[:, :W], t_wi[:, :W], t_w1[:, :W], ALU.is_gt)
            vop("tensor_tensor", t_w1[:, :W], t_wi[:, :W], t_w2[:, :W],
                ALU.subtract)
            vop("scalar_tensor_tensor", t_w2[:, :W], t_w1[:, :W], -CH,
                t_Ahi[:, :W], ALU.mult, ALU.add)
            vop("tensor_tensor", t_w2[:, :W], t_w2[:, :W], t_Alo[:, :W], ALU.add)
            if cosshift:
                vop("tensor_scalar", t_w2[:, :W], t_w2[:, :W], PH, None, ALU.add)
                vop("tensor_scalar", t_w2[:, :W], t_w2[:, :W], PM2, None, ALU.add)
                vop("tensor_scalar", t_w2[:, :W], t_w2[:, :W], PL2, None, ALU.add)
            vop("scalar_tensor_tensor", t_w2[:, :W], t_w1[:, :W], -CM,
                t_w2[:, :W], ALU.mult, ALU.add)
            vop("scalar_tensor_tensor", t_w2[:, :W], t_w1[:, :W], -CL,
                t_w2[:, :W], ALU.mult, ALU.add)
            vop("scalar_tensor_tensor", t_w2[:, :W], t_w1[:, :W], -CR,
                t_w2[:, :W], ALU.mult, ALU.add)
            vop("tensor_scalar", t_w1[:, :W], t_w2[:, :W], PIM, None, ALU.is_gt)
            vop("scalar_tensor_tensor", t_w2[:, :W], t_w1[:, :W], -CH,
                t_w2[:, :W], ALU.mult, ALU.add)
            vop("scalar_tensor_tensor", t_w2[:, :W], t_w1[:, :W], -CM,
                t_w2[:, :W], ALU.mult, ALU.add)
            vop("scalar_tensor_tensor", t_w2[:, :W], t_w1[:, :W], -CL,
                t_w2[:, :W], ALU.mult, ALU.add)
            vop("tensor_scalar", t_w1[:, :W], t_w2[:, :W], -PIM, None, ALU.is_lt)
            vop("scalar_tensor_tensor", t_w2[:, :W], t_w1[:, :W], CH,
                t_w2[:, :W], ALU.mult, ALU.add)
            vop("scalar_tensor_tensor", t_w2[:, :W], t_w1[:, :W], CM,
                t_w2[:, :W], ALU.mult, ALU.add)
            vop("scalar_tensor_tensor", t_w2[:, :W], t_w1[:, :W], CL,
                t_w2[:, :W], ALU.mult, ALU.add)
            vop("tensor_scalar", t_w2[:, :W], t_w2[:, :W], PIM, None, ALU.min)
            vop("tensor_scalar", t_w2[:, :W], t_w2[:, :W], -PIM, None, ALU.max)
            if signed:
                vop("tensor_scalar", t_w2[:, :W], t_w2[:, :W], g_sgn[:], None,
                    ALU.mult)
            wait(Sc, (sV, sV.v))
            sop("activation", out[:, c0out:c0out + W], t_w2[:, :W], AF.Sin)
            tok_sin[0] = (sS, sS.v)

        def gen_table_chunk(row_t, c0, W, Cout, Sout, c0out, scale, first):
            for slc in range(0, W, 512):
                w = [(sS, sS.v)]
                if first and slc == 0:
                    w += [(sV, sV.v), (sG, sG.v)]
                wait(T, *w)
                top("matmul", p0A[:, 0:512], g_ones_row[:],
                    row_t[0:1, c0 + slc:c0 + slc + 512], start=True, stop=True)
                ws = [(sT, sT.v)]
                if slc == 0:
                    ws.append(tok_ang[0])
                wait(Sc, *ws)
                sop("activation", t_posb[:, slc:slc + 512], p0A[:, 0:512], AF.Identity)
            bc_tok = (sS, sS.v)
            wait(V, bc_tok, tok_sin[0])
            vop("tensor_scalar", t_Ahi[:, :W], t_posb[:, :W], t_fhi[:], None, ALU.mult)
            vop("tensor_scalar", t_Alo[:, :W], t_posb[:, :W], t_flo[:], None, ALU.mult)
            tok_ang[0] = (sV, sV.v)
            reduce_and_sin(Sout, c0out, W, True, False)
            emit_ln(8)
            reduce_and_sin(Cout, c0out, W, False, True)
            if scale:
                wait(V, tok_sin[0])
                vop("tensor_scalar", Cout[:, c0out:c0out + W], Cout[:, c0out:c0out + W],
                    g_r128b[:], None, ALU.mult)
                vop("tensor_scalar", Sout[:, c0out:c0out + W], Sout[:, c0out:c0out + W],
                    g_r128b[:], None, ALU.mult)
            emit_ln(8)

        first = True
        for c0 in range(0, S, 2048):
            gen_table_chunk(g_pos, c0, 2048, g_Ck, g_Sk, c0, False, first)
            first = False
        gen_table_chunk(g_qpos, 0, OWN, g_Cq, g_Sq, 0, True, False)
        emit_ln(len(ln_jobs))
        emit_lnB()
        barrier_all()

    if upto <= 1:
        return nc, st, RECS

    def proj_pass(Wsrc, xln_src, n_win, kind):
        with ExitStack() as pp:
            sbp = lambda name, shape, dt: pp.enter_context(
                nc.sbuf_tensor(f"{kind}_" + name, shape, dt))
            Wsb = sbp("pp_W", [128, NIC * D], BF16)
            xT = [sbp(f"pp_xT{i}", [128, NIC * 512], BF16) for i in range(2)]
            qb = [sbp(f"pp_qb{i}", [128, 512], F32) for i in range(2)]
            qs = [sbp(f"pp_qs{i}", [128, 512], F32) for i in range(2)]
            t1 = [sbp(f"pp_t1{i}", [128, 512], F32) for i in range(2)]
            t2 = [sbp(f"pp_t2{i}", [128, 512], F32) for i in range(2)]
            ro = [sbp(f"pp_ro{i}", [128, 512], BF16) for i in range(2)]
            wsem = dm["w0"]
            for part in range(4):
                rows = Wsrc.ap()[part * 512:(part + 1) * 512, :].rearrange(
                    "(c p) o -> p c o", p=128)
                dst3 = Wsb[:, part * 4 * D:(part + 1) * 4 * D].rearrange(
                    "p (c o) -> p c o", c=4)
                wsem.inc(G.dma_start(dst3, rows), 16)
            xsem = [dm["xt0"], dm["xt1"]]
            xrel = [(sT, 0), (sT, 0)]
            psrel = [(sT, 0)] * 6
            orel = [(dm["q0"], 0), (dm["q1"], 0)]
            osem = [dm["q0"], dm["q1"]]
            if kind == "k":
                Ct, St, bcol = g_Ck, g_Sk, g_bkT
            elif kind == "q":
                Ct, St, bcol = g_Cq, g_Sq, g_bqT
            else:
                Ct, St, bcol = None, None, None

            for w in range(n_win):
                sl = w % 2
                wait(Y, xrel[sl])
                for ic in range(NIC):
                    xsem[sl].inc(Y.dma_start(
                        xT[sl][:, ic * 512:(ic + 1) * 512],
                        xln_src.ap()[w * 512:(w + 1) * 512, ic * 128:(ic + 1) * 128],
                        transpose=True), 16)
                if kind in ("q", "k"):
                    for h in range(NH):
                        bank = (w * NH + h) % 6
                        wait(T, (xsem[sl], xsem[sl].v), (wsem, wsem.v), psrel[bank])
                        for ic in range(NIC):
                            mm = T.matmul(pb[bank][:],
                                          Wsb[:, ic * D + h * 128: ic * D + (h + 1) * 128],
                                          xT[sl][:, ic * 512:(ic + 1) * 512],
                                          start=(ic == 0), stop=(ic == NIC - 1))
                        sT.inc(mm)
                        mmv = sT.v
                        eb = (w * NH + h) % 2
                        wait(Sc, (sT, mmv), orel[eb])
                        sop("activation", qb[eb][:], pb[bank][:], AF.Identity,
                            bias=bcol[:, h:h + 1])
                        psrel[bank] = (sS, sS.v)
                        wait(V, (sS, sS.v))
                        vop("stream_shuffle", qs[eb][:], qb[eb][:],
                            [p ^ 1 for p in range(32)])
                        vop("tensor_tensor", t1[eb][:], qb[eb][:],
                            Ct[:, w * 512:(w + 1) * 512], ALU.mult)
                        wait(G, (sV, sV.v))
                        gop("tensor_tensor", t2[eb][:], qs[eb][:],
                            St[:, w * 512:(w + 1) * 512], ALU.mult)
                        wait(V, (sG, sG.v))
                        vop("tensor_tensor", ro[eb][:], t1[eb][:], t2[eb][:], ALU.add)
                        wait(Sc, (sV, sV.v))
                        dst = d_qT if kind == "q" else d_kT
                        osem[eb].inc(Sc.dma_start(
                            dst.ap()[h * 128:(h + 1) * 128, w * 512:(w + 1) * 512],
                            ro[eb][:]), 16)
                        orel[eb] = (osem[eb], osem[eb].v)
                    xrel[sl] = (sT, sT.v)
                else:
                    for tb in range(4):
                        for dvc in range(4):
                            bank = (tb * 4 + dvc) % 6
                            wait(T, (xsem[sl], xsem[sl].v), (wsem, wsem.v), psrel[bank])
                            for ic in range(NIC):
                                mm = T.matmul(
                                    pb[bank][:],
                                    xT[sl][:, ic * 512 + tb * 128: ic * 512 + (tb + 1) * 128],
                                    Wsb[:, ic * D + dvc * 512: ic * D + (dvc + 1) * 512],
                                    start=(ic == 0), stop=(ic == NIC - 1))
                            sT.inc(mm)
                            eb = (tb * 4 + dvc) % 2
                            wait(V, (sT, sT.v), orel[eb])
                            vop("tensor_tensor", ro[eb][:], pb[bank][:],
                                g_bvb[:, dvc * 512:(dvc + 1) * 512], ALU.add)
                            psrel[bank] = (sV, sV.v)
                            wait(Sc, (sV, sV.v))
                            osem[eb].inc(Sc.dma_start(
                                d_v.ap()[w * 512 + tb * 128: w * 512 + (tb + 1) * 128,
                                         dvc * 512:(dvc + 1) * 512], ro[eb][:]), 16)
                            orel[eb] = (osem[eb], osem[eb].v)
                    xrel[sl] = (sT, sT.v)
            barrier_all()

    proj_pass(d_WqT, d_xlnq, NW_OWN, "q")
    proj_pass(d_WkT, d_xlnkv, NW_KV, "k")
    proj_pass(d_WvT, d_xlnkv, NW_KV, "v")
    p01.close()

    if upto == 2:
        return nc, st, RECS

    wstats = {}
    for mi in range(3):
        wstats[mi] = (sb(f"p4a_swn{mi}", [128, 1], F32),
                      sb(f"p4a_tau{mi}", [128, 1], F32),
                      sb(f"p4a_invtau{mi}", [128, 1], F32))
    p4as = ExitStack()
    sba = lambda name, shape, dt: p4as.enter_context(nc.sbuf_tensor(name, shape, dt))
    wa_tile = [sba(f"wa_t{i}", [128, 2048], F32) for i in range(2)]
    wa_acc = [sba(f"wa_acc{i}", [128, 1], F32) for i in range(3)]
    wa_tmp = sba("wa_tmp", [128, 1], F32)
    wa_tot = sba("wa_tot", [1, 1], F32)
    for mi in range(3):
        gop("memset", wa_acc[mi][:], 0.0)
    wa_state = {"i": 0, "rel": [(sS, 0), (sS, 0)], "first": True}

    def wa_chunk(mi, wd, r0, c0, W):
        i = wa_state["i"]
        wa_state["i"] = i + 1
        slw = i % 2
        sem = [dm["wa0"], dm["wa1"]][slw]
        wait(G, wa_state["rel"][slw])
        sem.inc(G.dma_start(wa_tile[slw][:, :W], wd.ap()[r0:r0 + 128, c0:c0 + W]), 16)
        toks = [(sem, sem.v), (sV, sV.v)]
        if wa_state["first"]:
            toks.append((sG, sG.v))
            wa_state["first"] = False
        wait(V, *toks)
        vop("tensor_reduce", wa_tmp[:], wa_tile[slw][:, :W], AX.X, ALU.add,
            apply_absolute_value=True)
        wa_state["rel"][slw] = (sV, sV.v)
        wait(V, (sV, sV.v))
        vop("tensor_tensor", wa_acc[mi][:], wa_acc[mi][:], wa_tmp[:], ALU.add)

    wa_items = []
    for mi, (wdt, Rr, Cc) in enumerate([(d_wgT, D, DF), (d_wuT, D, DF),
                                        (d_wdT, DF, D)]):
        for ti in range(Rr // 128):
            for c in range(Cc // 2048):
                wa_items.append((lambda mi=mi, wdt=wdt, ti=ti, c=c:
                                 wa_chunk(mi, wdt, ti * 128, c * 2048, 2048)))

    def wa_feed(n):
        for _ in range(n):
            if wa_items:
                wa_items.pop(0)()

    def wa_epilogue():
        for mi, (Rr, Cc) in enumerate([(D, DF), (D, DF), (DF, D)]):
            b_swn, b_tau, b_invtau = wstats[mi]
            wait(T, prel[5], (sS, sS.v), (sV, sV.v))
            top("matmul", pb[5][0:1, 0:1], wa_acc[mi][:], g_ones_col[:],
                start=True, stop=True)
            wait(Sc, (sT, sT.v))
            sop("activation", wa_tot[:], pb[5][0:1, 0:1], AF.Identity)
            wait(V, (sS, sS.v))
            vop("tensor_scalar", wa_tot[:], wa_tot[:], 1.0 / (Rr * Cc), 1e-5,
                ALU.mult, ALU.max)
            wait(T, (sV, sV.v), (sS, sS.v))
            top("matmul", pb[5][:, 0:1], g_ones_row[:], wa_tot[:],
                start=True, stop=True)
            wait(Sc, (sT, sT.v))
            sop("activation", b_swn[:], pb[5][:, 0:1], AF.Identity,
                bias=0.0, scale=-1.0)
            wait(Sc, (sS, sS.v))
            sop("activation", b_tau[:], pb[5][:, 0:1], AF.Identity,
                bias=0.0, scale=0.5)
            prel[5] = (sS, sS.v)
            wait(V, (sS, sS.v))
            vop("reciprocal", b_invtau[:], b_tau[:])

    with ExitStack() as p2:
        sb2 = lambda name, shape, dt: p2.enter_context(nc.sbuf_tensor(name, shape, dt))
        p2A = p2.enter_context(nc.psum_tensor("p2A", [128, 512], F32))
        p2S = p2.enter_context(nc.psum_tensor("p2S", [128, 512], F32))
        masks = sb2("p2_masks", [128, 32 * 512], BF16)
        qpb = sb2("p2_qpb", [128, 512], F32)
        kcol = sb2("p2_kcol", [128, 1], F32)
        qTb = [sb2(f"p2_qT{i}", [128, OWN], BF16) for i in range(2)]
        kTb = [sb2(f"p2_kT{i}", [128, S], BF16) for i in range(2)]
        vb = [sb2(f"p2_v{i}", [128, NKB * 128], BF16) for i in range(2)]
        NEB = 8
        Eb = [sb2(f"p2_E{i}", [128, 512], BF16) for i in range(NEB)]
        dmp = sb2("p2_dump", [128, S], BF16)
        nm = [sb2(f"p2_nm{i}", [1, 1], F32) for i in range(8)]
        negB = [sb2(f"p2_negB{i}", [128, 1], F32) for i in range(2)]
        invr = sb2("p2_invr", [1, 512], F32)
        bcb = sb2("p2_bcb", [128, 512], F32)
        ctxb = [sb2(f"p2_ctx{i}", [128, 512], BF16) for i in range(2)]

        p2Arel = [(sS, 0)]
        for t in range(4):
            wait(T, p2Arel[0])
            top("matmul", p2A[:], g_ones_row[:],
                g_qpos[0:1, t * 512:(t + 1) * 512], start=True, stop=True)
            wait(Sc, (sT, sT.v), (sV, sV.v))
            sop("activation", qpb[:], p2A[:], AF.Identity)
            p2Arel[0] = (sS, sS.v)
            for j in range(8):
                kb = 8 * t + j
                wait(V, (sS, sS.v), (sV, sV.v))
                vop("tensor_scalar", kcol[:], g_iota[:], float(kb * 128), None, ALU.add)
                wait(V, (sV, sV.v))
                vop("tensor_scalar", masks[:, (t * 8 + j) * 512:(t * 8 + j + 1) * 512],
                    qpb[:], kcol[:], None, ALU.is_ge)
                wait(V, (sV, sV.v))

        hsem = [[dm["q0"], dm["kv0"], dm["p0"]], [dm["q1"], dm["kv1"], dm["p1"]]]
        hrel = [(sT, 0), (sT, 0)]
        p2rel = [(sT, 0)]

        def load_head(h):
            sl = h % 2
            wait(Y, hrel[sl])
            hsem[sl][0].inc(Y.dma_start(qTb[sl][:], d_qT.ap()[h * 128:(h + 1) * 128, :]), 16)
            hsem[sl][1].inc(Y.dma_start(kTb[sl][:], d_kT.ap()[h * 128:(h + 1) * 128, :]), 16)
            for part in range(4):
                src = d_v.ap()[part * 1024:(part + 1) * 1024,
                               h * 128:(h + 1) * 128].rearrange(
                    "(kb p) d -> p kb d", p=128)
                dst3 = vb[sl][:, part * 1024:(part + 1) * 1024].rearrange(
                    "p (kb d) -> p kb d", kb=8)
                hsem[sl][2].inc(Y.dma_start(dst3, src), 16)

        load_head(0)
        Erel = [(sS, 0)] * NEB
        pending2 = []
        eitems = []
        ctxrel = [(dm["c0"], 0), (dm["c1"], 0)]
        csem = [dm["c0"], dm["c1"]]
        nbrel = [(sV, 0)]
        negB_toks = [(sS, 0), (sS, 0)]

        def normB(h):
            sl = h % 2
            wq = (hsem[sl][0], hsem[sl][0].v)
            wk = (hsem[sl][1], hsem[sl][1].v)
            NB = [5, 0, 1]
            pending = []
            done = [False, False]

            def nred(i, which, bk, tv):
                out = nm[sl * 4 + which]
                wait(V, (sT, tv), (sV, sV.v))
                if not done[which]:
                    vop("tensor_reduce", out[:], pb[bk][0:1, :], AX.X, ALU.max)
                    done[which] = True
                else:
                    vop("tensor_reduce", nm[sl * 4 + 2][:], pb[bk][0:1, :],
                        AX.X, ALU.max)
                    vop("tensor_tensor", out[:], out[:], nm[sl * 4 + 2][:],
                        ALU.max)
                prel[bk] = (sV, sV.v)

            i = 0
            for which, buf, W in [(0, qTb[sl], OWN), (1, kTb[sl], S)]:
                wait(V, wq, wk, (sV, sV.v), (sT, sT.v))
                vop("tensor_tensor", dmp[:, :W], buf[:], buf[:], ALU.mult)
                vsq = (sV, sV.v)
                for slc in range(W // 512):
                    bk = NB[i % 3]
                    if len(pending) >= 3:
                        nred(*pending.pop(0))
                    wait(T, vsq, prel[bk])
                    top("matmul", pb[bk][0:1, :], g_ones_colb[:],
                        dmp[:, slc * 512:(slc + 1) * 512], start=True, stop=True)
                    pending.append((i, which, bk, sT.v))
                    i += 1
            while pending:
                nred(*pending.pop(0))
            wait(Sc, (sV, sV.v), (sS, sS.v))
            sop("activation", nm[sl * 4 + 0][:], nm[sl * 4 + 0][:], AF.Ln)
            wait(Sc, (sS, sS.v))
            sop("activation", nm[sl * 4 + 1][:], nm[sl * 4 + 1][:], AF.Ln)
            wait(V, (sS, sS.v))
            vop("tensor_tensor", nm[sl * 4 + 3][:], nm[sl * 4 + 0][:],
                nm[sl * 4 + 1][:], ALU.add)
            vop("tensor_scalar", nm[sl * 4 + 3][:], nm[sl * 4 + 3][:], 0.5,
                None, ALU.mult)
            wait(Sc, (sV, sV.v))
            sop("activation", nm[sl * 4 + 3][:], nm[sl * 4 + 3][:], AF.Exp)
            wait(T, (sS, sS.v), p2Arel[0])
            top("matmul", p2A[:, 0:1], g_ones_row[:], nm[sl * 4 + 3][:],
                start=True, stop=True)
            wait(Sc, (sT, sT.v), negB_toks[sl])
            sop("activation", negB[sl][:], p2A[:, 0:1], AF.Identity,
                bias=0.0, scale=-1.0)
            p2Arel[0] = (sS, sS.v)
            negB_toks[sl] = (sS, sS.v)

        normB(0)
        for h in range(NH):
            sl = h % 2
            if h + 1 < NH:
                load_head(h + 1)
                normB(h + 1)
            wq = (hsem[sl][0], hsem[sl][0].v)
            wk = (hsem[sl][1], hsem[sl][1].v)
            wv = (hsem[sl][2], hsem[sl][2].v)
            negB_tok = negB_toks[sl]

            SCB = [0, 1, 4]

            def drain(pend, first_sum):
                pkb, pt, etok = pend[0] if isinstance(pend[0], tuple) else pend
                eb = pkb % NEB
                K_t = 8 * (pt + 1)
                wait(T, etok, wv)
                if first_sum:
                    wait(T, p2rel[0])
                if pkb == 0:
                    wait(T, prel[2 + pt % 2])
                top("matmul", p2S[0:1, :], g_ones_colb[:], Eb[eb][:],
                    start=(pkb == 0), stop=(pkb == K_t - 1))
                top("matmul", pb[2 + pt % 2][:],
                    vb[sl][:, pkb * 128:(pkb + 1) * 128], Eb[eb][:],
                    start=(pkb == 0), stop=(pkb == K_t - 1))
                Erel[eb] = (sT, sT.v)

            for t in range(4):
                K_t = 8 * (t + 1)
                for kb in range(K_t):
                    bank = SCB[kb % 3]
                    wait(T, prel[bank], wq, wk)
                    top("matmul", pb[bank][:],
                        kTb[sl][:, kb * 128:(kb + 1) * 128],
                        qTb[sl][:, t * 512:(t + 1) * 512], start=True, stop=True)
                    qk_tok = sT.v
                    if kb >= 3 and eitems:
                        eitems.pop(0)()
                    while len(pending2) >= 3:
                        p_ = pending2.pop(0)
                        drain(p_, p_[1])
                    eb = kb % NEB
                    wait(Sc, (sT, qk_tok), Erel[eb], negB_tok)
                    sop("activation", Eb[eb][:], pb[bank][:], AF.Exp,
                        bias=negB[sl][:])
                    prel[bank] = (sS, sS.v)
                    etok = (sS, sS.v)
                    if kb >= 8 * t:
                        j = kb - 8 * t
                        wait(V, (sS, sS.v), (sV, sV.v))
                        vop("tensor_tensor", Eb[eb][:], Eb[eb][:],
                            masks[:, (t * 8 + j) * 512:(t * 8 + j + 1) * 512], ALU.mult)
                        etok = (sV, sV.v)
                    pending2.append(((kb, t, etok), kb == 0))
                def mk_epi(h, t, cb):
                    def e1():
                        wait(V, (sT, sT.v), (sV, sV.v))
                        vop("reciprocal", invr[:], p2S[0:1, :])
                        p2rel[0] = (sV, sV.v)
                    def e2():
                        wait(T, p2Arel[0], (sV, sV.v))
                        top("matmul", p2A[:], g_ones_row[:], invr[:],
                            start=True, stop=True)
                    def e3():
                        wait(Sc, (sT, sT.v))
                        sop("activation", bcb[:], p2A[:], AF.Identity)
                        p2Arel[0] = (sS, sS.v)
                    def e4():
                        wait(V, (sS, sS.v), ctxrel[cb])
                        vop("tensor_tensor", ctxb[cb][:], pb[2 + t % 2][:],
                            bcb[:], ALU.mult)
                        prel[2 + t % 2] = (sV, sV.v)
                        wait(Sc, (sV, sV.v))
                        csem[cb].inc(Sc.dma_start(
                            d_ctxT.ap()[h * 128:(h + 1) * 128,
                                        t * 512:(t + 1) * 512], ctxb[cb][:]), 16)
                        ctxrel[cb] = (csem[cb], csem[cb].v)
                    return [e1, e2, e3, e4]
                if t == 3:
                    while pending2:
                        p_ = pending2.pop(0)
                        drain(p_, p_[1])
                eitems.extend(mk_epi(h, t, t % 2))
                wa_feed(2)
            hrel[sl] = (sT, sT.v)
        while pending2:
            p_ = pending2.pop(0)
            drain(p_, p_[1])
        while eitems:
            eitems.pop(0)()
        barrier_all()

    if upto == 3:
        return nc, st, RECS

    with ExitStack() as p3:
        sb3 = lambda name, shape, dt: p3.enter_context(nc.sbuf_tensor(name, shape, dt))
        Wosb = sb3("p3_Wo", [128, NIC * D], BF16)
        ctin = [sb3(f"p3_ct{i}", [128, NIC * 128], BF16) for i in range(2)]
        xqb = [sb3(f"p3_xq{i}", [128, D], F32) for i in range(2)]
        x1b = [sb3(f"p3_x1{i}", [128, D], F32) for i in range(2)]
        tqb = sb3("p3_tq", [128, D], F32)
        qib = sb3("p3_qi", [128, D], I32)
        xib = [sb3(f"p3_xi{i}", [128, D], BF16) for i in range(2)]
        NTB = OWN // 128
        sw_ = [sb3(f"p3_sw{i}", [128, NTB], F32) for i in range(8)]
        dump3 = sb3("p3_dump", [128, D], F32)

        wsem = dm["w0"]
        for part in range(4):
            rows = d_WoT.ap()[part * 512:(part + 1) * 512, :].rearrange(
                "(c p) o -> p c o", p=128)
            dst3 = Wosb[:, part * 4 * D:(part + 1) * 4 * D].rearrange(
                "p (c o) -> p c o", c=4)
            wsem.inc(G.dma_start(dst3, rows), 16)

        cin = [dm["c0"], dm["c1"]]
        xin = [dm["x0"], dm["x1"]]
        cinrel = [(sT, 0), (sT, 0)]
        xinrel = [(sV, 0), (sV, 0)]
        x1rel = [[(dm["st0"], 0)], [(dm["st1"], 0)]]
        xirel = [(dm["q0"], 0), (dm["q1"], 0)]
        x1sem = [dm["st0"], dm["st1"]]
        xisem = [dm["q0"], dm["q1"]]
        for tb in range(NTB):
            sl = tb % 2
            wait(Y, cinrel[sl])
            src = d_ctxT.ap()[:, tb * 128:(tb + 1) * 128].rearrange(
                "(c p) t -> p c t", p=128)
            dst3 = ctin[sl][:].rearrange("p (c t) -> p c t", c=NIC)
            cin[sl].inc(Y.dma_start(dst3, src), 16)
            wait(Y, xinrel[sl])
            xin[sl].inc(Y.dma_start(xqb[sl][:], d_xq.ap()[tb * 128:(tb + 1) * 128, :]), 16)
            for dmc in range(4):
                bank = (tb * 4 + dmc) % 6
                wait(T, (cin[sl], cin[sl].v), (wsem, wsem.v), prel[bank])
                for cfc in range(NIC):
                    mm = T.matmul(pb[bank][:],
                                  ctin[sl][:, cfc * 128:(cfc + 1) * 128],
                                  Wosb[:, cfc * D + dmc * 512: cfc * D + (dmc + 1) * 512],
                                  start=(cfc == 0), stop=(cfc == NIC - 1))
                sT.inc(mm)
                wait(V, (sT, sT.v), (xin[sl], xin[sl].v), (sV, sV.v),
                     *(x1rel[sl] if dmc == 0 else []))
                vop("tensor_tensor", x1b[sl][:, dmc * 512:(dmc + 1) * 512],
                    pb[bank][:], g_bob[:, dmc * 512:(dmc + 1) * 512], ALU.add)
                prel[bank] = (sV, sV.v)
                wait(V, (sV, sV.v))
                vop("tensor_tensor", x1b[sl][:, dmc * 512:(dmc + 1) * 512],
                    x1b[sl][:, dmc * 512:(dmc + 1) * 512],
                    xqb[sl][:, dmc * 512:(dmc + 1) * 512], ALU.add)
            xinrel[sl] = (sV, sV.v)
            cinrel[sl] = (sT, sT.v)
            x1tok = sV.v
            wait(Sc, (sV, x1tok))
            x1sem[sl].inc(Sc.dma_start(d_x1.ap()[tb * 128:(tb + 1) * 128, :], x1b[sl][:]), 16)
            wait(Sc, (sV, x1tok))
            sop("activation", dump3[:], x1b[sl][:], AF.Identity,
                accum_out=sw_[0][:, tb:tb + 1])
            wait(Sc, (sS, sS.v))
            sop("activation", dump3[:], x1b[sl][:], AF.Square,
                accum_out=sw_[1][:, tb:tb + 1])
            wait(V, (sV, sV.v), (sS, sS.v))
            vop("tensor_reduce", sw_[2][:, tb:tb + 1], x1b[sl][:], AX.X, ALU.max)
            wait(V, (sV, sV.v))
            vop("tensor_reduce", sw_[3][:, tb:tb + 1], x1b[sl][:], AX.X, ALU.min)
            x1rel[sl] = [(x1sem[sl], x1sem[sl].v), (sS, sS.v), (sV, sV.v)]
            wa_feed(4)
        wa_feed(len(wa_items))
        wa_epilogue()
        wait(V, (sV, sV.v), (sS, sS.v))
        vop("tensor_scalar", sw_[4][:], sw_[0][:], 1.0 / 2048.0, None, ALU.mult)
        vop("tensor_tensor", sw_[5][:], sw_[4][:], sw_[4][:], ALU.mult)
        vop("tensor_scalar", sw_[6][:], sw_[1][:], 1.0 / 2048.0, None, ALU.mult)
        vop("tensor_tensor", sw_[6][:], sw_[6][:], sw_[5][:], ALU.subtract)
        vop("tensor_scalar", sw_[6][:], sw_[6][:], 1e-5, None, ALU.add)
        wait(Sc, (sV, sV.v))
        sop("activation", sw_[6][:], sw_[6][:], AF.Ln)
        wait(Sc, (sS, sS.v))
        sop("activation", sw_[6][:], sw_[6][:], AF.Exp, bias=0.0, scale=-0.5)
        wait(V, (sS, sS.v))
        vop("tensor_tensor", sw_[2][:], sw_[2][:], sw_[4][:], ALU.subtract)
        vop("tensor_tensor", sw_[3][:], sw_[4][:], sw_[3][:], ALU.subtract)
        vop("tensor_tensor", sw_[2][:], sw_[2][:], sw_[3][:], ALU.max)
        vop("tensor_tensor", sw_[2][:], sw_[2][:], sw_[6][:], ALU.mult)
        vop("tensor_scalar", sw_[2][:], sw_[2][:], 1e-5, None, ALU.max)
        vop("tensor_scalar", g_sxinv[:, 0:NTB], sw_[2][:], g_rec127[:], None,
            ALU.mult)
        vop("reciprocal", sw_[5][:], g_sxinv[:, 0:NTB])
        vop("tensor_tensor", sw_[6][:], sw_[6][:], sw_[5][:], ALU.mult)
        vop("tensor_tensor", sw_[7][:], sw_[4][:], sw_[6][:], ALU.mult)
        vop("tensor_scalar", sw_[7][:], sw_[7][:], -1.0, None, ALU.mult)
        stattok = (sV, sV.v)
        wait(Y, (x1sem[0], x1sem[0].v), (x1sem[1], x1sem[1].v),
             (sS, sS.v), (sV, sV.v))
        x1crel = [(sV, 0), (sV, 0)]
        for tb in range(NTB):
            sl = tb % 2
            wait(Y, x1crel[sl])
            xin[sl].inc(Y.dma_start(x1b[sl][:], d_x1.ap()[tb * 128:(tb + 1) * 128, :]), 16)
            wait(V, (xin[sl], xin[sl].v), (sV, sV.v), stattok)
            vop("tensor_scalar", tqb[:], x1b[sl][:], sw_[6][:, tb:tb + 1],
                sw_[7][:, tb:tb + 1], ALU.mult, ALU.add)
            vop("tensor_copy", qib[:], tqb[:])
            vop("tensor_tensor", tqb[:], tqb[:], qib[:], ALU.subtract)
            vop("tensor_scalar", tqb[:], tqb[:], MAGIC, -MAGIC, ALU.add, ALU.add)
            wait(V, xirel[sl])
            vop("tensor_tensor", xib[sl][:], qib[:], tqb[:], ALU.add)
            x1crel[sl] = (sV, sV.v)
            wait(Sc, (sV, sV.v))
            xisem[sl].inc(Sc.dma_start(d_xint.ap()[tb * 128:(tb + 1) * 128, :],
                                       xib[sl][:]), 16)
            xirel[sl] = (xisem[sl], xisem[sl].v)
        barrier_all()
    p4as.close()

    if upto == 4:
        return nc, st, RECS

    swn_g, tau_g, invtau_g = wstats[0]
    swn_u, tau_u, invtau_u = wstats[1]
    swn_d, tau_d, invtau_d = wstats[2]
    hamax = g_hamax
    with ExitStack() as p4:
        sb4 = lambda name, shape, dt: p4.enter_context(nc.sbuf_tensor(name, shape, dt))
        p4b6 = p4.enter_context(nc.psum_tensor("p4b6", [128, 512], F32))
        p4b7 = p4.enter_context(nc.psum_tensor("p4b7", [128, 512], F32))
        pbx = pb + [p4b6, p4b7]
        xiT = [sb4(f"p4_xiT{w}", [128, NIC * 512], BF16) for w in range(NW_OWN)]
        wfb = [sb4(f"p4_wf{i}", [128, 2048], F32) for i in range(2)]
        pmb = [sb4(f"p4_pm{i}", [128, 2048], F32) for i in range(2)]
        wgt = [sb4(f"p4_wgt{i}", [128, NIC * 512], BF16) for i in range(2)]
        wut = [sb4(f"p4_wut{i}", [128, NIC * 512], BF16) for i in range(2)]
        sgb = [sb4(f"p4_sg{i}", [128, 512], F32) for i in range(2)]
        htb = [sb4(f"p4_ht{i}", [128, 512], F32) for i in range(2)]
        amx = sb4("p4_amx", [128, 1], F32)
        cgc = [sb4(f"p4_cg{i}", [128, 1], F32) for i in range(2)]
        cuc = [sb4(f"p4_cu{i}", [128, 1], F32) for i in range(2)]
        gop("memset", hamax[:], 0.0)

        xts = dm["xt0"]
        for wnd in range(NW_OWN):
            for ic in range(NIC):
                xts.inc(Y.dma_start(
                    xiT[wnd][:, ic * 512:(ic + 1) * 512],
                    d_xint.ap()[wnd * 512:(wnd + 1) * 512, ic * 128:(ic + 1) * 128],
                    transpose=True), 16)

        tern_state = {"i": 0, "rel": [(sG, 0), (sG, 0)]}

        def ternarize(wsrc_ap, outbuf, ooff, W, invtau, wsem, flat=False):
            i = tern_state["i"]
            tern_state["i"] = i + 1
            wslot = i % 2
            wsem = [dm["tn0"], dm["tn1"]][wslot]
            wait(Y, tern_state["rel"][wslot])
            dst = wfb[wslot][:, :W] if flat else wfb[wslot][:, :W].rearrange(
                "p (c o) -> p c o", o=512)
            wsem.inc(Y.dma_start(dst, wsrc_ap), 16)
            wait(V, (wsem, wsem.v), (sV, sV.v))
            vop("tensor_scalar", pmb[wslot][:, :W], wfb[wslot][:, :W], invtau[:], 1.0,
                ALU.mult, ALU.is_ge)
            vop("tensor_scalar", wfb[wslot][:, :W], wfb[wslot][:, :W], invtau[:], -1.0,
                ALU.mult, ALU.is_le)
            wait(G, (sV, sV.v), (sG, sG.v))
            gop("tensor_tensor", outbuf[:, ooff:ooff + W], wfb[wslot][:, :W],
                pmb[wslot][:, :W], ALU.subtract)
            tern_state["rel"][wslot] = (sG, sG.v)

        hstrel = [(dm["h0"], 0), (dm["h1"], 0)]
        wdtb = [sb4(f"p4_wdtb{i}", [128, 2048], BF16) for i in range(2)]
        wdsem = [dm["dd0"], dm["dd1"]]
        wdst = [dm["p0"], dm["p1"]]
        wdrel = [(dm["p0"], 0), (dm["p1"], 0)]

        def wd_tern_tile(ti):
            slw = ti % 2
            wait(G, wdrel[slw])
            ternarize(d_wdT.ap()[ti * 128:(ti + 1) * 128, :], wdtb[slw], 0, 2048,
                      invtau_d, wdsem[slw], flat=True)
            wait(Sc, (sG, sG.v))
            for dmc in range(4):
                wdst[slw].inc(Sc.dma_start(
                    d_wdt.ap()[dmc, ti // 8, :, (ti % 8) * 512:(ti % 8 + 1) * 512],
                    wdtb[slw][:, dmc * 512:(dmc + 1) * 512]), 16)
            wdrel[slw] = (wdst[slw], wdst[slw].v)

        striperel = [(sT, 0), (sT, 0)]
        terntoks = {}

        def tern_items(fw):
            slf = fw % 2
            gsem = [dm["g0"], dm["g1"]][slf]
            usem = [dm["u0"], dm["u1"]][slf]
            first = [True]
            items = []
            for ti in range(fw * 4, fw * 4 + 4):
                items.append(lambda ti=ti: wd_tern_tile(ti))

            def mk(wT, obuf, grp, invt, wsem):
                def f():
                    if first[0]:
                        wait(G, striperel[slf])
                        first[0] = False
                    src3 = wT.ap()[grp * 512:(grp + 1) * 512,
                                   fw * 512:(fw + 1) * 512].rearrange(
                        "(c p) o -> p c o", p=128)
                    ternarize(src3, obuf, grp * 2048, 2048, invt, wsem)
                return f
            for grp in range(4):
                items.append(mk(d_wgT, wgt[slf], grp, invtau_g, gsem))
            for grp in range(4):
                items.append(mk(d_wuT, wut[slf], grp, invtau_u, usem))
            return items

        def emit_tern(fw):
            for it_ in tern_items(fw):
                it_()
            terntoks[fw] = (sG, sG.v)

        emit_tern(0)
        for fw in range(DF // 512):
            slf = fw % 2
            pend = tern_items(fw + 1) if fw + 1 < DF // 512 else []
            terntok = terntoks[fw]
            for wnd in range(NW_OWN):
                for tb in range(4):
                    tok = wnd * 4 + tb
                    bg = (tb % 4)
                    bu = 4 + (tb % 4)
                    eb = tb % 2
                    wait(T, (xts, xts.v), terntok, prel[bg], prel[bu])
                    for ic in range(NIC):
                        mm = T.matmul(pbx[bg][:],
                                      xiT[wnd][:, ic * 512 + tb * 128:
                                               ic * 512 + (tb + 1) * 128],
                                      wgt[slf][:, ic * 512:(ic + 1) * 512],
                                      start=(ic == 0), stop=(ic == NIC - 1))
                    sT.inc(mm)
                    gmm = sT.v
                    for ic in range(NIC):
                        mm = T.matmul(pbx[bu][:],
                                      xiT[wnd][:, ic * 512 + tb * 128:
                                               ic * 512 + (tb + 1) * 128],
                                      wut[slf][:, ic * 512:(ic + 1) * 512],
                                      start=(ic == 0), stop=(ic == NIC - 1))
                    sT.inc(mm)
                    umm = sT.v
                    wait(V, (sV, sV.v), (sS, sS.v))
                    vop("tensor_tensor", cgc[eb][:], g_sxinv[:, tok:tok + 1],
                        swn_g[:], ALU.mult)
                    vop("tensor_tensor", cuc[eb][:], g_sxinv[:, tok:tok + 1],
                        swn_u[:], ALU.mult)
                    wait(Sc, (sT, gmm), (sV, sV.v))
                    sop("activation", sgb[eb][:], pbx[bg][:], AF.Silu, bias=0.0,
                        scale=cgc[eb][:])
                    prel[bg] = (sS, sS.v)
                    wait(V, (sS, sS.v), (sT, umm), hstrel[eb])
                    vop("scalar_tensor_tensor", htb[eb][:], pbx[bu][:], cuc[eb][:],
                        sgb[eb][:], ALU.mult, ALU.mult)
                    prel[bu] = (sV, sV.v)
                    vop("tensor_reduce", amx[:], htb[eb][:], AX.X, ALU.max,
                        apply_absolute_value=True)
                    vop("tensor_tensor", hamax[:, tok:tok + 1],
                        hamax[:, tok:tok + 1], amx[:], ALU.max)
                    wait(Sc, (sV, sV.v))
                    hsm = [dm["h0"], dm["h1"]][eb]
                    hsm.inc(Sc.dma_start(
                        d_hfp.ap()[wnd * 512 + tb * 128: wnd * 512 + (tb + 1) * 128,
                                   fw * 512:(fw + 1) * 512], htb[eb][:]), 16)
                    hstrel[eb] = (hsm, hsm.v)
                    if pend:
                        pend.pop(0)()
            while pend:
                pend.pop(0)()
            if fw + 1 < DF // 512:
                terntoks[fw + 1] = (sG, sG.v)
            striperel[slf] = (sT, sT.v)
        barrier_all()

    with ExitStack() as p4e:
        sb4 = lambda name, shape, dt: p4e.enter_context(nc.sbuf_tensor(name, shape, dt))
        ptb = [p4e.enter_context(nc.psum_tensor(f"ptb{i}", [128, 1024], BF16))
               for i in range(2)]
        NQ = 1024
        hq = [sb4(f"p4_hq{i}", [128, NQ], F32) for i in range(2)]
        hqt = sb4("p4_hqt", [128, NQ], F32)
        hqi = sb4("p4_hqi", [128, NQ], I32)
        hqo = sb4("p4_hqo", [128, NQ], BF16)
        st4 = [sb4(f"p4_st{i}", [128, 1], F32) for i in range(4)]
        hTg = [sb4(f"p4_hTg{i}", [128, 64 * 512], BF16) for i in range(2)]
        wdstr = [sb4(f"p4_wds{i}", [128, 8 * 512], BF16) for i in range(3)]
        x1in = [sb4(f"p4_x1i{i}", [128, 512], F32) for i in range(2)]
        evb = [sb4(f"p4_ev{i}", [128, 512], F32) for i in range(2)]
        outb = [sb4(f"p4_ob{i}", [128, 512], F32) for i in range(2)]
        cdc = sb4("p4_cd", [128, 8], F32)

        hin = [dm["h0"], dm["h1"]]
        hinrel = [(sV, 0), (sV, 0)]
        x1s = [dm["x0"], dm["x1"]]
        x1rel2 = [(sG, 0), (sG, 0)]
        wdls = [dm["wr0"], dm["wr1"], dm["wr2"]]
        wdlrel = [(sT, 0)] * 3
        wdcnt = [0]
        osem = [dm["o0"], dm["o1"]]
        obrel = [(dm["o0"], 0), (dm["o1"], 0)]
        ptbrel = [(sS, 0), (sS, 0)]
        hTgev = [(sS, 0), (sS, 0)]
        hTgrd = [(sT, 0), (sT, 0)]
        qtr = [(sT, 0)]

        def q_load(tb, qq):
            slh = (tb * (DF // NQ) + qq) % 2
            ring = Y if slh == 0 else Sc
            wait(ring, hinrel[slh])
            hin[slh].inc(ring.dma_start(
                hq[slh][:], d_hfp.ap()[tb * 128:(tb + 1) * 128,
                                       qq * NQ:(qq + 1) * NQ]), 16)

        def q_comp(tb, qq, lead):
            tbl = tb % 4
            slh = (tb * (DF // NQ) + qq) % 2
            if qq == 0:
                wait(V, (sV, sV.v))
                vop("tensor_scalar", st4[0][:], hamax[:, tb:tb + 1], 1e-5, None,
                    ALU.max)
                vop("reciprocal", st4[1][:], st4[0][:])
                vop("tensor_scalar", st4[1][:], st4[1][:], 127.0, None, ALU.mult)
                vop("tensor_tensor", g_shinv[:, tb:tb + 1], st4[0][:], g_rec127[:],
                    ALU.mult)
            wait(V, (hin[slh], hin[slh].v), (sV, sV.v), qtr[0])
            vop("tensor_scalar", hqt[:], hq[slh][:], st4[1][:], None, ALU.mult)
            hinrel[slh] = (sV, sV.v)
            vop("tensor_copy", hqi[:], hqt[:])
            vop("tensor_tensor", hqt[:], hqt[:], hqi[:], ALU.subtract)
            vop("tensor_scalar", hqt[:], hqt[:], MAGIC, -MAGIC, ALU.add, ALU.add)
            vop("tensor_tensor", hqo[:], hqi[:], hqt[:], ALU.add)
            qtok = (sV, sV.v)
            for j2 in range(2):
                wait(T, qtok, ptbrel[j2], hTgrd[lead] if qq == 0 and j2 == 0
                     else (sT, 0))
                for j in range(4):
                    tr = T.transpose(ptb[j2][:, j * 128:(j + 1) * 128],
                                     hqo[:, (j2 * 4 + j) * 128:
                                         (j2 * 4 + j + 1) * 128],
                                     g_ident[:])
                sT.inc(tr)
                wait(Sc, (sT, sT.v), (sS, sS.v))
                for j in range(4):
                    fc = qq * 8 + j2 * 4 + j
                    ev = Sc.activation(
                        hTg[lead][:, fc * 512 + tbl * 128:
                                  fc * 512 + (tbl + 1) * 128],
                        ptb[j2][:, j * 128:(j + 1) * 128], AF.Identity)
                sS.inc(ev)
                ptbrel[j2] = (sS, sS.v)
            qtr[0] = (sT, sT.v)

        def quant_tokb(tb, lead):
            for qq in range(DF // NQ):
                q_load(tb, qq)
                q_comp(tb, qq, lead)
            hTgev[lead] = (sS, sS.v)

        for tbl in range(4):
            quant_tokb(tbl, 0)
        for tw in range(4):
            cdg = (tw % 2) * 4
            for tbl in range(4):
                tb = tw * 4 + tbl
                wait(V, (sV, sV.v), (sS, sS.v))
                vop("tensor_tensor", cdc[:, cdg + tbl:cdg + tbl + 1],
                    g_shinv[:, tb:tb + 1], swn_d[:], ALU.mult)
            cdtok = (sV, sV.v)
            lead = tw % 2
            qjobs = []
            ncomp = [0]
            if tw + 1 < 4:
                qjobs = [((tw + 1) * 4 + tbl2, qq) for tbl2 in range(4)
                         for qq in range(DF // NQ)]
                for k in range(2):
                    q_load(*qjobs[k])

            def q_step(n):
                for _ in range(n):
                    k = ncomp[0]
                    if k >= len(qjobs):
                        return
                    ncomp[0] = k + 1
                    q_comp(*qjobs[k], lead=(tw + 1) % 2)
                    if k + 2 < len(qjobs):
                        q_load(*qjobs[k + 2])
            for dmc in range(4):
                wait(T, hTgev[lead], prel[0], prel[1], prel[2], prel[3])
                for wg8 in range(8):
                    slw = wdcnt[0] % 3
                    wdcnt[0] += 1
                    ring = Sc if wg8 % 2 else Y
                    wait(ring, wdlrel[slw])
                    wdls[slw].inc(ring.dma_start(wdstr[slw][:],
                                                 d_wdt.ap()[dmc, wg8]), 16)
                    wait(T, (wdls[slw], wdls[slw].v))
                    for c8 in range(8):
                        fc = wg8 * 8 + c8
                        for tbl in range(4):
                            mm = T.matmul(pb[tbl][:],
                                          hTg[lead][:, fc * 512 + tbl * 128:
                                              fc * 512 + (tbl + 1) * 128],
                                          wdstr[slw][:, c8 * 512:(c8 + 1) * 512],
                                          start=(fc == 0), stop=(fc == 63))
                    sT.inc(mm)
                    wdlrel[slw] = (sT, sT.v)
                mmdone = (sT, sT.v)
                for tbl in range(4):
                    tb = tw * 4 + tbl
                    ob = tbl % 2
                    wait(Sc, mmdone, cdtok, (sG, sG.v))
                    sop("activation", evb[ob][:], pb[tbl][:], AF.Identity,
                        bias=0.0, scale=cdc[:, cdg + tbl:cdg + tbl + 1])
                    prel[tbl] = (sS, sS.v)
                    wait(Y, x1rel2[ob])
                    x1s[ob].inc(Y.dma_start(
                        x1in[ob][:], d_x1.ap()[tb * 128:(tb + 1) * 128,
                                               dmc * 512:(dmc + 1) * 512]), 16)
                    wait(G, (sS, sS.v), (x1s[ob], x1s[ob].v), obrel[ob], (sG, sG.v))
                    gop("tensor_tensor", outb[ob][:], evb[ob][:], x1in[ob][:],
                        ALU.add)
                    x1rel2[ob] = (sG, sG.v)
                    wait(G, (sG, sG.v))
                    osem[ob].inc(G.dma_start(
                        d_out.ap()[tb * 128:(tb + 1) * 128,
                                   dmc * 512:(dmc + 1) * 512], outb[ob][:]), 16)
                    obrel[ob] = (osem[ob], osem[ob].v)
                q_step(2)
            hTgrd[lead] = (sT, sT.v)
            if tw + 1 < 4:
                q_step(len(qjobs))
                hTgev[(tw + 1) % 2] = (sS, sS.v)
        wait(Sc, (osem[0], osem[0].v), (osem[1], osem[1].v))
        wait(G, (osem[0], osem[0].v), (osem[1], osem[1].v))

    return nc, st, RECS


def _finalize(nc, st, RECS):
    with nc.Block() as block:
        block.sync(lambda e: RECS[0].replay(e))
        block.tensor(lambda e: RECS[1].replay(e))
        block.vector(lambda e: RECS[2].replay(e))
        block.scalar(lambda e: RECS[3].replay(e))
        block.gpsimd(lambda e: RECS[4].replay(e))
    st.close()
    return nc


def build(upto=99):
    nc, st, RECS = build_program(upto)
    return _finalize(nc, st, RECS)


def own_index(par):
    return np.concatenate(
        [np.arange(1024 * t + 512 * par, 1024 * t + 512 * par + 512)
         for t in range(4)])


def host_inputs(inputs):
    x = np.asarray(inputs["L0_attn_x"], np.float32)
    Wq = np.asarray(inputs["L0_attn_Wq"], np.float32)
    Wk = np.asarray(inputs["L0_attn_Wk"], np.float32)
    Wv = np.asarray(inputs["L0_attn_Wv"], np.float32)
    Wo = np.asarray(inputs["L0_attn_Wo"], np.float32)
    bq = np.asarray(inputs["L0_attn_bq"], np.float32)
    bk = np.asarray(inputs["L0_attn_bk"], np.float32)
    bv = np.asarray(inputs["L0_attn_bv"], np.float32)
    bo = np.asarray(inputs["L0_attn_bo"], np.float32)
    wg = np.asarray(inputs["L0_ffn_w_gate"], np.float32)
    wu = np.asarray(inputs["L0_ffn_w_up"], np.float32)
    wdn = np.asarray(inputs["L0_ffn_w_down"], np.float32)

    WqT = np.ascontiguousarray(Wq.T)
    WkT = np.ascontiguousarray(Wk.T)
    WvT = np.ascontiguousarray(Wv.T)
    WoT = np.ascontiguousarray(Wo.T)
    wgT = np.ascontiguousarray(wg.T)
    wuT = np.ascontiguousarray(wu.T)
    wdT = np.ascontiguousarray(wdn.T)
    bqT = np.ascontiguousarray(bq.reshape(NH, 128).T)
    bkT = np.ascontiguousarray(bk.reshape(NH, 128).T)
    bvb = np.ascontiguousarray(np.broadcast_to(bv, (128, D)))
    bob = np.ascontiguousarray(np.broadcast_to(bo, (128, D)))
    pos = np.arange(S, dtype=np.float32).reshape(1, S)
    jcol = np.repeat(np.arange(64, dtype=np.float32), 2).reshape(128, 1)
    sgn = np.tile(np.array([-1.0, 1.0], np.float32), 64).reshape(128, 1)

    in_maps = []
    for c in range(8):
        b, par = c // 2, c % 2
        idx = own_index(par)
        in_maps.append(dict(
            xq=np.ascontiguousarray(x[b][idx]),
            xkv=np.ascontiguousarray(x[b]),
            WqT=WqT, WkT=WkT, WvT=WvT, WoT=WoT,
            bqT=bqT, bkT=bkT, bvb=bvb, bob=bob,
            wgT=wgT, wuT=wuT, wdT=wdT,
            qpos=idx.astype(np.float32).reshape(1, OWN),
            pos=pos, jcol=jcol, sgn=sgn, ident=np.eye(128, dtype=np.float32),
        ))
    return in_maps


def kernel(**inputs):
    in_maps = host_inputs(inputs)
    nc = build()
    res = run_bass_kernel_spmd(nc, in_maps, core_ids=list(range(8)))
    out = np.empty((4, S, D), np.float32)
    for c in range(8):
        b, par = c // 2, c % 2
        out[b, own_index(par)] = np.asarray(res.results[c]["out"])
    return out


# version 23 (final)
# speedup vs baseline: 1.0443x; 1.0185x over previous
import numpy as np
from contextlib import ExitStack

import concourse.bass as bass
import concourse.mybir as mybir
from concourse.bass_utils import run_bass_kernel_spmd

F32 = mybir.dt.float32
BF16 = mybir.dt.bfloat16
I32 = mybir.dt.int32
AF = mybir.ActivationFunctionType
ALU = mybir.AluOpType
AX = mybir.AxisListType

D = 2048
S = 4096
NH = 16
DH = 128
DF = 8192
OWN = 2048
NIC = D // 128
NW_OWN = OWN // 512
NW_KV = S // 512
NKB = S // 128

TWO_PI = 2.0 * np.pi
CH = float(np.round(TWO_PI * 2 ** 11) * 2.0 ** -11)
CM = float(np.round((TWO_PI - CH) * 2 ** 25) * 2.0 ** -25)
CL = float(np.round((TWO_PI - CH - CM) * 2 ** 39) * 2.0 ** -39)
CR = float(np.float32(TWO_PI - CH - CM - CL))
IH = float(np.float32(1.0 / TWO_PI))
IL = float(np.float32(1.0 / TWO_PI - IH))
PH = float(np.float32(np.pi / 2))
PM2 = float(np.float32(np.pi / 2 - PH))
PL2 = float(np.float32(np.pi / 2 - PH - PM2))
PIM = float(np.nextafter(np.float32(np.pi), np.float32(0.0)))
MAGIC = float(3 * 2 ** 51)


class _RInst:
    __slots__ = ("incs",)

    def __init__(self):
        self.incs = []

    def then_inc(self, sem, val=1):
        self.incs.append((sem, val))
        return self


class Rec:
    def __init__(self, name):
        self.name = name
        self.ops = []

    def _op(self, meth, *a, **k):
        inst = _RInst()
        self.ops.append((meth, a, k, inst))
        return inst

    def __getattr__(self, meth):
        if meth.startswith("_") or meth in ("name", "ops", "replay"):
            raise AttributeError(meth)
        return lambda *a, **k: self._op(meth, *a, **k)

    def replay(self, eng):
        for meth, a, k, inst in self.ops:
            real = getattr(eng, meth)(*a, **k)
            for sem, val in inst.incs:
                real.then_inc(sem, val)


class Sem:
    def __init__(self, h, name):
        self.h = h
        self.name = name
        self.v = 0

    def inc(self, inst, n=1):
        inst.then_inc(self.h, n)
        self.v += n
        return self.v


def build_program(upto=99):
    nc = bass.Bass("TRN2", target_bir_lowering=False, debug=False)

    d_xq = nc.dram_tensor("xq", [OWN, D], F32, kind="ExternalInput")
    d_xkv = nc.dram_tensor("xkv", [S, D], F32, kind="ExternalInput")
    d_WqT = nc.dram_tensor("WqT", [D, D], F32, kind="ExternalInput")
    d_WkT = nc.dram_tensor("WkT", [D, D], F32, kind="ExternalInput")
    d_WvT = nc.dram_tensor("WvT", [D, D], F32, kind="ExternalInput")
    d_WoT = nc.dram_tensor("WoT", [D, D], F32, kind="ExternalInput")
    d_bqT = nc.dram_tensor("bqT", [128, NH], F32, kind="ExternalInput")
    d_bkT = nc.dram_tensor("bkT", [128, NH], F32, kind="ExternalInput")
    d_bvb = nc.dram_tensor("bvb", [128, D], F32, kind="ExternalInput")
    d_bob = nc.dram_tensor("bob", [128, D], F32, kind="ExternalInput")
    d_wgT = nc.dram_tensor("wgT", [D, DF], F32, kind="ExternalInput")
    d_wuT = nc.dram_tensor("wuT", [D, DF], F32, kind="ExternalInput")
    d_wdT = nc.dram_tensor("wdT", [DF, D], F32, kind="ExternalInput")
    d_qpos = nc.dram_tensor("qpos", [1, OWN], F32, kind="ExternalInput")
    d_pos = nc.dram_tensor("pos", [1, S], F32, kind="ExternalInput")
    d_jcol = nc.dram_tensor("jcol", [128, 1], F32, kind="ExternalInput")
    d_sgn = nc.dram_tensor("sgn", [128, 1], F32, kind="ExternalInput")
    d_ident = nc.dram_tensor("ident", [128, 128], F32, kind="ExternalInput")

    d_out = nc.dram_tensor("out", [OWN, D], F32, kind="ExternalOutput")
    d_xlnq = nc.dram_tensor("xlnq", [OWN, D], BF16)
    d_xlnkv = nc.dram_tensor("xlnkv", [S, D], BF16)
    d_qT = nc.dram_tensor("qTd", [D, OWN], BF16)
    d_kT = nc.dram_tensor("kTd", [D, S], BF16)
    d_v = nc.dram_tensor("vd", [S, D], BF16)
    d_ctxT = nc.dram_tensor("ctxTd", [D, OWN], BF16)
    d_x1 = nc.dram_tensor("x1d", [OWN, D], F32)
    d_xint = nc.dram_tensor("xintd", [OWN, D], BF16)
    d_hfp = nc.dram_tensor("hfpd", [OWN, DF], F32)
    d_hint = nc.dram_tensor("hintd", [OWN, DF], BF16)
    d_wdt = nc.dram_tensor("wdtd", [4, 8, 128, 4096], BF16)

    Y = Rec("sync")
    T = Rec("tensor")
    V = Rec("vector")
    Sc = Rec("scalar")
    G = Rec("gpsimd")
    RECS = [Y, T, V, Sc, G]

    st = ExitStack()
    sems = []

    def mksem(name):
        s = Sem(st.enter_context(nc.semaphore(name)), name)
        sems.append(s)
        return s

    sT = mksem("sT")
    sV = mksem("sV")
    sS = mksem("sS")
    sG = mksem("sG")
    dm = {n: mksem("d_" + n) for n in
          ["misc", "x0", "x1", "xt0", "xt1", "w0", "st0", "st1", "q0", "q1",
           "kv0", "kv1", "c0", "c1", "h0", "h1", "o0", "o1", "p0", "p1",
           "g0", "g1", "u0", "u1", "dd0", "dd1", "wr0", "wr1", "wr2", "tn0",
           "tn1", "wa0", "wa1"]}

    def wait(rec, *toks):
        for sem, val in toks:
            if val > 0:
                rec.wait_ge(sem.h, val)

    def barrier_all():
        for r in RECS:
            for s in sems:
                if s.v > 0:
                    r.wait_ge(s.h, s.v)

    sb = lambda name, shape, dt: st.enter_context(nc.sbuf_tensor(name, shape, dt))
    g_qpos = sb("g_qpos", [1, OWN], F32)
    g_bqT = sb("g_bqT", [128, NH], F32)
    g_bkT = sb("g_bkT", [128, NH], F32)
    g_bvb = sb("g_bvb", [128, D], F32)
    g_bob = sb("g_bob", [128, D], F32)
    g_ones_row = sb("g_ones_row", [1, 128], F32)
    g_ones_col = sb("g_ones_col", [128, 1], F32)
    g_sel4 = [sb(f"g_sel4_{t}", [128, 4], BF16) for t in range(4)]
    g_ones_colb = sb("g_ones_colb", [128, 1], BF16)
    g_ident = sb("g_ident", [128, 128], BF16)
    g_iota = sb("g_iota", [128, 1], F32)
    g_sxinv = sb("g_sxinv", [128, NW_OWN * 4], F32)
    g_shinv = sb("g_shinv", [128, NW_OWN * 4], F32)
    g_r128b = sb("g_r128b", [128, 1], F32)
    g_rec127 = sb("g_rec127", [128, 1], F32)
    g_hamax = sb("g_hamax", [128, NW_OWN * 4], F32)
    p01 = st.enter_context(ExitStack())
    sb01 = lambda name, shape, dt: p01.enter_context(nc.sbuf_tensor(name, shape, dt))
    g_pos = sb01("g_pos", [1, S], F32)
    g_jcol = sb01("g_jcol", [128, 1], F32)
    g_sgn = sb01("g_sgn", [128, 1], F32)
    g_c128 = sb01("g_c128", [1, 1], F32)
    g_c10k = sb01("g_c10k", [1, 1], F32)
    g_c127 = sb01("g_c127", [1, 1], F32)
    g_Ck = sb01("g_Ck", [128, S], F32)
    g_Sk = sb01("g_Sk", [128, S], F32)
    g_Cq = sb01("g_Cq", [128, OWN], F32)
    g_Sq = sb01("g_Sq", [128, OWN], F32)

    pb = [st.enter_context(nc.psum_tensor(f"pb{i}", [128, 512], F32)) for i in range(6)]
    prel = [(sT, 0)] * 8

    for ap_d, t_s in [(d_qpos, g_qpos), (d_pos, g_pos), (d_jcol, g_jcol),
                      (d_sgn, g_sgn), (d_bqT, g_bqT), (d_bkT, g_bkT),
                      (d_bvb, g_bvb), (d_bob, g_bob)]:
        dm["misc"].inc(Y.dma_start(t_s[:], ap_d.ap()), 16)

    def vop(meth, *a, **k):
        if sV.v:
            V.wait_ge(sV.h, sV.v)
        return sV.inc(getattr(V, meth)(*a, **k))

    def sop(meth, *a, **k):
        if sS.v:
            Sc.wait_ge(sS.h, sS.v)
        return sS.inc(getattr(Sc, meth)(*a, **k))

    def gop(meth, *a, **k):
        if sG.v:
            G.wait_ge(sG.h, sG.v)
        return sG.inc(getattr(G, meth)(*a, **k))

    def top(meth, *a, **k):
        return sT.inc(getattr(T, meth)(*a, **k))

    gop("memset", g_ones_row[:], 1.0)
    gop("memset", g_ones_col[:], 1.0)
    gop("memset", g_ones_colb[:], 1.0)
    for t in range(4):
        gop("memset", g_sel4[t][:], 0.0)
        gop("memset", g_sel4[t][:, t:t + 1], 1.0)
    gop("iota", g_iota[:], [[0, 1]], channel_multiplier=1,
        allow_small_or_imprecise_dtypes=True)
    gop("memset", g_c128[:], 128.0)
    gop("memset", g_c10k[:], 10000.0)
    gop("memset", g_c127[:], 127.0)
    gop("memset", g_sxinv[:], 0.0)
    gop("memset", g_shinv[:], 0.0)
    dm["w0"].inc(G.dma_start(g_ident[:], d_ident.ap()), 16)

    with ExitStack() as p0:
        sb0 = lambda name, shape, dt: p0.enter_context(nc.sbuf_tensor(name, shape, dt))
        t_ln10k = sb0("t_ln10k", [1, 1], F32)
        t_e64 = sb0("t_e64", [1, 1], F32)
        t_e64b = sb0("t_e64b", [128, 1], F32)
        t_fcol = sb0("t_fcol", [128, 1], F32)
        t_l128 = sb0("t_l128", [1, 1], F32)
        t_h2 = sb0("t_h2", [1, 1], F32)
        t_r128 = sb0("t_r128", [1, 1], F32)
        p0A = p0.enter_context(nc.psum_tensor("p0A", [128, 512], F32))
        p0B = p0.enter_context(nc.psum_tensor("p0B", [128, 512], F32))
        t_posb = sb0("t_posb", [128, 2048], F32)
        t_w1 = sb0("t_w1", [128, 2048], F32)
        t_w2 = sb0("t_w2", [128, 2048], F32)
        t_wi = sb0("t_wi", [128, 2048], I32)
        t_fhi = sb0("t_fhi", [128, 1], F32)
        t_flo = sb0("t_flo", [128, 1], F32)
        t_fi = sb0("t_fi", [128, 1], I32)
        t_fq = sb0("t_fq", [128, 1], F32)
        t_Ahi = sb0("t_Ahi", [128, 2048], F32)
        t_Alo = sb0("t_Alo", [128, 2048], F32)
        xb = [sb0(f"p1_x{i}", [128, D], F32) for i in range(2)]
        dump = sb0("p1_dump", [128, D], F32)
        NLN = OWN // 128 + S // 128
        lnsA = sb0("p1_lnsA", [128, NLN], F32)
        lnsB = sb0("p1_lnsB", [128, NLN], F32)
        lnM = sb0("p1_lnM", [128, NLN], F32)
        lnV = sb0("p1_lnV", [128, NLN], F32)
        lnb = [sb0(f"p1_ln{i}", [128, D], BF16) for i in range(2)]

        wait(Y, (dm["misc"], dm["misc"].v))
        wait(Sc, (sG, sG.v), (dm["misc"], dm["misc"].v))
        sop("activation", t_ln10k[:], g_c10k[:], AF.Ln)
        wait(V, (sS, sS.v))
        vop("tensor_scalar", t_e64[:], t_ln10k[:], -1.0 / 64.0, None, ALU.mult)
        wait(T, (sV, sV.v), (sG, sG.v))
        top("matmul", p0A[:, 0:1], g_ones_row[:], t_e64[:], start=True, stop=True)
        wait(Sc, (sT, sT.v))
        sop("activation", t_e64b[:], p0A[:, 0:1], AF.Identity)
        sop("activation", t_fcol[:], g_jcol[:], AF.Exp, bias=0.0, scale=t_e64b[:])
        sop("activation", t_l128[:], g_c128[:], AF.Ln)
        wait(V, (sS, sS.v))
        vop("tensor_scalar", t_h2[:], t_l128[:], -0.5, None, ALU.mult)
        wait(Sc, (sV, sV.v))
        sop("activation", t_r128[:], t_h2[:], AF.Exp)
        wait(T, (sS, sS.v))
        top("matmul", p0B[:, 0:1], g_ones_row[:], t_r128[:], start=True, stop=True)
        wait(Sc, (sT, sT.v))
        sop("activation", g_r128b[:], p0B[:, 0:1], AF.Identity)
        wait(V, (sG, sG.v), (sS, sS.v))
        vop("reciprocal", t_h2[:], g_c127[:])
        wait(T, (sV, sV.v))
        top("matmul", p0B[:, 0:1], g_ones_row[:], t_h2[:], start=True, stop=True)
        wait(Sc, (sT, sT.v))
        sop("activation", g_rec127[:], p0B[:, 0:1], AF.Identity)

        xin = [dm["x0"], dm["x1"]]
        sto = [dm["st0"], dm["st1"]]
        rel = [(sS, 0), (sS, 0)]
        relo = [(sS, 0), (sS, 0)]
        ln_jobs = []
        if upto >= 1:
            ln_jobs += [(d_xq, d_xlnq, tb) for tb in range(OWN // 128)]
            ln_jobs += [(d_xkv, d_xlnkv, tb) for tb in range(S // 128)]
        ln_i = [0]

        def emit_ln(n):
            for _ in range(n):
                i = ln_i[0]
                if i >= len(ln_jobs):
                    return
                ln_i[0] = i + 1
                lsrc, ldst, tb = ln_jobs[i]
                sl = i % 2
                wait(Y, rel[sl])
                dm_t = xin[sl]
                dm_t.inc(Y.dma_start(xb[sl][:], lsrc.ap()[tb * 128:(tb + 1) * 128, :]), 16)
                wait(Sc, (dm_t, dm_t.v))
                sop("activation", dump[:], xb[sl][:], AF.Identity,
                    accum_out=lnsA[:, i:i + 1])
                sop("activation", dump[:], xb[sl][:], AF.Square,
                    accum_out=lnsB[:, i:i + 1])
                rel[sl] = (sS, sS.v)

        def emit_lnB():
            if not ln_jobs:
                return
            wait(V, (sS, sS.v), (sV, sV.v))
            vop("tensor_scalar", lnM[:], lnsA[:], 1.0 / 2048.0, None, ALU.mult)
            vop("tensor_tensor", lnV[:], lnM[:], lnM[:], ALU.mult)
            vop("tensor_scalar", lnsB[:], lnsB[:], 1.0 / 2048.0, None, ALU.mult)
            vop("tensor_tensor", lnV[:], lnsB[:], lnV[:], ALU.subtract)
            vop("tensor_scalar", lnV[:], lnV[:], 1e-5, None, ALU.add)
            wait(Sc, (sV, sV.v))
            sop("activation", lnV[:], lnV[:], AF.Ln)
            sop("activation", lnV[:], lnV[:], AF.Exp, bias=0.0, scale=-0.5)
            wait(V, (sS, sS.v))
            vop("tensor_tensor", lnM[:], lnM[:], lnV[:], ALU.mult)
            vop("tensor_scalar", lnM[:], lnM[:], -1.0, None, ALU.mult)
            chain = (sV, sV.v)
            for i, (lsrc, ldst, tb) in enumerate(ln_jobs):
                sl = i % 2
                wait(Y, rel[sl])
                dm_t = xin[sl]
                dm_t.inc(Y.dma_start(xb[sl][:], lsrc.ap()[tb * 128:(tb + 1) * 128, :]), 16)
                wait(Sc, (dm_t, dm_t.v), chain, relo[sl])
                sop("activation", lnb[sl][:], xb[sl][:], AF.Identity,
                    bias=lnM[:, i:i + 1], scale=lnV[:, i:i + 1])
                rel[sl] = (sS, sS.v)
                wait(Sc, (sS, sS.v))
                dmo = sto[sl]
                dmo.inc(Sc.dma_start(ldst.ap()[tb * 128:(tb + 1) * 128, :], lnb[sl][:]), 16)
                relo[sl] = (dmo, dmo.v)

        tok_sin = [(sS, 0)]
        tok_ang = [(sV, 0)]

        wait(V, (sS, sS.v))
        vop("tensor_scalar", t_fhi[:], t_fcol[:], 4096.0, 0.5, ALU.mult, ALU.add)
        vop("tensor_copy", t_fi[:], t_fhi[:])
        vop("tensor_tensor", t_fq[:], t_fi[:], t_fhi[:], ALU.is_gt)
        vop("tensor_tensor", t_fhi[:], t_fi[:], t_fq[:], ALU.subtract)
        vop("tensor_scalar", t_fhi[:], t_fhi[:], 2.0 ** -12, None, ALU.mult)
        vop("tensor_tensor", t_flo[:], t_fcol[:], t_fhi[:], ALU.subtract)

        def reduce_and_sin(out, c0out, W, signed, cosshift):
            wait(V, tok_sin[0])
            vop("tensor_tensor", t_w2[:, :W], t_Ahi[:, :W], t_Alo[:, :W], ALU.add)
            if cosshift:
                vop("tensor_scalar", t_w2[:, :W], t_w2[:, :W], PH, None, ALU.add)
            vop("tensor_scalar", t_w1[:, :W], t_w2[:, :W], IH, None, ALU.mult)
            vop("scalar_tensor_tensor", t_w1[:, :W], t_w2[:, :W], IL,
                t_w1[:, :W], ALU.mult, ALU.add)
            vop("tensor_scalar", t_w1[:, :W], t_w1[:, :W], 0.5, None, ALU.add)
            vop("tensor_copy", t_wi[:, :W], t_w1[:, :W])
            vop("tensor_tensor", t_w2[:, :W], t_wi[:, :W], t_w1[:, :W], ALU.is_gt)
            vop("tensor_tensor", t_w1[:, :W], t_wi[:, :W], t_w2[:, :W],
                ALU.subtract)
            vop("scalar_tensor_tensor", t_w2[:, :W], t_w1[:, :W], -CH,
                t_Ahi[:, :W], ALU.mult, ALU.add)
            vop("tensor_tensor", t_w2[:, :W], t_w2[:, :W], t_Alo[:, :W], ALU.add)
            if cosshift:
                vop("tensor_scalar", t_w2[:, :W], t_w2[:, :W], PH, None, ALU.add)
                vop("tensor_scalar", t_w2[:, :W], t_w2[:, :W], PM2, None, ALU.add)
                vop("tensor_scalar", t_w2[:, :W], t_w2[:, :W], PL2, None, ALU.add)
            vop("scalar_tensor_tensor", t_w2[:, :W], t_w1[:, :W], -CM,
                t_w2[:, :W], ALU.mult, ALU.add)
            vop("scalar_tensor_tensor", t_w2[:, :W], t_w1[:, :W], -CL,
                t_w2[:, :W], ALU.mult, ALU.add)
            vop("scalar_tensor_tensor", t_w2[:, :W], t_w1[:, :W], -CR,
                t_w2[:, :W], ALU.mult, ALU.add)
            vop("tensor_scalar", t_w1[:, :W], t_w2[:, :W], PIM, None, ALU.is_gt)
            vop("scalar_tensor_tensor", t_w2[:, :W], t_w1[:, :W], -CH,
                t_w2[:, :W], ALU.mult, ALU.add)
            vop("scalar_tensor_tensor", t_w2[:, :W], t_w1[:, :W], -CM,
                t_w2[:, :W], ALU.mult, ALU.add)
            vop("scalar_tensor_tensor", t_w2[:, :W], t_w1[:, :W], -CL,
                t_w2[:, :W], ALU.mult, ALU.add)
            vop("tensor_scalar", t_w1[:, :W], t_w2[:, :W], -PIM, None, ALU.is_lt)
            vop("scalar_tensor_tensor", t_w2[:, :W], t_w1[:, :W], CH,
                t_w2[:, :W], ALU.mult, ALU.add)
            vop("scalar_tensor_tensor", t_w2[:, :W], t_w1[:, :W], CM,
                t_w2[:, :W], ALU.mult, ALU.add)
            vop("scalar_tensor_tensor", t_w2[:, :W], t_w1[:, :W], CL,
                t_w2[:, :W], ALU.mult, ALU.add)
            vop("tensor_scalar", t_w2[:, :W], t_w2[:, :W], PIM, None, ALU.min)
            vop("tensor_scalar", t_w2[:, :W], t_w2[:, :W], -PIM, None, ALU.max)
            if signed:
                vop("tensor_scalar", t_w2[:, :W], t_w2[:, :W], g_sgn[:], None,
                    ALU.mult)
            wait(Sc, (sV, sV.v))
            sop("activation", out[:, c0out:c0out + W], t_w2[:, :W], AF.Sin)
            tok_sin[0] = (sS, sS.v)

        def gen_table_chunk(row_t, c0, W, Cout, Sout, c0out, scale, first):
            for slc in range(0, W, 512):
                w = [(sS, sS.v)]
                if first and slc == 0:
                    w += [(sV, sV.v), (sG, sG.v)]
                wait(T, *w)
                top("matmul", p0A[:, 0:512], g_ones_row[:],
                    row_t[0:1, c0 + slc:c0 + slc + 512], start=True, stop=True)
                ws = [(sT, sT.v)]
                if slc == 0:
                    ws.append(tok_ang[0])
                wait(Sc, *ws)
                sop("activation", t_posb[:, slc:slc + 512], p0A[:, 0:512], AF.Identity)
            bc_tok = (sS, sS.v)
            wait(V, bc_tok, tok_sin[0])
            vop("tensor_scalar", t_Ahi[:, :W], t_posb[:, :W], t_fhi[:], None, ALU.mult)
            vop("tensor_scalar", t_Alo[:, :W], t_posb[:, :W], t_flo[:], None, ALU.mult)
            tok_ang[0] = (sV, sV.v)
            reduce_and_sin(Sout, c0out, W, True, False)
            emit_ln(8)
            reduce_and_sin(Cout, c0out, W, False, True)
            if scale:
                wait(V, tok_sin[0])
                vop("tensor_scalar", Cout[:, c0out:c0out + W], Cout[:, c0out:c0out + W],
                    g_r128b[:], None, ALU.mult)
                vop("tensor_scalar", Sout[:, c0out:c0out + W], Sout[:, c0out:c0out + W],
                    g_r128b[:], None, ALU.mult)
            emit_ln(8)

        first = True
        for c0 in range(0, S, 2048):
            gen_table_chunk(g_pos, c0, 2048, g_Ck, g_Sk, c0, False, first)
            first = False
        gen_table_chunk(g_qpos, 0, OWN, g_Cq, g_Sq, 0, True, False)
        emit_ln(len(ln_jobs))
        emit_lnB()
        barrier_all()

    if upto <= 1:
        return nc, st, RECS

    def proj_pass(Wsrc, xln_src, n_win, kind):
        with ExitStack() as pp:
            sbp = lambda name, shape, dt: pp.enter_context(
                nc.sbuf_tensor(f"{kind}_" + name, shape, dt))
            Wsb = sbp("pp_W", [128, NIC * D], BF16)
            xT = [sbp(f"pp_xT{i}", [128, NIC * 512], BF16) for i in range(2)]
            qb = [sbp(f"pp_qb{i}", [128, 512], F32) for i in range(2)]
            qs = [sbp(f"pp_qs{i}", [128, 512], F32) for i in range(2)]
            t1 = [sbp(f"pp_t1{i}", [128, 512], F32) for i in range(2)]
            t2 = [sbp(f"pp_t2{i}", [128, 512], F32) for i in range(2)]
            ro = [sbp(f"pp_ro{i}", [128, 512], BF16) for i in range(2)]
            wsem = dm["w0"]
            for part in range(4):
                rows = Wsrc.ap()[part * 512:(part + 1) * 512, :].rearrange(
                    "(c p) o -> p c o", p=128)
                dst3 = Wsb[:, part * 4 * D:(part + 1) * 4 * D].rearrange(
                    "p (c o) -> p c o", c=4)
                wsem.inc(G.dma_start(dst3, rows), 16)
            xsem = [dm["xt0"], dm["xt1"]]
            xrel = [(sT, 0), (sT, 0)]
            psrel = [(sT, 0)] * 6
            orel = [(dm["q0"], 0), (dm["q1"], 0)]
            osem = [dm["q0"], dm["q1"]]
            if kind == "k":
                Ct, St, bcol = g_Ck, g_Sk, g_bkT
            elif kind == "q":
                Ct, St, bcol = g_Cq, g_Sq, g_bqT
            else:
                Ct, St, bcol = None, None, None

            for w in range(n_win):
                sl = w % 2
                wait(Y, xrel[sl])
                for ic in range(NIC):
                    xsem[sl].inc(Y.dma_start(
                        xT[sl][:, ic * 512:(ic + 1) * 512],
                        xln_src.ap()[w * 512:(w + 1) * 512, ic * 128:(ic + 1) * 128],
                        transpose=True), 16)
                if kind in ("q", "k"):
                    for h in range(NH):
                        bank = (w * NH + h) % 6
                        wait(T, (xsem[sl], xsem[sl].v), (wsem, wsem.v), psrel[bank])
                        for ic in range(NIC):
                            mm = T.matmul(pb[bank][:],
                                          Wsb[:, ic * D + h * 128: ic * D + (h + 1) * 128],
                                          xT[sl][:, ic * 512:(ic + 1) * 512],
                                          start=(ic == 0), stop=(ic == NIC - 1))
                        sT.inc(mm)
                        mmv = sT.v
                        eb = (w * NH + h) % 2
                        wait(Sc, (sT, mmv), orel[eb])
                        sop("activation", qb[eb][:], pb[bank][:], AF.Identity,
                            bias=bcol[:, h:h + 1])
                        psrel[bank] = (sS, sS.v)
                        wait(V, (sS, sS.v))
                        vop("stream_shuffle", qs[eb][:], qb[eb][:],
                            [p ^ 1 for p in range(32)])
                        vop("tensor_tensor", t1[eb][:], qb[eb][:],
                            Ct[:, w * 512:(w + 1) * 512], ALU.mult)
                        wait(G, (sV, sV.v))
                        gop("tensor_tensor", t2[eb][:], qs[eb][:],
                            St[:, w * 512:(w + 1) * 512], ALU.mult)
                        wait(V, (sG, sG.v))
                        vop("tensor_tensor", ro[eb][:], t1[eb][:], t2[eb][:], ALU.add)
                        wait(Sc, (sV, sV.v))
                        dst = d_qT if kind == "q" else d_kT
                        osem[eb].inc(Sc.dma_start(
                            dst.ap()[h * 128:(h + 1) * 128, w * 512:(w + 1) * 512],
                            ro[eb][:]), 16)
                        orel[eb] = (osem[eb], osem[eb].v)
                    xrel[sl] = (sT, sT.v)
                else:
                    for tb in range(4):
                        for dvc in range(4):
                            bank = (tb * 4 + dvc) % 6
                            wait(T, (xsem[sl], xsem[sl].v), (wsem, wsem.v), psrel[bank])
                            for ic in range(NIC):
                                mm = T.matmul(
                                    pb[bank][:],
                                    xT[sl][:, ic * 512 + tb * 128: ic * 512 + (tb + 1) * 128],
                                    Wsb[:, ic * D + dvc * 512: ic * D + (dvc + 1) * 512],
                                    start=(ic == 0), stop=(ic == NIC - 1))
                            sT.inc(mm)
                            eb = (tb * 4 + dvc) % 2
                            wait(V, (sT, sT.v), orel[eb])
                            vop("tensor_tensor", ro[eb][:], pb[bank][:],
                                g_bvb[:, dvc * 512:(dvc + 1) * 512], ALU.add)
                            psrel[bank] = (sV, sV.v)
                            wait(Sc, (sV, sV.v))
                            osem[eb].inc(Sc.dma_start(
                                d_v.ap()[w * 512 + tb * 128: w * 512 + (tb + 1) * 128,
                                         dvc * 512:(dvc + 1) * 512], ro[eb][:]), 16)
                            orel[eb] = (osem[eb], osem[eb].v)
                    xrel[sl] = (sT, sT.v)
            barrier_all()

    proj_pass(d_WqT, d_xlnq, NW_OWN, "q")
    proj_pass(d_WkT, d_xlnkv, NW_KV, "k")
    proj_pass(d_WvT, d_xlnkv, NW_KV, "v")
    p01.close()

    if upto == 2:
        return nc, st, RECS

    wstats = {}
    for mi in range(3):
        wstats[mi] = (sb(f"p4a_swn{mi}", [128, 1], F32),
                      sb(f"p4a_tau{mi}", [128, 1], F32),
                      sb(f"p4a_invtau{mi}", [128, 1], F32))
    p4as = ExitStack()
    sba = lambda name, shape, dt: p4as.enter_context(nc.sbuf_tensor(name, shape, dt))
    wa_tile = [sba(f"wa_t{i}", [128, 2048], F32) for i in range(2)]
    wa_dump = sba("wa_dump", [128, 2048], F32)
    wa_acc = [sba(f"wa_acc{i}", [128, 1], F32) for i in range(3)]
    wa_tmp = sba("wa_tmp", [128, 1], F32)
    wa_tot = sba("wa_tot", [1, 1], F32)
    for mi in range(3):
        gop("memset", wa_acc[mi][:], 0.0)
    wa_state = {"i": 0, "rel": [(sS, 0), (sS, 0)], "first": True}

    def wa_chunk(mi, wd, r0, c0, W):
        i = wa_state["i"]
        wa_state["i"] = i + 1
        slw = i % 2
        sem = [dm["wa0"], dm["wa1"]][slw]
        wait(G, wa_state["rel"][slw])
        sem.inc(G.dma_start(wa_tile[slw][:, :W], wd.ap()[r0:r0 + 128, c0:c0 + W]), 16)
        toks = [(sem, sem.v)]
        if wa_state["first"]:
            toks.append((sG, sG.v))
            wa_state["first"] = False
        wait(Sc, *toks)
        sop("activation", wa_dump[:, :W], wa_tile[slw][:, :W], AF.Abs,
            accum_out=wa_tmp[:])
        wa_state["rel"][slw] = (sS, sS.v)
        sop("activation", wa_acc[mi][:], wa_tmp[:], AF.Identity,
            bias=wa_acc[mi][:], scale=1.0)

    wa_items = []
    for mi, (wdt, Rr, Cc) in enumerate([(d_wgT, D, DF), (d_wuT, D, DF),
                                        (d_wdT, DF, D)]):
        for ti in range(Rr // 128):
            for c in range(Cc // 2048):
                wa_items.append((lambda mi=mi, wdt=wdt, ti=ti, c=c:
                                 wa_chunk(mi, wdt, ti * 128, c * 2048, 2048)))

    def wa_feed(n):
        for _ in range(n):
            if wa_items:
                wa_items.pop(0)()

    def wa_epilogue():
        for mi, (Rr, Cc) in enumerate([(D, DF), (D, DF), (DF, D)]):
            b_swn, b_tau, b_invtau = wstats[mi]
            wait(T, prel[5], (sS, sS.v))
            top("matmul", pb[5][0:1, 0:1], wa_acc[mi][:], g_ones_col[:],
                start=True, stop=True)
            wait(Sc, (sT, sT.v))
            sop("activation", wa_tot[:], pb[5][0:1, 0:1], AF.Identity)
            wait(V, (sS, sS.v))
            vop("tensor_scalar", wa_tot[:], wa_tot[:], 1.0 / (Rr * Cc), 1e-5,
                ALU.mult, ALU.max)
            wait(T, (sV, sV.v), (sS, sS.v))
            top("matmul", pb[5][:, 0:1], g_ones_row[:], wa_tot[:],
                start=True, stop=True)
            wait(Sc, (sT, sT.v))
            sop("activation", b_swn[:], pb[5][:, 0:1], AF.Identity,
                bias=0.0, scale=-1.0)
            wait(Sc, (sS, sS.v))
            sop("activation", b_tau[:], pb[5][:, 0:1], AF.Identity,
                bias=0.0, scale=0.5)
            prel[5] = (sS, sS.v)
            wait(V, (sS, sS.v))
            vop("reciprocal", b_invtau[:], b_tau[:])

    with ExitStack() as p2:
        sb2 = lambda name, shape, dt: p2.enter_context(nc.sbuf_tensor(name, shape, dt))
        p2A = p2.enter_context(nc.psum_tensor("p2A", [128, 512], F32))
        p2S = p2.enter_context(nc.psum_tensor("p2S", [128, 512], F32))
        masks = sb2("p2_masks", [128, 32 * 512], BF16)
        qpb = sb2("p2_qpb", [128, 512], F32)
        kcol = sb2("p2_kcol", [128, 1], F32)
        qTb = [sb2(f"p2_qT{i}", [128, OWN], BF16) for i in range(2)]
        kTb = [sb2(f"p2_kT{i}", [128, S], BF16) for i in range(2)]
        vb = [sb2(f"p2_v{i}", [128, NKB * 128], BF16) for i in range(2)]
        NEB = 8
        Eb = [sb2(f"p2_E{i}", [128, 512], BF16) for i in range(NEB)]
        dmp = sb2("p2_dump", [128, S], BF16)
        nm = [sb2(f"p2_nm{i}", [1, 1], F32) for i in range(8)]
        negB = [sb2(f"p2_negB{i}", [128, 1], F32) for i in range(2)]
        invr = sb2("p2_invr", [1, 512], F32)
        bcb = sb2("p2_bcb", [128, 512], F32)
        ctxb = [sb2(f"p2_ctx{i}", [128, 512], BF16) for i in range(2)]

        p2Arel = [(sS, 0)]
        for t in range(4):
            wait(T, p2Arel[0])
            top("matmul", p2A[:], g_ones_row[:],
                g_qpos[0:1, t * 512:(t + 1) * 512], start=True, stop=True)
            wait(Sc, (sT, sT.v), (sV, sV.v))
            sop("activation", qpb[:], p2A[:], AF.Identity)
            p2Arel[0] = (sS, sS.v)
            for j in range(8):
                kb = 8 * t + j
                wait(V, (sS, sS.v), (sV, sV.v))
                vop("tensor_scalar", kcol[:], g_iota[:], float(kb * 128), None, ALU.add)
                wait(V, (sV, sV.v))
                vop("tensor_scalar", masks[:, (t * 8 + j) * 512:(t * 8 + j + 1) * 512],
                    qpb[:], kcol[:], None, ALU.is_ge)
                wait(V, (sV, sV.v))

        hsem = [[dm["q0"], dm["kv0"], dm["p0"]], [dm["q1"], dm["kv1"], dm["p1"]]]
        hrel = [(sT, 0), (sT, 0)]
        p2rel = [(sT, 0)]

        def load_head(h):
            sl = h % 2
            wait(Y, hrel[sl])
            hsem[sl][0].inc(Y.dma_start(qTb[sl][:], d_qT.ap()[h * 128:(h + 1) * 128, :]), 16)
            hsem[sl][1].inc(Y.dma_start(kTb[sl][:], d_kT.ap()[h * 128:(h + 1) * 128, :]), 16)
            for part in range(4):
                src = d_v.ap()[part * 1024:(part + 1) * 1024,
                               h * 128:(h + 1) * 128].rearrange(
                    "(kb p) d -> p kb d", p=128)
                dst3 = vb[sl][:, part * 1024:(part + 1) * 1024].rearrange(
                    "p (kb d) -> p kb d", kb=8)
                hsem[sl][2].inc(Y.dma_start(dst3, src), 16)

        load_head(0)
        Erel = [(sS, 0)] * NEB
        ctxrel = [(dm["c0"], 0), (dm["c1"], 0)]
        csem = [dm["c0"], dm["c1"]]
        nbrel = [(sV, 0)]
        negB_toks = [(sS, 0), (sS, 0)]

        def normB(h):
            sl = h % 2
            wq = (hsem[sl][0], hsem[sl][0].v)
            wk = (hsem[sl][1], hsem[sl][1].v)
            NB = [5, 0, 1]
            pending = []
            done = [False, False]

            def nred(i, which, bk, tv):
                out = nm[sl * 4 + which]
                wait(V, (sT, tv), (sV, sV.v))
                if not done[which]:
                    vop("tensor_reduce", out[:], pb[bk][0:1, :], AX.X, ALU.max)
                    done[which] = True
                else:
                    vop("tensor_reduce", nm[sl * 4 + 2][:], pb[bk][0:1, :],
                        AX.X, ALU.max)
                    vop("tensor_tensor", out[:], out[:], nm[sl * 4 + 2][:],
                        ALU.max)
                prel[bk] = (sV, sV.v)

            i = 0
            for which, buf, W in [(0, qTb[sl], OWN), (1, kTb[sl], S)]:
                wait(V, wq, wk, (sV, sV.v), (sT, sT.v))
                vop("tensor_tensor", dmp[:, :W], buf[:], buf[:], ALU.mult)
                vsq = (sV, sV.v)
                for slc in range(W // 512):
                    bk = NB[i % 3]
                    if len(pending) >= 3:
                        nred(*pending.pop(0))
                    wait(T, vsq, prel[bk])
                    top("matmul", pb[bk][0:1, :], g_ones_colb[:],
                        dmp[:, slc * 512:(slc + 1) * 512], start=True, stop=True)
                    pending.append((i, which, bk, sT.v))
                    i += 1
            while pending:
                nred(*pending.pop(0))
            wait(Sc, (sV, sV.v), (sS, sS.v))
            sop("activation", nm[sl * 4 + 0][:], nm[sl * 4 + 0][:], AF.Ln)
            wait(Sc, (sS, sS.v))
            sop("activation", nm[sl * 4 + 1][:], nm[sl * 4 + 1][:], AF.Ln)
            wait(V, (sS, sS.v))
            vop("tensor_tensor", nm[sl * 4 + 3][:], nm[sl * 4 + 0][:],
                nm[sl * 4 + 1][:], ALU.add)
            vop("tensor_scalar", nm[sl * 4 + 3][:], nm[sl * 4 + 3][:], 0.5,
                None, ALU.mult)
            wait(Sc, (sV, sV.v))
            sop("activation", nm[sl * 4 + 3][:], nm[sl * 4 + 3][:], AF.Exp)
            wait(T, (sS, sS.v), p2Arel[0])
            top("matmul", p2A[:, 0:1], g_ones_row[:], nm[sl * 4 + 3][:],
                start=True, stop=True)
            wait(Sc, (sT, sT.v), negB_toks[sl])
            sop("activation", negB[sl][:], p2A[:, 0:1], AF.Identity,
                bias=0.0, scale=-1.0)
            p2Arel[0] = (sS, sS.v)
            negB_toks[sl] = (sS, sS.v)

        normB(0)
        for h in range(NH):
            sl = h % 2
            if h + 1 < NH:
                load_head(h + 1)
                normB(h + 1)
            wq = (hsem[sl][0], hsem[sl][0].v)
            wk = (hsem[sl][1], hsem[sl][1].v)
            wv = (hsem[sl][2], hsem[sl][2].v)
            negB_tok = negB_toks[sl]

            SCB = [0, 1, 4]

            def drain(pend, first_sum):
                pkb, pt, etok = pend
                eb = pkb % NEB
                K_t = 8 * (pt + 1)
                wait(T, etok, wv)
                if first_sum:
                    wait(T, p2rel[0])
                if pkb == 0:
                    wait(T, prel[2 + pt % 2])
                top("matmul", p2S[0:1, :], g_ones_colb[:], Eb[eb][:],
                    start=(pkb == 0), stop=(pkb == K_t - 1))
                top("matmul", pb[2 + pt % 2][:],
                    vb[sl][:, pkb * 128:(pkb + 1) * 128], Eb[eb][:],
                    start=(pkb == 0), stop=(pkb == K_t - 1))
                Erel[eb] = (sT, sT.v)

            pending2 = []
            for t in range(4):
                K_t = 8 * (t + 1)
                for kb in range(K_t):
                    bank = SCB[kb % 3]
                    wait(T, prel[bank], wq, wk)
                    top("matmul", pb[bank][:],
                        kTb[sl][:, kb * 128:(kb + 1) * 128],
                        qTb[sl][:, t * 512:(t + 1) * 512], start=True, stop=True)
                    qk_tok = sT.v
                    while len(pending2) >= 3:
                        p_ = pending2.pop(0)
                        drain(p_, p_[0] == 0)
                    eb = kb % NEB
                    wait(Sc, (sT, qk_tok), Erel[eb], negB_tok)
                    sop("activation", Eb[eb][:], pb[bank][:], AF.Exp,
                        bias=negB[sl][:])
                    prel[bank] = (sS, sS.v)
                    etok = (sS, sS.v)
                    if kb >= 8 * t:
                        j = kb - 8 * t
                        wait(V, (sS, sS.v), (sV, sV.v))
                        vop("tensor_tensor", Eb[eb][:], Eb[eb][:],
                            masks[:, (t * 8 + j) * 512:(t * 8 + j + 1) * 512], ALU.mult)
                        etok = (sV, sV.v)
                    pending2.append((kb, t, etok))
                while pending2:
                    p_ = pending2.pop(0)
                    drain(p_, p_[0] == 0)
                avdone = (sT, sT.v)
                wait(V, avdone, (sV, sV.v))
                vop("reciprocal", invr[:], p2S[0:1, :])
                p2rel[0] = (sV, sV.v)
                wait(T, p2Arel[0], (sV, sV.v))
                top("matmul", p2A[:], g_ones_row[:], invr[:], start=True, stop=True)
                wait(Sc, (sT, sT.v))
                sop("activation", bcb[:], p2A[:], AF.Identity)
                p2Arel[0] = (sS, sS.v)
                cb = t % 2
                wait(V, (sS, sS.v), ctxrel[cb])
                vop("tensor_tensor", ctxb[cb][:], pb[2 + t % 2][:], bcb[:], ALU.mult)
                prel[2 + t % 2] = (sV, sV.v)
                wait(Sc, (sV, sV.v))
                csem[cb].inc(Sc.dma_start(
                    d_ctxT.ap()[h * 128:(h + 1) * 128, t * 512:(t + 1) * 512],
                    ctxb[cb][:]), 16)
                ctxrel[cb] = (csem[cb], csem[cb].v)
            hrel[sl] = (sT, sT.v)
            wa_feed(6)
        barrier_all()

    if upto == 3:
        return nc, st, RECS

    with ExitStack() as p3:
        sb3 = lambda name, shape, dt: p3.enter_context(nc.sbuf_tensor(name, shape, dt))
        Wosb = sb3("p3_Wo", [128, NIC * D], BF16)
        ctin = [sb3(f"p3_ct{i}", [128, NIC * 128], BF16) for i in range(2)]
        xqb = [sb3(f"p3_xq{i}", [128, D], F32) for i in range(2)]
        x1b = [sb3(f"p3_x1{i}", [128, D], F32) for i in range(2)]
        tqb = sb3("p3_tq", [128, D], F32)
        qib = sb3("p3_qi", [128, D], I32)
        xib = [sb3(f"p3_xi{i}", [128, D], BF16) for i in range(2)]
        NTB = OWN // 128
        sw_ = [sb3(f"p3_sw{i}", [128, NTB], F32) for i in range(8)]
        dump3 = sb3("p3_dump", [128, D], F32)

        wsem = dm["w0"]
        for part in range(4):
            rows = d_WoT.ap()[part * 512:(part + 1) * 512, :].rearrange(
                "(c p) o -> p c o", p=128)
            dst3 = Wosb[:, part * 4 * D:(part + 1) * 4 * D].rearrange(
                "p (c o) -> p c o", c=4)
            wsem.inc(G.dma_start(dst3, rows), 16)

        cin = [dm["c0"], dm["c1"]]
        xin = [dm["x0"], dm["x1"]]
        cinrel = [(sT, 0), (sT, 0)]
        xinrel = [(sV, 0), (sV, 0)]
        x1rel = [[(dm["st0"], 0)], [(dm["st1"], 0)]]
        xirel = [(dm["q0"], 0), (dm["q1"], 0)]
        x1sem = [dm["st0"], dm["st1"]]
        xisem = [dm["q0"], dm["q1"]]
        for tb in range(NTB):
            sl = tb % 2
            wait(Y, cinrel[sl])
            src = d_ctxT.ap()[:, tb * 128:(tb + 1) * 128].rearrange(
                "(c p) t -> p c t", p=128)
            dst3 = ctin[sl][:].rearrange("p (c t) -> p c t", c=NIC)
            cin[sl].inc(Y.dma_start(dst3, src), 16)
            wait(Y, xinrel[sl])
            xin[sl].inc(Y.dma_start(xqb[sl][:], d_xq.ap()[tb * 128:(tb + 1) * 128, :]), 16)
            for dmc in range(4):
                bank = (tb * 4 + dmc) % 6
                wait(T, (cin[sl], cin[sl].v), (wsem, wsem.v), prel[bank])
                for cfc in range(NIC):
                    mm = T.matmul(pb[bank][:],
                                  ctin[sl][:, cfc * 128:(cfc + 1) * 128],
                                  Wosb[:, cfc * D + dmc * 512: cfc * D + (dmc + 1) * 512],
                                  start=(cfc == 0), stop=(cfc == NIC - 1))
                sT.inc(mm)
                wait(V, (sT, sT.v), (xin[sl], xin[sl].v), (sV, sV.v),
                     *(x1rel[sl] if dmc == 0 else []))
                vop("tensor_tensor", x1b[sl][:, dmc * 512:(dmc + 1) * 512],
                    pb[bank][:], g_bob[:, dmc * 512:(dmc + 1) * 512], ALU.add)
                prel[bank] = (sV, sV.v)
                wait(V, (sV, sV.v))
                vop("tensor_tensor", x1b[sl][:, dmc * 512:(dmc + 1) * 512],
                    x1b[sl][:, dmc * 512:(dmc + 1) * 512],
                    xqb[sl][:, dmc * 512:(dmc + 1) * 512], ALU.add)
            xinrel[sl] = (sV, sV.v)
            cinrel[sl] = (sT, sT.v)
            x1tok = sV.v
            wait(Sc, (sV, x1tok))
            x1sem[sl].inc(Sc.dma_start(d_x1.ap()[tb * 128:(tb + 1) * 128, :], x1b[sl][:]), 16)
            wait(Sc, (sV, x1tok))
            sop("activation", dump3[:], x1b[sl][:], AF.Identity,
                accum_out=sw_[0][:, tb:tb + 1])
            wait(Sc, (sS, sS.v))
            sop("activation", dump3[:], x1b[sl][:], AF.Square,
                accum_out=sw_[1][:, tb:tb + 1])
            wait(V, (sV, sV.v), (sS, sS.v))
            vop("tensor_reduce", sw_[2][:, tb:tb + 1], x1b[sl][:], AX.X, ALU.max)
            wait(V, (sV, sV.v))
            vop("tensor_reduce", sw_[3][:, tb:tb + 1], x1b[sl][:], AX.X, ALU.min)
            x1rel[sl] = [(x1sem[sl], x1sem[sl].v), (sS, sS.v), (sV, sV.v)]
            wa_feed(4)
        wait(V, (sV, sV.v), (sS, sS.v))
        vop("tensor_scalar", sw_[4][:], sw_[0][:], 1.0 / 2048.0, None, ALU.mult)
        vop("tensor_tensor", sw_[5][:], sw_[4][:], sw_[4][:], ALU.mult)
        vop("tensor_scalar", sw_[6][:], sw_[1][:], 1.0 / 2048.0, None, ALU.mult)
        vop("tensor_tensor", sw_[6][:], sw_[6][:], sw_[5][:], ALU.subtract)
        vop("tensor_scalar", sw_[6][:], sw_[6][:], 1e-5, None, ALU.add)
        wait(Sc, (sV, sV.v))
        sop("activation", sw_[6][:], sw_[6][:], AF.Ln)
        wait(Sc, (sS, sS.v))
        sop("activation", sw_[6][:], sw_[6][:], AF.Exp, bias=0.0, scale=-0.5)
        wait(V, (sS, sS.v))
        vop("tensor_tensor", sw_[2][:], sw_[2][:], sw_[4][:], ALU.subtract)
        vop("tensor_tensor", sw_[3][:], sw_[4][:], sw_[3][:], ALU.subtract)
        vop("tensor_tensor", sw_[2][:], sw_[2][:], sw_[3][:], ALU.max)
        vop("tensor_tensor", sw_[2][:], sw_[2][:], sw_[6][:], ALU.mult)
        vop("tensor_scalar", sw_[2][:], sw_[2][:], 1e-5, None, ALU.max)
        vop("tensor_scalar", g_sxinv[:, 0:NTB], sw_[2][:], g_rec127[:], None,
            ALU.mult)
        vop("reciprocal", sw_[5][:], g_sxinv[:, 0:NTB])
        vop("tensor_tensor", sw_[6][:], sw_[6][:], sw_[5][:], ALU.mult)
        vop("tensor_tensor", sw_[7][:], sw_[4][:], sw_[6][:], ALU.mult)
        vop("tensor_scalar", sw_[7][:], sw_[7][:], -1.0, None, ALU.mult)
        stattok = (sV, sV.v)
        wait(Y, (x1sem[0], x1sem[0].v), (x1sem[1], x1sem[1].v),
             (sS, sS.v), (sV, sV.v))
        x1crel = [(sV, 0), (sV, 0)]
        for tb in range(NTB):
            sl = tb % 2
            wait(Y, x1crel[sl])
            xin[sl].inc(Y.dma_start(x1b[sl][:], d_x1.ap()[tb * 128:(tb + 1) * 128, :]), 16)
            wait(V, (xin[sl], xin[sl].v), (sV, sV.v), stattok)
            vop("tensor_scalar", tqb[:], x1b[sl][:], sw_[6][:, tb:tb + 1],
                sw_[7][:, tb:tb + 1], ALU.mult, ALU.add)
            vop("tensor_copy", qib[:], tqb[:])
            vop("tensor_tensor", tqb[:], tqb[:], qib[:], ALU.subtract)
            vop("tensor_scalar", tqb[:], tqb[:], MAGIC, -MAGIC, ALU.add, ALU.add)
            wait(V, xirel[sl])
            vop("tensor_tensor", xib[sl][:], qib[:], tqb[:], ALU.add)
            x1crel[sl] = (sV, sV.v)
            wait(Sc, (sV, sV.v))
            xisem[sl].inc(Sc.dma_start(d_xint.ap()[tb * 128:(tb + 1) * 128, :],
                                       xib[sl][:]), 16)
            xirel[sl] = (xisem[sl], xisem[sl].v)
            wa_feed(2)
        wa_feed(len(wa_items))
        wa_epilogue()
        barrier_all()
    p4as.close()

    if upto == 4:
        return nc, st, RECS

    swn_g, tau_g, invtau_g = wstats[0]
    swn_u, tau_u, invtau_u = wstats[1]
    swn_d, tau_d, invtau_d = wstats[2]
    hamax = g_hamax
    with ExitStack() as p4:
        sb4 = lambda name, shape, dt: p4.enter_context(nc.sbuf_tensor(name, shape, dt))
        p4b6 = p4.enter_context(nc.psum_tensor("p4b6", [128, 512], F32))
        p4b7 = p4.enter_context(nc.psum_tensor("p4b7", [128, 512], F32))
        pbx = pb + [p4b6, p4b7]
        xiT = [sb4(f"p4_xiT{w}", [128, NIC * 512], BF16) for w in range(NW_OWN)]
        wfb = [sb4(f"p4_wf{i}", [128, 2048], F32) for i in range(2)]
        pmb = [sb4(f"p4_pm{i}", [128, 2048], F32) for i in range(2)]
        wgt = [sb4(f"p4_wgt{i}", [128, NIC * 512], BF16) for i in range(2)]
        wut = [sb4(f"p4_wut{i}", [128, NIC * 512], BF16) for i in range(2)]
        sgb = [sb4(f"p4_sg{i}", [128, 512], F32) for i in range(2)]
        htb = [sb4(f"p4_ht{i}", [128, 512], F32) for i in range(2)]
        amx = sb4("p4_amx", [128, 1], F32)
        cgc = [sb4(f"p4_cg{i}", [128, 1], F32) for i in range(2)]
        cuc = [sb4(f"p4_cu{i}", [128, 1], F32) for i in range(2)]
        gop("memset", hamax[:], 0.0)

        xts = dm["xt0"]
        for wnd in range(NW_OWN):
            for ic in range(NIC):
                xts.inc(Y.dma_start(
                    xiT[wnd][:, ic * 512:(ic + 1) * 512],
                    d_xint.ap()[wnd * 512:(wnd + 1) * 512, ic * 128:(ic + 1) * 128],
                    transpose=True), 16)

        tern_state = {"i": 0, "rel": [(sG, 0), (sG, 0)]}

        def ternarize(wsrc_ap, outbuf, ooff, W, invtau, wsem, flat=False):
            i = tern_state["i"]
            tern_state["i"] = i + 1
            wslot = i % 2
            wsem = [dm["tn0"], dm["tn1"]][wslot]
            wait(Y, tern_state["rel"][wslot])
            dst = wfb[wslot][:, :W] if flat else wfb[wslot][:, :W].rearrange(
                "p (c o) -> p c o", o=512)
            wsem.inc(Y.dma_start(dst, wsrc_ap), 16)
            wait(V, (wsem, wsem.v), (sV, sV.v))
            vop("tensor_scalar", pmb[wslot][:, :W], wfb[wslot][:, :W], invtau[:], 1.0,
                ALU.mult, ALU.is_ge)
            vop("tensor_scalar", wfb[wslot][:, :W], wfb[wslot][:, :W], invtau[:], -1.0,
                ALU.mult, ALU.is_le)
            wait(G, (sV, sV.v), (sG, sG.v))
            gop("tensor_tensor", outbuf[:, ooff:ooff + W], wfb[wslot][:, :W],
                pmb[wslot][:, :W], ALU.subtract)
            tern_state["rel"][wslot] = (sG, sG.v)

        hstrel = [(dm["h0"], 0), (dm["h1"], 0)]
        wdtb = [sb4(f"p4_wdtb{i}", [128, 2048], BF16) for i in range(2)]
        wdsem = [dm["dd0"], dm["dd1"]]
        wdst = [dm["p0"], dm["p1"]]
        wdrel = [(dm["p0"], 0), (dm["p1"], 0)]

        def wd_tern_tile(ti):
            slw = ti % 2
            wait(G, wdrel[slw])
            ternarize(d_wdT.ap()[ti * 128:(ti + 1) * 128, :], wdtb[slw], 0, 2048,
                      invtau_d, wdsem[slw], flat=True)
            wait(Sc, (sG, sG.v))
            for dmc in range(4):
                wdst[slw].inc(Sc.dma_start(
                    d_wdt.ap()[dmc, ti // 8, :, (ti % 8) * 512:(ti % 8 + 1) * 512],
                    wdtb[slw][:, dmc * 512:(dmc + 1) * 512]), 16)
            wdrel[slw] = (wdst[slw], wdst[slw].v)

        striperel = [(sT, 0), (sT, 0)]
        terntoks = {}

        def tern_items(fw):
            slf = fw % 2
            gsem = [dm["g0"], dm["g1"]][slf]
            usem = [dm["u0"], dm["u1"]][slf]
            first = [True]
            items = []
            for ti in range(fw * 4, fw * 4 + 4):
                items.append(lambda ti=ti: wd_tern_tile(ti))

            def mk(wT, obuf, grp, invt, wsem):
                def f():
                    if first[0]:
                        wait(G, striperel[slf])
                        first[0] = False
                    src3 = wT.ap()[grp * 512:(grp + 1) * 512,
                                   fw * 512:(fw + 1) * 512].rearrange(
                        "(c p) o -> p c o", p=128)
                    ternarize(src3, obuf, grp * 2048, 2048, invt, wsem)
                return f
            for grp in range(4):
                items.append(mk(d_wgT, wgt[slf], grp, invtau_g, gsem))
            for grp in range(4):
                items.append(mk(d_wuT, wut[slf], grp, invtau_u, usem))
            return items

        def emit_tern(fw):
            for it_ in tern_items(fw):
                it_()
            terntoks[fw] = (sG, sG.v)

        emit_tern(0)
        for fw in range(DF // 512):
            slf = fw % 2
            pend = tern_items(fw + 1) if fw + 1 < DF // 512 else []
            terntok = terntoks[fw]
            for wnd in range(NW_OWN):
                for tb in range(4):
                    tok = wnd * 4 + tb
                    bg = (tb % 4)
                    bu = 4 + (tb % 4)
                    eb = tb % 2
                    wait(T, (xts, xts.v), terntok, prel[bg], prel[bu])
                    for ic in range(NIC):
                        mm = T.matmul(pbx[bg][:],
                                      xiT[wnd][:, ic * 512 + tb * 128:
                                               ic * 512 + (tb + 1) * 128],
                                      wgt[slf][:, ic * 512:(ic + 1) * 512],
                                      start=(ic == 0), stop=(ic == NIC - 1))
                    sT.inc(mm)
                    gmm = sT.v
                    for ic in range(NIC):
                        mm = T.matmul(pbx[bu][:],
                                      xiT[wnd][:, ic * 512 + tb * 128:
                                               ic * 512 + (tb + 1) * 128],
                                      wut[slf][:, ic * 512:(ic + 1) * 512],
                                      start=(ic == 0), stop=(ic == NIC - 1))
                    sT.inc(mm)
                    umm = sT.v
                    wait(V, (sV, sV.v), (sS, sS.v))
                    vop("tensor_tensor", cgc[eb][:], g_sxinv[:, tok:tok + 1],
                        swn_g[:], ALU.mult)
                    vop("tensor_tensor", cuc[eb][:], g_sxinv[:, tok:tok + 1],
                        swn_u[:], ALU.mult)
                    wait(Sc, (sT, gmm), (sV, sV.v))
                    sop("activation", sgb[eb][:], pbx[bg][:], AF.Silu, bias=0.0,
                        scale=cgc[eb][:])
                    prel[bg] = (sS, sS.v)
                    wait(V, (sS, sS.v), (sT, umm), hstrel[eb])
                    vop("scalar_tensor_tensor", htb[eb][:], pbx[bu][:], cuc[eb][:],
                        sgb[eb][:], ALU.mult, ALU.mult)
                    prel[bu] = (sV, sV.v)
                    vop("tensor_reduce", amx[:], htb[eb][:], AX.X, ALU.max,
                        apply_absolute_value=True)
                    vop("tensor_tensor", hamax[:, tok:tok + 1],
                        hamax[:, tok:tok + 1], amx[:], ALU.max)
                    wait(Sc, (sV, sV.v))
                    hsm = [dm["h0"], dm["h1"]][eb]
                    hsm.inc(Sc.dma_start(
                        d_hfp.ap()[wnd * 512 + tb * 128: wnd * 512 + (tb + 1) * 128,
                                   fw * 512:(fw + 1) * 512], htb[eb][:]), 16)
                    hstrel[eb] = (hsm, hsm.v)
                    if pend:
                        pend.pop(0)()
            while pend:
                pend.pop(0)()
            if fw + 1 < DF // 512:
                terntoks[fw + 1] = (sG, sG.v)
            striperel[slf] = (sT, sT.v)
        barrier_all()

    with ExitStack() as p4e:
        sb4 = lambda name, shape, dt: p4e.enter_context(nc.sbuf_tensor(name, shape, dt))
        ptb = [p4e.enter_context(nc.psum_tensor(f"ptb{i}", [128, 1024], BF16))
               for i in range(2)]
        NQ = 1024
        hq = [sb4(f"p4_hq{i}", [128, NQ], F32) for i in range(2)]
        hqt = sb4("p4_hqt", [128, NQ], F32)
        hqi = sb4("p4_hqi", [128, NQ], I32)
        hqo = sb4("p4_hqo", [128, NQ], BF16)
        st4 = [sb4(f"p4_st{i}", [128, 1], F32) for i in range(4)]
        hTg = [sb4(f"p4_hTg{i}", [128, 64 * 512], BF16) for i in range(2)]
        wdstr = [sb4(f"p4_wds{i}", [128, 8 * 512], BF16) for i in range(3)]
        x1in = [sb4(f"p4_x1i{i}", [128, 512], F32) for i in range(2)]
        evb = [sb4(f"p4_ev{i}", [128, 512], F32) for i in range(2)]
        outb = [sb4(f"p4_ob{i}", [128, 512], F32) for i in range(2)]
        cdc = sb4("p4_cd", [128, 8], F32)

        hin = [dm["h0"], dm["h1"]]
        hinrel = [(sV, 0), (sV, 0)]
        x1s = [dm["x0"], dm["x1"]]
        x1rel2 = [(sG, 0), (sG, 0)]
        wdls = [dm["wr0"], dm["wr1"], dm["wr2"]]
        wdlrel = [(sT, 0)] * 3
        wdcnt = [0]
        osem = [dm["o0"], dm["o1"]]
        obrel = [(dm["o0"], 0), (dm["o1"], 0)]
        ptbrel = [(sS, 0), (sS, 0)]
        hTgev = [(sS, 0), (sS, 0)]
        hTgrd = [(sT, 0), (sT, 0)]
        qtr = [(sT, 0)]

        def q_load(tb, qq):
            slh = (tb * (DF // NQ) + qq) % 2
            ring = Y if slh == 0 else Sc
            wait(ring, hinrel[slh])
            hin[slh].inc(ring.dma_start(
                hq[slh][:], d_hfp.ap()[tb * 128:(tb + 1) * 128,
                                       qq * NQ:(qq + 1) * NQ]), 16)

        def q_comp(tb, qq, lead):
            tbl = tb % 4
            slh = (tb * (DF // NQ) + qq) % 2
            if qq == 0:
                wait(V, (sV, sV.v))
                vop("tensor_scalar", st4[0][:], hamax[:, tb:tb + 1], 1e-5, None,
                    ALU.max)
                vop("reciprocal", st4[1][:], st4[0][:])
                vop("tensor_scalar", st4[1][:], st4[1][:], 127.0, None, ALU.mult)
                vop("tensor_tensor", g_shinv[:, tb:tb + 1], st4[0][:], g_rec127[:],
                    ALU.mult)
            wait(V, (hin[slh], hin[slh].v), (sV, sV.v), qtr[0])
            vop("tensor_scalar", hqt[:], hq[slh][:], st4[1][:], None, ALU.mult)
            hinrel[slh] = (sV, sV.v)
            vop("tensor_copy", hqi[:], hqt[:])
            vop("tensor_tensor", hqt[:], hqt[:], hqi[:], ALU.subtract)
            vop("tensor_scalar", hqt[:], hqt[:], MAGIC, -MAGIC, ALU.add, ALU.add)
            vop("tensor_tensor", hqo[:], hqi[:], hqt[:], ALU.add)
            qtok = (sV, sV.v)
            for j2 in range(2):
                wait(T, qtok, ptbrel[j2], hTgrd[lead] if qq == 0 and j2 == 0
                     else (sT, 0))
                for j in range(4):
                    tr = T.transpose(ptb[j2][:, j * 128:(j + 1) * 128],
                                     hqo[:, (j2 * 4 + j) * 128:
                                         (j2 * 4 + j + 1) * 128],
                                     g_ident[:])
                sT.inc(tr)
                wait(Sc, (sT, sT.v), (sS, sS.v))
                for j in range(4):
                    fc = qq * 8 + j2 * 4 + j
                    ev = Sc.activation(
                        hTg[lead][:, fc * 512 + tbl * 128:
                                  fc * 512 + (tbl + 1) * 128],
                        ptb[j2][:, j * 128:(j + 1) * 128], AF.Identity)
                sS.inc(ev)
                ptbrel[j2] = (sS, sS.v)
            qtr[0] = (sT, sT.v)

        def quant_tokb(tb, lead):
            for qq in range(DF // NQ):
                q_load(tb, qq)
                q_comp(tb, qq, lead)
            hTgev[lead] = (sS, sS.v)

        for tbl in range(4):
            quant_tokb(tbl, 0)
        for tw in range(4):
            cdg = (tw % 2) * 4
            for tbl in range(4):
                tb = tw * 4 + tbl
                wait(V, (sV, sV.v), (sS, sS.v))
                vop("tensor_tensor", cdc[:, cdg + tbl:cdg + tbl + 1],
                    g_shinv[:, tb:tb + 1], swn_d[:], ALU.mult)
            cdtok = (sV, sV.v)
            lead = tw % 2
            qjobs = []
            ncomp = [0]
            if tw + 1 < 4:
                qjobs = [((tw + 1) * 4 + tbl2, qq) for tbl2 in range(4)
                         for qq in range(DF // NQ)]
                for k in range(2):
                    q_load(*qjobs[k])

            def q_step(n):
                for _ in range(n):
                    k = ncomp[0]
                    if k >= len(qjobs):
                        return
                    ncomp[0] = k + 1
                    q_comp(*qjobs[k], lead=(tw + 1) % 2)
                    if k + 2 < len(qjobs):
                        q_load(*qjobs[k + 2])
            for dmc in range(4):
                wait(T, hTgev[lead], prel[0], prel[1], prel[2], prel[3])
                for wg8 in range(8):
                    slw = wdcnt[0] % 3
                    wdcnt[0] += 1
                    ring = Sc if wg8 % 2 else Y
                    wait(ring, wdlrel[slw])
                    wdls[slw].inc(ring.dma_start(wdstr[slw][:],
                                                 d_wdt.ap()[dmc, wg8]), 16)
                    wait(T, (wdls[slw], wdls[slw].v))
                    for c8 in range(8):
                        fc = wg8 * 8 + c8
                        for tbl in range(4):
                            mm = T.matmul(pb[tbl][:],
                                          hTg[lead][:, fc * 512 + tbl * 128:
                                              fc * 512 + (tbl + 1) * 128],
                                          wdstr[slw][:, c8 * 512:(c8 + 1) * 512],
                                          start=(fc == 0), stop=(fc == 63))
                    sT.inc(mm)
                    wdlrel[slw] = (sT, sT.v)
                mmdone = (sT, sT.v)
                for tbl in range(4):
                    tb = tw * 4 + tbl
                    ob = tbl % 2
                    wait(Sc, mmdone, cdtok, (sG, sG.v))
                    sop("activation", evb[ob][:], pb[tbl][:], AF.Identity,
                        bias=0.0, scale=cdc[:, cdg + tbl:cdg + tbl + 1])
                    prel[tbl] = (sS, sS.v)
                    wait(Y, x1rel2[ob])
                    x1s[ob].inc(Y.dma_start(
                        x1in[ob][:], d_x1.ap()[tb * 128:(tb + 1) * 128,
                                               dmc * 512:(dmc + 1) * 512]), 16)
                    wait(G, (sS, sS.v), (x1s[ob], x1s[ob].v), obrel[ob], (sG, sG.v))
                    gop("tensor_tensor", outb[ob][:], evb[ob][:], x1in[ob][:],
                        ALU.add)
                    x1rel2[ob] = (sG, sG.v)
                    wait(G, (sG, sG.v))
                    osem[ob].inc(G.dma_start(
                        d_out.ap()[tb * 128:(tb + 1) * 128,
                                   dmc * 512:(dmc + 1) * 512], outb[ob][:]), 16)
                    obrel[ob] = (osem[ob], osem[ob].v)
                q_step(2)
            hTgrd[lead] = (sT, sT.v)
            if tw + 1 < 4:
                q_step(len(qjobs))
                hTgev[(tw + 1) % 2] = (sS, sS.v)
        wait(Sc, (osem[0], osem[0].v), (osem[1], osem[1].v))
        wait(G, (osem[0], osem[0].v), (osem[1], osem[1].v))

    return nc, st, RECS


def _finalize(nc, st, RECS):
    with nc.Block() as block:
        block.sync(lambda e: RECS[0].replay(e))
        block.tensor(lambda e: RECS[1].replay(e))
        block.vector(lambda e: RECS[2].replay(e))
        block.scalar(lambda e: RECS[3].replay(e))
        block.gpsimd(lambda e: RECS[4].replay(e))
    st.close()
    return nc


def build(upto=99):
    nc, st, RECS = build_program(upto)
    return _finalize(nc, st, RECS)


def own_index(par):
    return np.concatenate(
        [np.arange(1024 * t + 512 * par, 1024 * t + 512 * par + 512)
         for t in range(4)])


def host_inputs(inputs):
    x = np.asarray(inputs["L0_attn_x"], np.float32)
    Wq = np.asarray(inputs["L0_attn_Wq"], np.float32)
    Wk = np.asarray(inputs["L0_attn_Wk"], np.float32)
    Wv = np.asarray(inputs["L0_attn_Wv"], np.float32)
    Wo = np.asarray(inputs["L0_attn_Wo"], np.float32)
    bq = np.asarray(inputs["L0_attn_bq"], np.float32)
    bk = np.asarray(inputs["L0_attn_bk"], np.float32)
    bv = np.asarray(inputs["L0_attn_bv"], np.float32)
    bo = np.asarray(inputs["L0_attn_bo"], np.float32)
    wg = np.asarray(inputs["L0_ffn_w_gate"], np.float32)
    wu = np.asarray(inputs["L0_ffn_w_up"], np.float32)
    wdn = np.asarray(inputs["L0_ffn_w_down"], np.float32)

    WqT = np.ascontiguousarray(Wq.T)
    WkT = np.ascontiguousarray(Wk.T)
    WvT = np.ascontiguousarray(Wv.T)
    WoT = np.ascontiguousarray(Wo.T)
    wgT = np.ascontiguousarray(wg.T)
    wuT = np.ascontiguousarray(wu.T)
    wdT = np.ascontiguousarray(wdn.T)
    bqT = np.ascontiguousarray(bq.reshape(NH, 128).T)
    bkT = np.ascontiguousarray(bk.reshape(NH, 128).T)
    bvb = np.ascontiguousarray(np.broadcast_to(bv, (128, D)))
    bob = np.ascontiguousarray(np.broadcast_to(bo, (128, D)))
    pos = np.arange(S, dtype=np.float32).reshape(1, S)
    jcol = np.repeat(np.arange(64, dtype=np.float32), 2).reshape(128, 1)
    sgn = np.tile(np.array([-1.0, 1.0], np.float32), 64).reshape(128, 1)

    in_maps = []
    for c in range(8):
        b, par = c // 2, c % 2
        idx = own_index(par)
        in_maps.append(dict(
            xq=np.ascontiguousarray(x[b][idx]),
            xkv=np.ascontiguousarray(x[b]),
            WqT=WqT, WkT=WkT, WvT=WvT, WoT=WoT,
            bqT=bqT, bkT=bkT, bvb=bvb, bob=bob,
            wgT=wgT, wuT=wuT, wdT=wdT,
            qpos=idx.astype(np.float32).reshape(1, OWN),
            pos=pos, jcol=jcol, sgn=sgn, ident=np.eye(128, dtype=np.float32),
        ))
    return in_maps


def kernel(**inputs):
    in_maps = host_inputs(inputs)
    nc = build()
    res = run_bass_kernel_spmd(nc, in_maps, core_ids=list(range(8)))
    out = np.empty((4, S, D), np.float32)
    for c in range(8):
        b, par = c // 2, c % 2
        out[b, own_index(par)] = np.asarray(res.results[c]["out"])
    return out
